# version 03
# baseline (speedup 1.0000x reference)
import numpy as np
import ml_dtypes
import concourse.bass as bass
import concourse.mybir as mybir
from concourse.bass_utils import run_bass_kernel_spmd
from contextlib import ExitStack

F32 = mybir.dt.float32
BF16 = mybir.dt.bfloat16
I32 = mybir.dt.int32
AF = mybir.ActivationFunctionType
ALU = mybir.AluOpType

DM, DI, DH, NST, DTR = 1024, 2048, 4096, 16, 64
DBG = False
L, B = 8192, 2
T = 512
NCH = L // T
QT = 2048
EPS = 1e-5
KT = DM // 128
NT = DI // 128
HT = DH // 128
HHT = NT // 2

ENGS = ("sync", "tensor", "scalar", "vector")


class Prog:
    def __init__(self, nc):
        self.nc = nc
        self.ops = {e: [] for e in ENGS}
        self.cnt = {}
        self.sems = {}
        self.last_wait = {e: {} for e in ENGS}

    def sem(self, ctx, name):
        h = ctx.enter_context(self.nc.semaphore(name))
        self.cnt[name] = 0
        self.sems[name] = h
        return h

    def on(self, eng, fn):
        self.ops[eng].append(fn)

    def wait(self, eng, name, val=None):
        if isinstance(val, tuple):
            name, val = val
        if name in ("sLD", "sST") and val is None:
            for s in self.pools[name]:
                self.wait(eng, s, self.cnt[s])
            return
        if val is None:
            val = self.cnt[name]
        if val <= 0 or self.last_wait[eng].get(name, -1) >= val:
            return
        self.last_wait[eng][name] = val
        h = self.sems[name]
        self.on(eng, lambda e, h=h, v=val: e.wait_ge(h, v))

    def mark(self, eng, name, fn, amt):
        self.cnt[name] += amt
        h = self.sems[name]
        self.on(eng, lambda e, h=h, a=amt, f=fn: f(e).then_inc(h, a))
        return self.cnt[name]

    def emit(self, eng, fn):
        self.on(eng, fn)

    def barrier(self):
        for e in ENGS:
            for s in ("sPE", "sAC", "sDV"):
                self.wait(e, s)
            self.wait(e, "sLD")
            self.wait(e, "sST")


def build_program():
    nc = bass.Bass("TRN2", target_bir_lowering=False)
    dram = nc.dram_tensor

    x_in = dram("x_seq", [DM, L], F32, kind="ExternalInput").ap()
    qidx_in = dram("qidx", [1, 1], I32, kind="ExternalInput").ap()
    w_in = dram("w_in", [DM, 2 * DI], F32, kind="ExternalInput").ap()
    wconv_in = dram("wconv_t", [128, NT * 4], F32, kind="ExternalInput").ap()
    bconv_in = dram("bconv_t", [128, NT], F32, kind="ExternalInput").ap()
    wxp_in = dram("w_xproj", [DI, 96], F32, kind="ExternalInput").ap()
    wdt_in = dram("w_dt", [DTR, DI], F32, kind="ExternalInput").ap()
    bdt_in = dram("bdt_t", [128, NT], F32, kind="ExternalInput").ap()
    alog_in = dram("alog_t", [128, NT * NST], F32, kind="ExternalInput").ap()
    d_in = dram("d_t", [128, NT], F32, kind="ExternalInput").ap()
    w_out_in = dram("w_out", [DI, DM], F32, kind="ExternalInput").ap()
    w1_in = dram("w1t", [DM, DH], F32, kind="ExternalInput").ap()
    w2_in = dram("w2t", [DM, DH], F32, kind="ExternalInput").ap()
    h1_in = dram("h1t", [DH, DH], F32, kind="ExternalInput").ap()
    h2_in = dram("h2t", [DH, DH], F32, kind="ExternalInput").ap()
    w3_in = dram("w3t", [DH, DM], F32, kind="ExternalInput").ap()
    s1_in = dram("s1_t", [128, HT], F32, kind="ExternalInput").ap()
    s2_in = dram("s2_t", [128, HT], F32, kind="ExternalInput").ap()
    sh1_in = dram("sh1_t", [128, HT], F32, kind="ExternalInput").ap()
    sh2_in = dram("sh2_t", [128, HT], F32, kind="ExternalInput").ap()
    s3_in = dram("s3_t", [128, KT], F32, kind="ExternalInput").ap()
    sel_in = dram("sel_bf", [NST, NST * 128], BF16, kind="ExternalInput").ap()
    ones_in = dram("ones_bf", [128, 1], BF16, kind="ExternalInput").ap()
    one1_in = dram("one1_bf", [1, 128], BF16, kind="ExternalInput").ap()
    cst_in = dram("cst", [128, 4], F32, kind="ExternalInput").ap()

    y_out = dram("y_out", [DM, QT], F32, kind="ExternalOutput").ap()

    wbf_in = dram("wbf_in", [8 * 128, KT * 512], BF16, kind="Internal").ap()
    wbf_out = dram("wbf_out", [4 * 128, NT * 256], BF16, kind="Internal").ap()
    wbf_1 = dram("wbf_1", [HT * 128, KT * 128], BF16, kind="Internal").ap()
    wbf_2 = dram("wbf_2", [HT * 128, KT * 128], BF16, kind="Internal").ap()
    wbf_h1 = dram("wbf_h1", [HT * 128, HT * 128], BF16, kind="Internal").ap()
    wbf_h2 = dram("wbf_h2", [HT * 128, HT * 128], BF16, kind="Internal").ap()
    wbf_3 = dram("wbf_3", [KT * 128, HT * 128], BF16, kind="Internal").ap()
    x2_dr = dram("x2_dr", [DM, L], F32, kind="Internal").ap()
    ln2_dr = dram("ln2_dr", [DM, L], BF16, kind="Internal").ap()
    if DBG:
        dbg_xc = dram("dbg_xc", [DI, L], F32, kind="Internal").ap()
        dbg_dl = dram("dbg_dl", [DI, L], F32, kind="Internal").ap()
        dbg_y = dram("dbg_y", [DI, L], F32, kind="Internal").ap()
        dbg_mo = dram("dbg_mo", [DM, L], F32, kind="Internal").ap()
        dbg_ln1 = dram("dbg_ln1", [DM, L], BF16, kind="Internal").ap()
        dbg_xi = dram("dbg_xi", [DI, L], BF16, kind="Internal").ap()
        dbg_z = dram("dbg_z", [DI, L], BF16, kind="Internal").ap()

    with ExitStack() as ctx:
        p = Prog(nc)
        for s in ("sPE", "sAC", "sDV"):
            p.sem(ctx, s)
        p.pools = {"sLD": [], "sST": []}
        for i in range(16):
            n = f"sL{i}"
            p.sem(ctx, n)
            p.pools["sLD"].append(n)
        for i in range(8):
            n = f"sS{i}"
            p.sem(ctx, n)
            p.pools["sST"].append(n)
        p.rot = {"sLD": 0, "sST": 0}
        sb = lambda n, s, d=F32: ctx.enter_context(nc.sbuf_tensor(n, s, d))

        A_sb = sb("A_sb", [128, NT * NST])
        D_sb = sb("D_sb", [128, NT])
        bconv_sb = sb("bconv_sb", [128, NT])
        bdt_sb = sb("bdt_sb", [128, NT])
        wconv_sb = sb("wconv_sb", [128, NT * 4])
        wxp_bf = sb("wxp_bf", [128, NT * 96], BF16)
        wdt_bf = sb("wdt_bf", [DTR, DI], BF16)
        sel_sb = sb("sel_sb", [NST, NST * 128], BF16)
        ones_sb = sb("ones_sb", [128, 1], BF16)
        one1_sb = sb("one1_sb", [1, 128], BF16)
        cst_sb = sb("cst_sb", [128, 4])
        mr_bf = sb("mr_bf", [1, 2 * T], BF16)
        hist_bf = sb("hist_bf", [128, NT * 3], BF16)
        hcar_sb = sb("hcar_sb", [128, NT * NST])
        qidx_sb = ctx.enter_context(nc.sbuf_tensor("qidx_sb", [1, 1], I32))
        dlr_bf = sb("dlr_bf", [DTR, T], BF16)
        b_bf = sb("b_bf", [NST, T], BF16)
        c_bf = sb("c_bf", [NST, T], BF16)
        wwin = sb("wwin", [128, 4096], BF16)

        ps = [ctx.enter_context(nc.psum_tensor(f"ps{i}", [128, 512], F32))
              for i in range(8)]

        block = ctx.enter_context(nc.Block())

        def _dma(pool, fn):
            i = p.rot[pool]
            p.rot[pool] = (i + 1) % len(p.pools[pool])
            sname = p.pools[pool][i]
            p.wait("sync", sname, p.cnt[sname])
            v = p.mark("sync", sname, fn, 16)
            return (sname, v)

        def ld(dst, src):
            return _dma("sLD",
                        lambda e, d=dst, r=src: e.dma_start(out=d, in_=r))

        def st(dst, src):
            return _dma("sST",
                        lambda e, d=dst, r=src: e.dma_start(out=d, in_=r))

        def pe(fn):
            p.emit("tensor", fn)

        def pem(fn):
            return p.mark("tensor", "sPE", fn, 1)

        def ac(fn):
            p.wait("scalar", "sAC", p.cnt["sAC"])
            return p.mark("scalar", "sAC", fn, 1)

        acm = ac

        def dv(fn):
            p.wait("vector", "sDV", p.cnt["sDV"])
            return p.mark("vector", "sDV", fn, 1)

        dvm = dv

        with nc.sbuf_tensor("cast_f", [128, 4096], F32) as cast_f, \
             nc.sbuf_tensor("cast_b", [128, 4096], BF16) as cast_b:

            ld(D_sb[:], d_in)
            ld(bconv_sb[:], bconv_in)
            ld(bdt_sb[:], bdt_in)
            ld(wconv_sb[:], wconv_in)
            ld(sel_sb[:], sel_in)
            ld(ones_sb[:], ones_in)
            ld(one1_sb[:], one1_in)
            ld(qidx_sb[:], qidx_in)
            ld(cst_sb[:], cst_in)

            ld(cast_f[:, 0:NT * NST], alog_in)
            p.wait("scalar", "sLD")
            acm(lambda e: e.activation(out=A_sb[:], in_=cast_f[:, 0:NT * NST],
                                       func=AF.Exp))
            p.wait("sync", "sAC")
            p.wait("vector", "sAC")
            dvm(lambda e: e.tensor_scalar_mul(out=A_sb[:], in0=A_sb[:],
                                              scalar1=-1.0))

            ld(cast_f[:, 0:NT * 96],
               wxp_in.rearrange("(n p) c -> p n c", p=128))
            p.wait("scalar", "sLD")
            acm(lambda e: e.copy(out=wxp_bf[:], in_=cast_f[:, 0:NT * 96]))
            p.wait("sync", "sAC")
            ld(cast_f[0:DTR, 0:DI], wdt_in)
            p.wait("scalar", "sLD")
            acm(lambda e: e.copy(out=wdt_bf[:], in_=cast_f[0:DTR, 0:DI]))
            p.wait("sync", "sAC")

            def cast_tiled(src, dst, krows, mcols, slab_m, mc):
                for k in range(krows):
                    p.wait("sync", "sAC")
                    tl = ld(cast_f[:, 0:mcols * 128],
                            src[k * 128:(k + 1) * 128, :])
                    p.wait("scalar", "sLD", tl)
                    tc = acm(lambda e: e.copy(out=cast_b[:, 0:mcols * 128],
                                              in_=cast_f[:, 0:mcols * 128]))
                    p.wait("sync", "sAC", tc)
                    tsl = 0
                    for mb in range(mcols // slab_m):
                        w = slab_m * 128
                        tsl = st(dst[mb * 128:(mb + 1) * 128,
                                     k * w:(k + 1) * w],
                                 cast_b[:, mb * w:(mb + 1) * w])
                    p.wait("scalar", "sST")

            cast_tiled(w_in, wbf_in, KT, 2 * NT, 4, 512)
            cast_tiled(w_out_in, wbf_out, NT, KT, 2, 256)
            cast_tiled(w1_in, wbf_1, KT, HT, 1, 128)
            cast_tiled(w2_in, wbf_2, KT, HT, 1, 128)
            cast_tiled(h1_in, wbf_h1, HT, HT, 1, 128)
            cast_tiled(h2_in, wbf_h2, HT, HT, 1, 128)
            cast_tiled(w3_in, wbf_3, HT, KT, 1, 128)
            p.barrier()

        dvm(lambda e: e.memset(hist_bf[:], 0.0))
        dvm(lambda e: e.memset(hcar_sb[:], 0.0))
        p.barrier()

        def layer_norm2(v, out_bf, xb, qb):
            pe_done = []
            for k in range(KT):
                s = slice(k * T, (k + 1) * T)
                sl = slice((k % 2) * T, (k % 2 + 1) * T)
                if k >= 2:
                    p.wait("scalar", "sPE", pe_done[k - 2])
                ac(lambda e, s=s, sl=sl: e.copy(out=xb[:, sl], in_=v[:, s]))
                ta = acm(lambda e, s=s, sl=sl: e.activation(
                    out=qb[:, sl], in_=v[:, s], func=AF.Square))
                p.wait("tensor", "sAC", ta)
                if k < KT - 1:
                    pe(lambda e, sl=sl, k=k: e.matmul(
                        out=ps[6][0:1, 0:T], lhsT=ones_sb[:], rhs=xb[:, sl],
                        start=(k == 0), stop=False))
                else:
                    pem(lambda e, sl=sl, k=k: e.matmul(
                        out=ps[6][0:1, 0:T], lhsT=ones_sb[:], rhs=xb[:, sl],
                        start=(k == 0), stop=True))
                pe_done.append(pem(lambda e, sl=sl, k=k: e.matmul(
                    out=ps[7][0:1, 0:T], lhsT=ones_sb[:], rhs=qb[:, sl],
                    start=(k == 0), stop=(k == KT - 1))))
            p.wait("vector", "sPE")
            dv(lambda e: e.tensor_scalar_mul(out=ps[6][0:1, 0:T],
                                             in0=ps[6][0:1, 0:T],
                                             scalar1=1.0 / DM))
            td = dvm(lambda e: e.tensor_scalar_mul(out=ps[7][0:1, 0:T],
                                                   in0=ps[7][0:1, 0:T],
                                                   scalar1=1.0 / DM))
            p.wait("scalar", "sDV", td)
            ac(lambda e: e.copy(out=mr_bf[0:1, 0:T], in_=ps[6][0:1, 0:T]))
            ta = acm(lambda e: e.activation(out=mr_bf[0:1, T:2 * T],
                                            in_=ps[6][0:1, 0:T],
                                            func=AF.Square))
            p.wait("vector", "sAC", ta)
            td = dvm(lambda e: e.tensor_sub(out=ps[7][0:1, 0:T],
                                            in0=ps[7][0:1, 0:T],
                                            in1=mr_bf[0:1, T:2 * T]))
            p.wait("scalar", "sDV", td)
            ta = acm(lambda e: e.activation(out=ps[7][0:1, 0:T],
                                            in_=ps[7][0:1, 0:T],
                                            func=AF.Sqrt,
                                            bias=cst_sb[0:1, 0:1]))
            p.wait("vector", "sAC", ta)
            td = dvm(lambda e: e.reciprocal(out=ps[7][0:1, 0:T],
                                            in_=ps[7][0:1, 0:T]))
            p.wait("scalar", "sDV", td)
            ta = acm(lambda e: e.copy(out=mr_bf[0:1, T:2 * T],
                                      in_=ps[7][0:1, 0:T]))
            p.wait("tensor", "sAC", ta)
            pem(lambda e: e.matmul(out=ps[6][:, 0:T], lhsT=one1_sb[:],
                                   rhs=mr_bf[0:1, 0:T], start=True, stop=True))
            tp = pem(lambda e: e.matmul(out=ps[7][:, 0:T], lhsT=one1_sb[:],
                                        rhs=mr_bf[0:1, T:2 * T],
                                        start=True, stop=True))
            p.wait("vector", "sPE", tp)
            for k in range(KT):
                s = slice(k * T, (k + 1) * T)
                dv(lambda e, s=s: e.scalar_tensor_tensor(
                    out=v[:, s], in0=ps[6][:, 0:T], scalar=-1.0, in1=v[:, s],
                    op0=ALU.mult, op1=ALU.add))
                dv(lambda e, s=s: e.tensor_tensor(
                    out=out_bf[:, s], in0=v[:, s], in1=ps[7][:, 0:T],
                    op=ALU.mult))
            return dvm(lambda e: e.nop())

        with ExitStack() as mctx:
            msb = lambda n, s, d=F32: mctx.enter_context(nc.sbuf_tensor(n, s, d))
            sA = msb("sA", [128, KT * T])
            sB = msb("sB", [128, NT * T])
            sC = msb("sC", [128, HHT * T])
            sD = msb("sD", [128, HHT * T])
            sE = msb("sE", [128, HHT * T])
            sF = msb("sF", [128, HHT * T])
            sG = msb("sG", [128, 4 * T])
            bA = msb("bA", [128, KT * T], BF16)
            bX = msb("bX", [128, 2 * T], BF16)
            bQ = msb("bQ", [128, 2 * T], BF16)
            bI = msb("bI", [128, NT * T], BF16)
            bZ = msb("bZ", [128, NT * T], BF16)
            bC = msb("bC", [128, NT * T], BF16)

            for c in range(NCH):
                tok = slice(c * T, (c + 1) * T)
                p.barrier()

                tl = ld(sA.rearrange("p (k t) -> p k t", k=KT),
                        x_in.rearrange("(k p) t -> p k t", p=128)[:, :, tok])
                p.wait("scalar", "sLD", tl)
                p.wait("vector", "sLD", tl)
                tln = layer_norm2(sA, bA, bX, bQ)
                p.wait("tensor", "sDV", tln)
                if DBG:
                    p.wait("sync", "sDV", tln)
                    st(dbg_ln1.rearrange("(k p) t -> p k t", p=128)[:, :, tok],
                       bA.rearrange("p (k t) -> p k t", k=KT))

                for mb in range(8):
                    tl = ld(wwin[:], wbf_in[mb * 128:(mb + 1) * 128, :])
                    p.wait("tensor", "sLD", tl)
                    for m in range(4):
                        for k in range(KT):
                            f = (lambda e, m=m, k=k: e.matmul(
                                out=ps[m][:, 0:T],
                                lhsT=wwin[:, k * 512 + m * 128:
                                          k * 512 + (m + 1) * 128],
                                rhs=bA[:, k * T:(k + 1) * T],
                                start=(k == 0), stop=(k == KT - 1)))
                            tp = pem(f) if k == KT - 1 else pe(f)
                    p.wait("scalar", "sPE", tp)
                    for m in range(4):
                        gi = mb * 4 + m
                        if gi < NT:
                            dst = bI[:, gi * T:(gi + 1) * T]
                        else:
                            dst = bZ[:, (gi - NT) * T:(gi - NT + 1) * T]
                        ac(lambda e, m=m, d=dst: e.copy(out=d,
                                                        in_=ps[m][:, 0:T]))
                    ta = acm(lambda e: e.nop())
                    p.wait("tensor", "sAC", ta)
                    p.wait("sync", "sPE", tp)
                p.wait("vector", "sAC")
                if DBG:
                    p.wait("sync", "sAC")
                    st(dbg_xi.rearrange("(n p) t -> p n t", p=128)[:, :, tok],
                       bI.rearrange("p (n t) -> p n t", n=NT)[:, :, 0:T])
                    st(dbg_z.rearrange("(n p) t -> p n t", p=128)[:, :, tok],
                       bZ.rearrange("p (n t) -> p n t", n=NT))

                for hf in range(2):
                    for j in range(HHT):
                        n = hf * HHT + j
                        asl = slice(j * T, (j + 1) * T)
                        for k in range(4):
                            wcol = wconv_sb[:, n * 4 + k:n * 4 + k + 1]
                            lead = 3 - k
                            if lead > 0:
                                hsl = hist_bf[:, n * 3 + k:n * 3 + 3]
                                if k == 0:
                                    dv(lambda e, a=asl, h=hsl, w=wcol, ld_=lead:
                                       e.tensor_scalar(
                                           out=sC[:, a][:, 0:ld_], in0=h,
                                           scalar1=w, scalar2=None,
                                           op0=ALU.mult))
                                else:
                                    dv(lambda e, a=asl, h=hsl, w=wcol, ld_=lead:
                                       e.scalar_tensor_tensor(
                                           out=sC[:, a][:, 0:ld_], in0=h,
                                           scalar=w,
                                           in1=sC[:, a][:, 0:ld_],
                                           op0=ALU.mult, op1=ALU.add))
                            bsl = bI[:, n * T:n * T + T - lead]
                            osl = sC[:, j * T + lead:(j + 1) * T]
                            if k == 0:
                                dv(lambda e, b=bsl, o=osl, w=wcol:
                                   e.tensor_scalar(out=o, in0=b, scalar1=w,
                                                   scalar2=None, op0=ALU.mult))
                            else:
                                dv(lambda e, b=bsl, o=osl, w=wcol:
                                   e.scalar_tensor_tensor(
                                       out=o, in0=b, scalar=w, in1=o,
                                       op0=ALU.mult, op1=ALU.add))
                        dv(lambda e, n=n: e.tensor_copy(
                            out=hist_bf[:, n * 3:n * 3 + 3],
                            in_=bI[:, n * T + T - 3:n * T + T]))
                    td = dvm(lambda e: e.nop())
                    p.wait("scalar", "sDV", td)
                    for j in range(HHT):
                        n = hf * HHT + j
                        ac(lambda e, j=j, n=n: e.activation(
                            out=sB[:, n * T:(n + 1) * T],
                            in_=sC[:, j * T:(j + 1) * T], func=AF.Silu,
                            bias=bconv_sb[:, n:n + 1]))
                    ta = acm(lambda e: e.nop())
                    p.wait("vector", "sAC", ta)
                for n in range(NT):
                    s = slice(n * T, (n + 1) * T)
                    dv(lambda e, s=s: e.tensor_copy(out=bC[:, s], in_=sB[:, s]))
                td = dvm(lambda e: e.nop())
                p.wait("tensor", "sDV", td)
                if DBG:
                    p.wait("sync", "sDV", td)
                    st(dbg_xc.rearrange("(n p) t -> p n t", p=128)[:, :, tok],
                       sB.rearrange("p (n t) -> p n t", n=NT))

                for (bank, c0, c1) in ((0, 0, DTR), (4, DTR, DTR + NST),
                                       (5, DTR + NST, 96)):
                    for k in range(NT):
                        f = (lambda e, k=k, bank=bank, c0=c0, c1=c1: e.matmul(
                            out=ps[bank][0:c1 - c0, 0:T],
                            lhsT=wxp_bf[:, k * 96 + c0:k * 96 + c1],
                            rhs=bC[:, k * T:(k + 1) * T],
                            start=(k == 0), stop=(k == NT - 1)))
                        tp = pem(f) if k == NT - 1 else pe(f)
                p.wait("scalar", "sPE", tp)
                ac(lambda e: e.copy(out=dlr_bf[:], in_=ps[0][0:DTR, 0:T]))
                ac(lambda e: e.copy(out=b_bf[:], in_=ps[4][0:NST, 0:T]))
                ta = acm(lambda e: e.copy(out=c_bf[:], in_=ps[5][0:NST, 0:T]))
                p.wait("tensor", "sAC", ta)

                for hf in range(2):
                    for j in range(HHT):
                        n = hf * HHT + j
                        tp = pem(lambda e, n=n, j=j: e.matmul(
                            out=ps[j % 4][:, 0:T],
                            lhsT=wdt_bf[:, n * 128:(n + 1) * 128],
                            rhs=dlr_bf[:], start=True, stop=True))
                        if j % 4 == 3:
                            p.wait("scalar", "sPE", tp)
                            for jj in range(j - 3, j + 1):
                                n2 = hf * HHT + jj
                                ac(lambda e, jj=jj, n2=n2: e.copy(
                                    out=sC[:, jj * T:(jj + 1) * T],
                                    in_=ps[jj % 4][:, 0:T]))
                            ta = acm(lambda e: e.nop())
                            p.wait("tensor", "sAC", ta)
                    p.wait("vector", "sAC")
                    for j in range(HHT):
                        n = hf * HHT + j
                        s = slice(j * T, (j + 1) * T)
                        dv(lambda e, s=s, n=n: e.tensor_scalar_add(
                            out=sC[:, s], in0=sC[:, s],
                            scalar1=bdt_sb[:, n:n + 1]))
                    td = dvm(lambda e: e.nop())
                    p.wait("scalar", "sDV", td)
                    for j in range(HHT):
                        s = slice(j * T, (j + 1) * T)
                        ac(lambda e, s=s: e.activation(
                            out=sD[:, s], in_=sC[:, s], func=AF.Abs))
                        ac(lambda e, s=s: e.activation(
                            out=sD[:, s], in_=sD[:, s], func=AF.Exp,
                            scale=cst_sb[:, 2:3]))
                        ac(lambda e, s=s: e.activation(
                            out=sD[:, s], in_=sD[:, s], func=AF.Ln,
                            bias=cst_sb[:, 1:2]))
                    ta = acm(lambda e: e.nop())
                    p.wait("vector", "sAC", ta)
                    for j in range(HHT):
                        n = hf * HHT + j
                        s = slice(j * T, (j + 1) * T)
                        dv(lambda e, s=s: e.tensor_scalar_max(
                            out=sE[:, s], in0=sC[:, s], scalar1=0.0))
                        dv(lambda e, s=s: e.tensor_add(
                            out=sE[:, s], in0=sE[:, s], in1=sD[:, s]))
                        dv(lambda e, s=s, n=n: e.tensor_mul(
                            out=sF[:, s], in0=sE[:, s],
                            in1=sB[:, n * T:(n + 1) * T]))
                    td = dvm(lambda e: e.nop())
                    p.wait("scalar", "sDV", td)
                    p.wait("tensor", "sDV", td)
                    if DBG:
                        p.wait("sync", "sDV", td)
                        st(dbg_dl.rearrange("(n p) t -> p n t", p=128)
                           [:, hf * HHT:(hf + 1) * HHT, tok],
                           sE.rearrange("p (n t) -> p n t", n=HHT))

                    scan_done = {}
                    for n in range(NST):
                        pem(lambda e, n=n: e.matmul(
                            out=ps[4][:, 0:T],
                            lhsT=sel_sb[:, n * 128:(n + 1) * 128],
                            rhs=b_bf[:], start=True, stop=True))
                        tp = pem(lambda e, n=n: e.matmul(
                            out=ps[5][:, 0:T],
                            lhsT=sel_sb[:, n * 128:(n + 1) * 128],
                            rhs=c_bf[:], start=True, stop=True))
                        p.wait("scalar", "sPE", tp)
                        p.wait("vector", "sPE", tp)
                        for j in range(HHT):
                            ti = hf * HHT + j
                            it = n * HHT + j
                            da = sG[:, (it % 2) * T:(it % 2 + 1) * T]
                            dbx = sG[:, 2 * T:3 * T]
                            hh = sG[:, 3 * T:4 * T]
                            s = slice(j * T, (j + 1) * T)
                            if it >= 2:
                                p.wait("scalar", "sDV", scan_done[it - 2])
                            ta = acm(lambda e, s=s, da=da, ti=ti, n=n:
                                     e.activation(
                                         out=da, in_=sE[:, s], func=AF.Exp,
                                         scale=A_sb[:, ti * NST + n:
                                                    ti * NST + n + 1]))
                            dv(lambda e, s=s, dbx=dbx: e.tensor_tensor(
                                out=dbx, in0=sF[:, s], in1=ps[4][:, 0:T],
                                op=ALU.mult))
                            p.wait("vector", "sAC", ta)
                            dv(lambda e, da=da, dbx=dbx, hh=hh, ti=ti, n=n:
                               e.tensor_tensor_scan(
                                   out=hh, data0=da, data1=dbx,
                                   initial=hcar_sb[:, ti * NST + n:
                                                   ti * NST + n + 1],
                                   op0=ALU.mult, op1=ALU.add))
                            dv(lambda e, hh=hh, ti=ti, n=n: e.tensor_copy(
                                out=hcar_sb[:, ti * NST + n:ti * NST + n + 1],
                                in_=hh[:, T - 1:T]))
                            dv(lambda e, hh=hh, da=da: e.tensor_tensor(
                                out=da, in0=hh, in1=ps[5][:, 0:T],
                                op=ALU.mult))
                            if n == 0:
                                sd = dvm(lambda e, s=s, da=da: e.tensor_copy(
                                    out=sA[:, s], in_=da))
                            else:
                                sd = dvm(lambda e, s=s, da=da: e.tensor_add(
                                    out=sA[:, s], in0=sA[:, s], in1=da))
                            scan_done[it] = sd
                        p.wait("tensor", "sDV")

                    if DBG:
                        p.wait("sync", "sDV")
                        st(dbg_y.rearrange("(n p) t -> p n t", p=128)
                           [:, hf * HHT:(hf + 1) * HHT, tok],
                           sA.rearrange("p (n t) -> p n t", n=HHT))
                        p.wait("vector", "sST")
                    for j in range(HHT):
                        n = hf * HHT + j
                        s = slice(j * T, (j + 1) * T)
                        ac(lambda e, s=s, n=n: e.activation(
                            out=sD[:, s], in_=bZ[:, n * T:(n + 1) * T],
                            func=AF.Silu))
                    ta = acm(lambda e: e.nop())
                    p.wait("vector", "sAC", ta)
                    for j in range(HHT):
                        n = hf * HHT + j
                        s = slice(j * T, (j + 1) * T)
                        dv(lambda e, s=s, n=n: e.scalar_tensor_tensor(
                            out=sA[:, s], in0=sB[:, n * T:(n + 1) * T],
                            scalar=D_sb[:, n:n + 1], in1=sA[:, s],
                            op0=ALU.mult, op1=ALU.add))
                        dv(lambda e, s=s, n=n: e.tensor_tensor(
                            out=bC[:, n * T:(n + 1) * T], in0=sA[:, s],
                            in1=sD[:, s], op=ALU.mult))
                    td = dvm(lambda e: e.nop())
                    p.wait("tensor", "sDV", td)
                    p.wait("scalar", "sDV", td)

                for mb in range(4):
                    tl = ld(wwin[:], wbf_out[mb * 128:(mb + 1) * 128, :])
                    p.wait("tensor", "sLD", tl)
                    for m in range(2):
                        for k in range(NT):
                            f = (lambda e, m=m, k=k: e.matmul(
                                out=ps[m][:, 0:T],
                                lhsT=wwin[:, k * 256 + m * 128:
                                          k * 256 + (m + 1) * 128],
                                rhs=bC[:, k * T:(k + 1) * T],
                                start=(k == 0), stop=(k == NT - 1)))
                            tp = pem(f) if k == NT - 1 else pe(f)
                    p.wait("scalar", "sPE", tp)
                    for m in range(2):
                        gi = mb * 2 + m
                        ac(lambda e, m=m, gi=gi: e.copy(
                            out=sE[:, gi * T:(gi + 1) * T], in_=ps[m][:, 0:T]))
                    ta = acm(lambda e: e.nop())
                    p.wait("tensor", "sAC", ta)
                    p.wait("sync", "sPE", tp)
                p.wait("vector", "sAC")

                if DBG:
                    p.wait("sync", "sAC")
                    st(dbg_mo.rearrange("(k p) t -> p k t", p=128)[:, :, tok],
                       sE.rearrange("p (k t) -> p k t", k=KT))
                    p.wait("vector", "sST")
                    p.wait("scalar", "sST")
                tl = ld(sA.rearrange("p (k t) -> p k t", k=KT),
                        x_in.rearrange("(k p) t -> p k t", p=128)[:, :, tok])
                p.wait("scalar", "sLD", tl)
                p.wait("vector", "sLD", tl)
                tln = layer_norm2(sE, bA, bX, bQ)
                p.wait("vector", "sDV", tln)
                for k in range(KT):
                    s = slice(k * T, (k + 1) * T)
                    dv(lambda e, s=s: e.tensor_tensor(
                        out=sE[:, s], in0=sE[:, s], in1=ps[7][:, 0:T],
                        op=ALU.mult))
                    dv(lambda e, s=s: e.tensor_add(
                        out=sA[:, s], in0=sA[:, s], in1=sE[:, s]))
                td = dvm(lambda e: e.nop())
                p.wait("sync", "sDV", td)
                p.wait("scalar", "sDV", td)
                tsx = st(x2_dr.rearrange("(k p) t -> p k t", p=128)[:, :, tok],
                         sA.rearrange("p (k t) -> p k t", k=KT))
                p.wait("vector", "sST", tsx)
                tln = layer_norm2(sA, bA, bX, bQ)
                p.wait("sync", "sDV", tln)
                ts = st(ln2_dr.rearrange("(k p) t -> p k t", p=128)[:, :, tok],
                        bA.rearrange("p (k t) -> p k t", k=KT))
                p.wait("vector", "sST", ts)
                p.wait("scalar", "sST", ts)

        p.barrier()
        with ExitStack() as fctx:
            fsb = lambda n, s, d=F32: fctx.enter_context(nc.sbuf_tensor(n, s, d))
            fa = fsb("fa", [128, HT * T], BF16)
            fb = fsb("fb", [128, HT * T], BF16)
            fx2 = fsb("fx2", [128, KT * T])
            ffv = fsb("ffv", [128, KT * T])
            flb = fsb("flb", [128, KT * T], BF16)
            fxb = fsb("fxb", [128, 2 * T], BF16)
            fqb = fsb("fqb", [128, 2 * T], BF16)
            fss = fsb("fss", [128, T])
            scal_sb = fsb("scal_sb", [128, 4 * HT + KT])
            ld(scal_sb[:, 0:HT], s1_in)
            ld(scal_sb[:, HT:2 * HT], s2_in)
            ld(scal_sb[:, 2 * HT:3 * HT], sh1_in)
            ld(scal_sb[:, 3 * HT:4 * HT], sh2_in)
            tl = ld(scal_sb[:, 4 * HT:4 * HT + KT], s3_in)
            p.wait("scalar", "sLD", tl)
            p.wait("vector", "sLD", tl)

            for j in range(QT // T):
                p.barrier()
                def dyn_ld(dst, src_r, jj=j):
                    def f(e, dst=dst, src_r=src_r, jj=jj):
                        base = e.value_load(qidx_sb[0:1, 0:1])
                        return e.dma_start(out=dst,
                                           in_=src_r[:, :, bass.ts(base + jj, T)])
                    return _dma("sLD", f)

                t1 = dyn_ld(flb.rearrange("p (k t) -> p k t", k=KT),
                            ln2_dr.rearrange("(k p) t -> p k t", p=128))
                t2 = dyn_ld(fx2.rearrange("p (k t) -> p k t", k=KT),
                            x2_dr.rearrange("(k p) t -> p k t", p=128))
                for tok in (t1, t2):
                    p.wait("tensor", "sLD", tok)
                    p.wait("vector", "sLD", tok)
                    p.wait("scalar", "sLD", tok)

                for m in range(HT):
                    t1 = ld(wwin[:, 0:KT * 128],
                            wbf_1[m * 128:(m + 1) * 128, :])
                    t2 = ld(wwin[:, 2048:2048 + KT * 128],
                            wbf_2[m * 128:(m + 1) * 128, :])
                    p.wait("tensor", "sLD", t1)
                    p.wait("tensor", "sLD", t2)
                    for k in range(KT):
                        f = (lambda e, k=k: e.matmul(
                            out=ps[0][:, 0:T],
                            lhsT=wwin[:, k * 128:(k + 1) * 128],
                            rhs=flb[:, k * T:(k + 1) * T],
                            start=(k == 0), stop=(k == KT - 1)))
                        tp = pem(f) if k == KT - 1 else pe(f)
                    for k in range(KT):
                        f = (lambda e, k=k: e.matmul(
                            out=ps[1][:, 0:T],
                            lhsT=wwin[:, 2048 + k * 128:2048 + (k + 1) * 128],
                            rhs=flb[:, k * T:(k + 1) * T],
                            start=(k == 0), stop=(k == KT - 1)))
                        tp = pem(f) if k == KT - 1 else pe(f)
                    p.wait("scalar", "sPE", tp)
                    ta = acm(lambda e, m=m: e.activation(
                        out=fss[:], in_=ps[0][:, 0:T], func=AF.Silu,
                        scale=scal_sb[:, m:m + 1]))
                    p.wait("vector", "sAC", ta)
                    td = dvm(lambda e, m=m: e.scalar_tensor_tensor(
                        out=fa[:, m * T:(m + 1) * T], in0=ps[1][:, 0:T],
                        scalar=scal_sb[:, HT + m:HT + m + 1], in1=fss[:],
                        op0=ALU.mult, op1=ALU.mult))
                    p.wait("tensor", "sDV", td)
                    p.wait("sync", "sPE", tp)

                def hlayer(wsrc, scol, src, dst):
                    for m in range(HT):
                        toks = [ld(wwin[:, kb * 2048:(kb + 1) * 2048],
                                   wsrc[m * 128:(m + 1) * 128,
                                        kb * 2048:(kb + 1) * 2048])
                                for kb in range(2)]
                        for tok in toks:
                            p.wait("tensor", "sLD", tok)
                        for k in range(HT):
                            f = (lambda e, k=k: e.matmul(
                                out=ps[2][:, 0:T],
                                lhsT=wwin[:, k * 128:(k + 1) * 128],
                                rhs=src[:, k * T:(k + 1) * T],
                                start=(k == 0), stop=(k == HT - 1)))
                            tp = pem(f) if k == HT - 1 else pe(f)
                        p.wait("scalar", "sPE", tp)
                        ta = acm(lambda e, m=m: e.activation(
                            out=dst[:, m * T:(m + 1) * T], in_=ps[2][:, 0:T],
                            func=AF.Gelu,
                            scale=scal_sb[:, scol + m:scol + m + 1]))
                        p.wait("tensor", "sAC", ta)
                        p.wait("sync", "sPE", tp)

                hlayer(wbf_h1, 2 * HT, fa, fb)
                hlayer(wbf_h2, 3 * HT, fb, fa)

                for m in range(KT):
                    toks = [ld(wwin[:, kb * 2048:(kb + 1) * 2048],
                               wbf_3[m * 128:(m + 1) * 128,
                                     kb * 2048:(kb + 1) * 2048])
                            for kb in range(2)]
                    for tok in toks:
                        p.wait("tensor", "sLD", tok)
                    for k in range(HT):
                        f = (lambda e, k=k: e.matmul(
                            out=ps[3][:, 0:T],
                            lhsT=wwin[:, k * 128:(k + 1) * 128],
                            rhs=fa[:, k * T:(k + 1) * T],
                            start=(k == 0), stop=(k == HT - 1)))
                        tp = pem(f) if k == HT - 1 else pe(f)
                    p.wait("scalar", "sPE", tp)
                    ta = acm(lambda e, m=m: e.activation(
                        out=ffv[:, m * T:(m + 1) * T], in_=ps[3][:, 0:T],
                        func=AF.Copy,
                        scale=scal_sb[:, 4 * HT + m:4 * HT + m + 1]))
                    p.wait("tensor", "sAC", ta)
                    p.wait("sync", "sPE", tp)
                p.wait("vector", "sAC")

                tln = layer_norm2(ffv, flb, fxb, fqb)
                p.wait("vector", "sDV", tln)
                for k in range(KT):
                    s = slice(k * T, (k + 1) * T)
                    dv(lambda e, s=s: e.tensor_tensor(
                        out=ffv[:, s], in0=ffv[:, s], in1=ps[7][:, 0:T],
                        op=ALU.mult))
                    dv(lambda e, s=s: e.tensor_add(
                        out=fx2[:, s], in0=fx2[:, s], in1=ffv[:, s]))
                td = dvm(lambda e: e.nop())
                p.wait("sync", "sDV", td)
                st(y_out.rearrange("(k p) t -> p k t", p=128)
                   [:, :, j * T:(j + 1) * T],
                   fx2.rearrange("p (k t) -> p k t", k=KT))

        p.wait("sync", "sST")

        @block.sync
        def _(sync):
            for fn in p.ops["sync"]:
                fn(sync)

        @block.tensor
        def _(tensor):
            for fn in p.ops["tensor"]:
                fn(tensor)

        @block.scalar
        def _(scalar):
            for fn in p.ops["scalar"]:
                fn(scalar)

        @block.vector
        def _(vector):
            for fn in p.ops["vector"]:
                fn(vector)

    return nc


def _pack_wdt(w):
    out = np.empty((128, DI // 2), np.float32)
    for n in range(NT):
        out[DTR * (n % 2):DTR * (n % 2) + DTR,
            (n // 2) * 128:(n // 2) * 128 + 128] = w[:, n * 128:(n + 1) * 128]
    return np.ascontiguousarray(out)


def _prep_inputs(kw):
    f32 = np.float32
    cac = lambda a: np.ascontiguousarray(np.asarray(a, f32))

    def tile_vec(v, ntiles):
        return cac(np.asarray(v, f32).reshape(ntiles, 128).T)

    x = np.asarray(kw["L0_attn_x"], f32)
    x_t = np.ascontiguousarray(np.transpose(x, (0, 2, 1)))

    A_log = np.asarray(kw["L0_attn_A_log"], f32)
    alog_t = cac(A_log.reshape(NT, 128, NST).transpose(1, 0, 2)
                 .reshape(128, NT * NST))
    wconv = np.asarray(kw["L0_attn_W_conv"], f32).reshape(DI, 4)
    wconv_t = cac(wconv.reshape(NT, 128, 4).transpose(1, 0, 2)
                  .reshape(128, NT * 4))
    sel = np.zeros((NST, NST * 128), ml_dtypes.bfloat16)
    for n in range(NST):
        sel[n, n * 128:(n + 1) * 128] = 1
    common = dict(
        w_in=cac(kw["L0_attn_W_in"]),
        wconv_t=wconv_t,
        bconv_t=tile_vec(kw["L0_attn_b_conv"], NT),
        w_xproj=cac(kw["L0_attn_W_xproj"]),
        w_dt=cac(kw["L0_attn_W_dt"]),
        bdt_t=tile_vec(kw["L0_attn_b_dt"], NT),
        alog_t=alog_t,
        d_t=tile_vec(kw["L0_attn_D"], NT),
        w_out=cac(kw["L0_attn_W_out"]),
        w1t=cac(np.asarray(kw["L0_ffn_w1_w"], f32).T),
        w2t=cac(np.asarray(kw["L0_ffn_w2_w"], f32).T),
        h1t=cac(np.asarray(kw["L0_ffn_h1_w"], f32).T),
        h2t=cac(np.asarray(kw["L0_ffn_h2_w"], f32).T),
        w3t=cac(np.asarray(kw["L0_ffn_w3_w"], f32).T),
        s1_t=tile_vec(kw["L0_ffn_w1_s"], HT),
        s2_t=tile_vec(kw["L0_ffn_w2_s"], HT),
        sh1_t=tile_vec(kw["L0_ffn_h1_s"], HT),
        sh2_t=tile_vec(kw["L0_ffn_h2_s"], HT),
        s3_t=tile_vec(kw["L0_ffn_w3_s"], KT),
        sel_bf=sel,
        ones_bf=np.ones((128, 1), ml_dtypes.bfloat16),
        one1_bf=np.ones((1, 128), ml_dtypes.bfloat16),
        cst=np.tile(np.array([[EPS, 1.0, -1.0, 0.0]], np.float32), (128, 1)),
    )
    in_maps = []
    for c in range(8):
        s, q = c // 4, c % 4
        m = dict(common)
        m["x_seq"] = x_t[s]
        m["qidx"] = np.full((1, 1), q * (QT // T), np.int32)
        in_maps.append(m)
    return in_maps


def kernel(**inputs):
    in_maps = _prep_inputs(inputs)
    nc = build_program()
    res = run_bass_kernel_spmd(nc, in_maps, core_ids=list(range(8)))
    out = np.empty((B, L, DM), np.float32)
    for c in range(8):
        s, q = c // 4, c % 4
        out[s, q * QT:(q + 1) * QT, :] = res.results[c]["y_out"].T
    return out


# version 04
# speedup vs baseline: 1.1743x; 1.1743x over previous
import numpy as np
import ml_dtypes
import concourse.bass as bass
import concourse.mybir as mybir
from concourse.bass_utils import run_bass_kernel_spmd
from contextlib import ExitStack

F32 = mybir.dt.float32
BF16 = mybir.dt.bfloat16
I32 = mybir.dt.int32
AF = mybir.ActivationFunctionType
ALU = mybir.AluOpType

DM, DI, DH, NST, DTR = 1024, 2048, 4096, 16, 64
DBG = False
L, B = 8192, 2
T = 512
NCH = L // T
QT = 2048
EPS = 1e-5
KT = DM // 128
NT = DI // 128
HT = DH // 128
HHT = NT // 2

ENGS = ("sync", "tensor", "scalar", "vector")


class Prog:
    def __init__(self, nc):
        self.nc = nc
        self.ops = {e: [] for e in ENGS}
        self.cnt = {}
        self.sems = {}
        self.last_wait = {e: {} for e in ENGS}

    def sem(self, ctx, name):
        h = ctx.enter_context(self.nc.semaphore(name))
        self.cnt[name] = 0
        self.sems[name] = h
        return h

    def on(self, eng, fn):
        self.ops[eng].append(fn)

    def wait(self, eng, name, val=None):
        if isinstance(name, tuple):
            name, val = name
        elif isinstance(val, tuple):
            name, val = val
        if name in ("sLD", "sST") and val is None:
            for s in self.pools[name]:
                self.wait(eng, s, self.cnt[s])
            return
        if val is None:
            val = self.cnt[name]
        if val <= 0 or self.last_wait[eng].get(name, -1) >= val:
            return
        self.last_wait[eng][name] = val
        h = self.sems[name]
        self.on(eng, lambda e, h=h, v=val: e.wait_ge(h, v))

    def mark(self, eng, name, fn, amt):
        self.cnt[name] += amt
        h = self.sems[name]
        self.on(eng, lambda e, h=h, a=amt, f=fn: f(e).then_inc(h, a))
        return self.cnt[name]

    def emit(self, eng, fn):
        self.on(eng, fn)

    def barrier(self):
        for e in ENGS:
            for s in ("sPE", "sAC", "sDV"):
                self.wait(e, s)
            self.wait(e, "sLD")
            self.wait(e, "sST")


def build_program():
    nc = bass.Bass("TRN2", target_bir_lowering=False)
    dram = nc.dram_tensor

    x_in = dram("x_seq", [DM, L], F32, kind="ExternalInput").ap()
    qidx_in = dram("qidx", [1, 1], I32, kind="ExternalInput").ap()
    w_in = dram("w_in", [DM, 2 * DI], F32, kind="ExternalInput").ap()
    wconv_in = dram("wconv_t", [128, NT * 4], F32, kind="ExternalInput").ap()
    bconv_in = dram("bconv_t", [128, NT], F32, kind="ExternalInput").ap()
    wxp_in = dram("w_xproj", [DI, 96], F32, kind="ExternalInput").ap()
    wdt_in = dram("w_dt", [DTR, DI], F32, kind="ExternalInput").ap()
    bdt_in = dram("bdt_t", [128, NT], F32, kind="ExternalInput").ap()
    alog_in = dram("alog_t", [128, NT * NST], F32, kind="ExternalInput").ap()
    d_in = dram("d_t", [128, NT], F32, kind="ExternalInput").ap()
    w_out_in = dram("w_out", [DI, DM], F32, kind="ExternalInput").ap()
    w1_in = dram("w1t", [DM, DH], F32, kind="ExternalInput").ap()
    w2_in = dram("w2t", [DM, DH], F32, kind="ExternalInput").ap()
    h1_in = dram("h1t", [DH, DH], F32, kind="ExternalInput").ap()
    h2_in = dram("h2t", [DH, DH], F32, kind="ExternalInput").ap()
    w3_in = dram("w3t", [DH, DM], F32, kind="ExternalInput").ap()
    s1_in = dram("s1_t", [128, HT], F32, kind="ExternalInput").ap()
    s2_in = dram("s2_t", [128, HT], F32, kind="ExternalInput").ap()
    sh1_in = dram("sh1_t", [128, HT], F32, kind="ExternalInput").ap()
    sh2_in = dram("sh2_t", [128, HT], F32, kind="ExternalInput").ap()
    s3_in = dram("s3_t", [128, KT], F32, kind="ExternalInput").ap()
    sel_in = dram("sel_bf", [NST, NST * 128], BF16, kind="ExternalInput").ap()
    ones_in = dram("ones_bf", [128, 1], BF16, kind="ExternalInput").ap()
    one1_in = dram("one1_bf", [1, 128], BF16, kind="ExternalInput").ap()
    id_in = dram("id_bf", [128, 128], BF16, kind="ExternalInput").ap()
    cst_in = dram("cst", [128, 4], F32, kind="ExternalInput").ap()

    y_out = dram("y_out", [DM, QT], F32, kind="ExternalOutput").ap()

    wbf_in = dram("wbf_in", [8 * 128, KT * 512], BF16, kind="Internal").ap()
    wbf_out = dram("wbf_out", [4 * 128, NT * 256], BF16, kind="Internal").ap()
    wbf_1 = dram("wbf_1", [HT * 128, KT * 128], BF16, kind="Internal").ap()
    wbf_2 = dram("wbf_2", [HT * 128, KT * 128], BF16, kind="Internal").ap()
    wbf_h1 = dram("wbf_h1", [HT * 128, HT * 128], BF16, kind="Internal").ap()
    wbf_h2 = dram("wbf_h2", [HT * 128, HT * 128], BF16, kind="Internal").ap()
    wbf_3 = dram("wbf_3", [KT * 128, HT * 128], BF16, kind="Internal").ap()
    bc_dr = dram("bc_dr", [2 * NST, T], BF16, kind="Internal").ap()
    x2_dr = dram("x2_dr", [DM, L], F32, kind="Internal").ap()
    ln2_dr = dram("ln2_dr", [DM, L], BF16, kind="Internal").ap()
    if DBG:
        dbg_xc = dram("dbg_xc", [DI, L], F32, kind="Internal").ap()
        dbg_dl = dram("dbg_dl", [DI, L], F32, kind="Internal").ap()
        dbg_y = dram("dbg_y", [DI, L], F32, kind="Internal").ap()
        dbg_mo = dram("dbg_mo", [DM, L], F32, kind="Internal").ap()
        dbg_ln1 = dram("dbg_ln1", [DM, L], BF16, kind="Internal").ap()
        dbg_xi = dram("dbg_xi", [DI, L], BF16, kind="Internal").ap()
        dbg_z = dram("dbg_z", [DI, L], BF16, kind="Internal").ap()

    with ExitStack() as ctx:
        p = Prog(nc)
        for s in ("sPE", "sAC", "sDV"):
            p.sem(ctx, s)
        p.pools = {"sLD": [], "sST": []}
        for i in range(16):
            n = f"sL{i}"
            p.sem(ctx, n)
            p.pools["sLD"].append(n)
        for i in range(8):
            n = f"sS{i}"
            p.sem(ctx, n)
            p.pools["sST"].append(n)
        p.rot = {"sLD": 0, "sST": 0}
        sb = lambda n, s, d=F32: ctx.enter_context(nc.sbuf_tensor(n, s, d))

        A_sb = sb("A_sb", [128, NT * NST])
        D_sb = sb("D_sb", [128, NT])
        bconv_sb = sb("bconv_sb", [128, NT])
        bdt_sb = sb("bdt_sb", [128, NT])
        wconv_sb = sb("wconv_sb", [128, NT * 4])
        wxp_bf = sb("wxp_bf", [128, NT * 96], BF16)
        wdt_bf = sb("wdt_bf", [DTR, DI], BF16)
        sel_sb = sb("sel_sb", [NST, NST * 128], BF16)
        ones_sb = sb("ones_sb", [128, 1], BF16)
        one1_sb = sb("one1_sb", [1, 128], BF16)
        cst_sb = sb("cst_sb", [128, 4])
        mr_bf = sb("mr_bf", [1, 2 * T], BF16)
        hist_bf = sb("hist_bf", [128, NT * 3], BF16)
        hcar_sb = sb("hcar_sb", [128, NT * NST])
        qidx_sb = ctx.enter_context(nc.sbuf_tensor("qidx_sb", [1, 1], I32))
        dlr_bf = sb("dlr_bf", [DTR, T], BF16)
        b_bf = sb("b_bf", [NST, T], BF16)
        c_bf = sb("c_bf", [NST, T], BF16)
        wwin = sb("wwin", [128, 4096], BF16)
        id_sb = sb("id_sb", [128, 128], BF16)

        ps = [ctx.enter_context(nc.psum_tensor(f"ps{i}", [128, 512], F32))
              for i in range(8)]

        block = ctx.enter_context(nc.Block())

        def _dma(pool, fn):
            i = p.rot[pool]
            p.rot[pool] = (i + 1) % len(p.pools[pool])
            sname = p.pools[pool][i]
            p.wait("sync", sname, p.cnt[sname])
            v = p.mark("sync", sname, fn, 16)
            return (sname, v)

        def ld(dst, src):
            return _dma("sLD",
                        lambda e, d=dst, r=src: e.dma_start(out=d, in_=r))

        def st(dst, src):
            return _dma("sST",
                        lambda e, d=dst, r=src: e.dma_start(out=d, in_=r))

        def pe(fn):
            p.emit("tensor", fn)

        def pem(fn):
            return p.mark("tensor", "sPE", fn, 1)

        def ac(fn):
            p.wait("scalar", "sAC", p.cnt["sAC"])
            return p.mark("scalar", "sAC", fn, 1)

        acm = ac

        def dv(fn):
            p.wait("vector", "sDV", p.cnt["sDV"])
            return p.mark("vector", "sDV", fn, 1)

        dvm = dv

        with nc.sbuf_tensor("cast_f", [128, 4096], F32) as cast_f, \
             nc.sbuf_tensor("cast_b", [128, 4096], BF16) as cast_b:

            ld(D_sb[:], d_in)
            ld(bconv_sb[:], bconv_in)
            ld(bdt_sb[:], bdt_in)
            ld(wconv_sb[:], wconv_in)
            ld(sel_sb[:], sel_in)
            ld(ones_sb[:], ones_in)
            ld(one1_sb[:], one1_in)
            ld(id_sb[:], id_in)
            ld(qidx_sb[:], qidx_in)
            ld(cst_sb[:], cst_in)

            ld(cast_f[:, 0:NT * NST], alog_in)
            p.wait("scalar", "sLD")
            acm(lambda e: e.activation(out=A_sb[:], in_=cast_f[:, 0:NT * NST],
                                       func=AF.Exp))
            p.wait("sync", "sAC")
            p.wait("vector", "sAC")
            dvm(lambda e: e.tensor_scalar_mul(out=A_sb[:], in0=A_sb[:],
                                              scalar1=-1.0))

            ld(cast_f[:, 0:NT * 96],
               wxp_in.rearrange("(n p) c -> p n c", p=128))
            p.wait("scalar", "sLD")
            acm(lambda e: e.copy(out=wxp_bf[:], in_=cast_f[:, 0:NT * 96]))
            p.wait("sync", "sAC")
            ld(cast_f[0:DTR, 0:DI], wdt_in)
            p.wait("scalar", "sLD")
            acm(lambda e: e.copy(out=wdt_bf[:], in_=cast_f[0:DTR, 0:DI]))
            p.wait("sync", "sAC")

            def cast_tiled(src, dst, krows, mcols, slab_m, mc):
                for k in range(krows):
                    p.wait("sync", "sAC")
                    tl = ld(cast_f[:, 0:mcols * 128],
                            src[k * 128:(k + 1) * 128, :])
                    p.wait("scalar", "sLD", tl)
                    tc = acm(lambda e: e.copy(out=cast_b[:, 0:mcols * 128],
                                              in_=cast_f[:, 0:mcols * 128]))
                    p.wait("sync", "sAC", tc)
                    tsl = 0
                    for mb in range(mcols // slab_m):
                        w = slab_m * 128
                        tsl = st(dst[mb * 128:(mb + 1) * 128,
                                     k * w:(k + 1) * w],
                                 cast_b[:, mb * w:(mb + 1) * w])
                    p.wait("scalar", "sST")

            cast_tiled(w_in, wbf_in, KT, 2 * NT, 4, 512)
            cast_tiled(w_out_in, wbf_out, NT, KT, 2, 256)
            cast_tiled(w1_in, wbf_1, KT, HT, 1, 128)
            cast_tiled(w2_in, wbf_2, KT, HT, 1, 128)
            cast_tiled(h1_in, wbf_h1, HT, HT, 1, 128)
            cast_tiled(h2_in, wbf_h2, HT, HT, 1, 128)
            cast_tiled(w3_in, wbf_3, HT, KT, 1, 128)
            p.barrier()

        dvm(lambda e: e.memset(hist_bf[:], 0.0))
        dvm(lambda e: e.memset(hcar_sb[:], 0.0))
        p.barrier()

        def layer_norm2(v, out_bf, xb, qb):
            pe_done = []
            if getattr(p, "_ln_last_pe", 0):
                p.wait("scalar", "sPE", p._ln_last_pe)
            for k in range(KT):
                s = slice(k * T, (k + 1) * T)
                sl = slice((k % 2) * T, (k % 2 + 1) * T)
                if k >= 2:
                    p.wait("scalar", "sPE", pe_done[k - 2])
                ac(lambda e, s=s, sl=sl: e.copy(out=xb[:, sl], in_=v[:, s]))
                ta = acm(lambda e, s=s, sl=sl: e.activation(
                    out=qb[:, sl], in_=v[:, s], func=AF.Square))
                p.wait("tensor", "sAC", ta)
                if k < KT - 1:
                    pe(lambda e, sl=sl, k=k: e.matmul(
                        out=ps[6][0:1, 0:T], lhsT=ones_sb[:], rhs=xb[:, sl],
                        start=(k == 0), stop=False))
                else:
                    pem(lambda e, sl=sl, k=k: e.matmul(
                        out=ps[6][0:1, 0:T], lhsT=ones_sb[:], rhs=xb[:, sl],
                        start=(k == 0), stop=True))
                pe_done.append(pem(lambda e, sl=sl, k=k: e.matmul(
                    out=ps[7][0:1, 0:T], lhsT=ones_sb[:], rhs=qb[:, sl],
                    start=(k == 0), stop=(k == KT - 1))))
            p._ln_last_pe = pe_done[-1]
            p.wait("vector", "sPE")
            dv(lambda e: e.tensor_scalar_mul(out=ps[6][0:1, 0:T],
                                             in0=ps[6][0:1, 0:T],
                                             scalar1=1.0 / DM))
            td = dvm(lambda e: e.tensor_scalar_mul(out=ps[7][0:1, 0:T],
                                                   in0=ps[7][0:1, 0:T],
                                                   scalar1=1.0 / DM))
            p.wait("scalar", "sDV", td)
            ac(lambda e: e.copy(out=mr_bf[0:1, 0:T], in_=ps[6][0:1, 0:T]))
            ta = acm(lambda e: e.activation(out=mr_bf[0:1, T:2 * T],
                                            in_=ps[6][0:1, 0:T],
                                            func=AF.Square))
            p.wait("vector", "sAC", ta)
            td = dvm(lambda e: e.tensor_sub(out=ps[7][0:1, 0:T],
                                            in0=ps[7][0:1, 0:T],
                                            in1=mr_bf[0:1, T:2 * T]))
            p.wait("scalar", "sDV", td)
            ta = acm(lambda e: e.activation(out=ps[7][0:1, 0:T],
                                            in_=ps[7][0:1, 0:T],
                                            func=AF.Sqrt,
                                            bias=cst_sb[0:1, 0:1]))
            p.wait("vector", "sAC", ta)
            td = dvm(lambda e: e.reciprocal(out=ps[7][0:1, 0:T],
                                            in_=ps[7][0:1, 0:T]))
            p.wait("scalar", "sDV", td)
            ta = acm(lambda e: e.copy(out=mr_bf[0:1, T:2 * T],
                                      in_=ps[7][0:1, 0:T]))
            p.wait("tensor", "sAC", ta)
            pem(lambda e: e.matmul(out=ps[6][:, 0:T], lhsT=one1_sb[:],
                                   rhs=mr_bf[0:1, 0:T], start=True, stop=True))
            tp = pem(lambda e: e.matmul(out=ps[7][:, 0:T], lhsT=one1_sb[:],
                                        rhs=mr_bf[0:1, T:2 * T],
                                        start=True, stop=True))
            p.wait("vector", "sPE", tp)
            for k in range(KT):
                s = slice(k * T, (k + 1) * T)
                dv(lambda e, s=s: e.scalar_tensor_tensor(
                    out=v[:, s], in0=ps[6][:, 0:T], scalar=-1.0, in1=v[:, s],
                    op0=ALU.mult, op1=ALU.add))
                dv(lambda e, s=s: e.tensor_tensor(
                    out=out_bf[:, s], in0=v[:, s], in1=ps[7][:, 0:T],
                    op=ALU.mult))
            return dvm(lambda e: e.nop())

        with ExitStack() as mctx:
            msb = lambda n, s, d=F32: mctx.enter_context(nc.sbuf_tensor(n, s, d))
            sA = msb("sA", [128, KT * T])
            sB = msb("sB", [128, NT * T])
            sC = msb("sC", [128, HHT * T])
            sD = msb("sD", [128, HHT * T])
            sE = msb("sE", [128, HHT * T])
            sF = msb("sF", [128, HHT * T], BF16)
            sG = msb("sG", [128, 6 * T], BF16)
            bcb = msb("bcb", [128, 4 * T], BF16)
            bA = msb("bA", [128, KT * T], BF16)
            bX = msb("bX", [128, 2 * T], BF16)
            bQ = msb("bQ", [128, 2 * T], BF16)
            bI = msb("bI", [128, NT * T], BF16)
            bZ = msb("bZ", [128, NT * T], BF16)
            bC = msb("bC", [128, NT * T], BF16)

            bc_last_toks = []
            edges = {}
            for c in range(NCH):
                tok = slice(c * T, (c + 1) * T)
                if c > 0:
                    bc_last_toks = bc_last_toks[-4:]

                if "sA_free" in edges:
                    p.wait("sync", "sDV", edges["sA_free"])
                    p.wait("sync", edges["x2_store"])
                    p.wait("sync", edges["ln2_store"])
                tl = ld(sA.rearrange("p (k t) -> p k t", k=KT),
                        x_in.rearrange("(k p) t -> p k t", p=128)[:, :, tok])
                p.wait("scalar", "sLD", tl)
                p.wait("vector", "sLD", tl)
                if "tp_win" in edges:
                    p.wait("vector", "sPE", edges["tp_win"])
                    p.wait("scalar", "sPE", edges["tp_win"])
                tln = layer_norm2(sA, bA, bX, bQ)
                p.wait("tensor", "sDV", tln)
                if DBG:
                    p.wait("sync", "sDV", tln)
                    st(dbg_ln1.rearrange("(k p) t -> p k t", p=128)[:, :, tok],
                       bA.rearrange("p (k t) -> p k t", k=KT))

                if "td_conv" in edges:
                    p.wait("scalar", "sDV", edges["td_conv"])
                for mb in range(8):
                    tl = ld(wwin[:], wbf_in[mb * 128:(mb + 1) * 128, :])
                    p.wait("tensor", "sLD", tl)
                    for m in range(4):
                        for k in range(KT):
                            f = (lambda e, m=m, k=k: e.matmul(
                                out=ps[m][:, 0:T],
                                lhsT=wwin[:, k * 512 + m * 128:
                                          k * 512 + (m + 1) * 128],
                                rhs=bA[:, k * T:(k + 1) * T],
                                start=(k == 0), stop=(k == KT - 1)))
                            tp = pem(f) if k == KT - 1 else pe(f)
                    p.wait("scalar", "sPE", tp)
                    for m in range(4):
                        gi = mb * 4 + m
                        if gi < NT:
                            dst = bI[:, gi * T:(gi + 1) * T]
                        else:
                            dst = bZ[:, (gi - NT) * T:(gi - NT + 1) * T]
                        ac(lambda e, m=m, d=dst: e.copy(out=d,
                                                        in_=ps[m][:, 0:T]))
                    ta = acm(lambda e: e.nop())
                    p.wait("tensor", "sAC", ta)
                    p.wait("sync", "sPE", tp)
                edges["tp_win"] = tp
                p.wait("vector", "sAC")
                if DBG:
                    p.wait("sync", "sAC")
                    st(dbg_xi.rearrange("(n p) t -> p n t", p=128)[:, :, tok],
                       bI.rearrange("p (n t) -> p n t", n=NT)[:, :, 0:T])
                    st(dbg_z.rearrange("(n p) t -> p n t", p=128)[:, :, tok],
                       bZ.rearrange("p (n t) -> p n t", n=NT))

                if "ta_sp" in edges:
                    p.wait("vector", "sAC", edges["ta_sp"])
                if "td_gate" in edges:
                    p.wait("scalar", "sDV", edges["td_gate"])
                for hf in range(2):
                    for j in range(HHT):
                        n = hf * HHT + j
                        asl = slice(j * T, (j + 1) * T)
                        for k in range(4):
                            wcol = wconv_sb[:, n * 4 + k:n * 4 + k + 1]
                            lead = 3 - k
                            if lead > 0:
                                hsl = hist_bf[:, n * 3 + k:n * 3 + 3]
                                if k == 0:
                                    dv(lambda e, a=asl, h=hsl, w=wcol, ld_=lead:
                                       e.tensor_scalar(
                                           out=sC[:, a][:, 0:ld_], in0=h,
                                           scalar1=w, scalar2=None,
                                           op0=ALU.mult))
                                else:
                                    dv(lambda e, a=asl, h=hsl, w=wcol, ld_=lead:
                                       e.scalar_tensor_tensor(
                                           out=sC[:, a][:, 0:ld_], in0=h,
                                           scalar=w,
                                           in1=sC[:, a][:, 0:ld_],
                                           op0=ALU.mult, op1=ALU.add))
                            bsl = bI[:, n * T:n * T + T - lead]
                            osl = sC[:, j * T + lead:(j + 1) * T]
                            if k == 0:
                                dv(lambda e, b=bsl, o=osl, w=wcol:
                                   e.tensor_scalar(out=o, in0=b, scalar1=w,
                                                   scalar2=None, op0=ALU.mult))
                            else:
                                dv(lambda e, b=bsl, o=osl, w=wcol:
                                   e.scalar_tensor_tensor(
                                       out=o, in0=b, scalar=w, in1=o,
                                       op0=ALU.mult, op1=ALU.add))
                        dv(lambda e, n=n: e.tensor_copy(
                            out=hist_bf[:, n * 3:n * 3 + 3],
                            in_=bI[:, n * T + T - 3:n * T + T]))
                    td = dvm(lambda e: e.nop())
                    edges["td_conv"] = td
                    p.wait("scalar", "sDV", td)
                    for j in range(HHT):
                        n = hf * HHT + j
                        ac(lambda e, j=j, n=n: e.activation(
                            out=sB[:, n * T:(n + 1) * T],
                            in_=sC[:, j * T:(j + 1) * T], func=AF.Silu,
                            bias=bconv_sb[:, n:n + 1]))
                    ta = acm(lambda e: e.nop())
                    p.wait("vector", "sAC", ta)
                if "tp_wout" in edges:
                    p.wait("vector", "sPE", edges["tp_wout"])
                for n in range(NT):
                    s = slice(n * T, (n + 1) * T)
                    dv(lambda e, s=s: e.tensor_copy(out=bC[:, s], in_=sB[:, s]))
                td = dvm(lambda e: e.nop())
                p.wait("tensor", "sDV", td)
                if DBG:
                    p.wait("sync", "sDV", td)
                    st(dbg_xc.rearrange("(n p) t -> p n t", p=128)[:, :, tok],
                       sB.rearrange("p (n t) -> p n t", n=NT))

                for (bank, c0, c1) in ((0, 0, DTR), (4, DTR, DTR + NST),
                                       (5, DTR + NST, 96)):
                    for k in range(NT):
                        f = (lambda e, k=k, bank=bank, c0=c0, c1=c1: e.matmul(
                            out=ps[bank][0:c1 - c0, 0:T],
                            lhsT=wxp_bf[:, k * 96 + c0:k * 96 + c1],
                            rhs=bC[:, k * T:(k + 1) * T],
                            start=(k == 0), stop=(k == NT - 1)))
                        tp = pem(f) if k == NT - 1 else pe(f)
                p.wait("scalar", "sPE", tp)
                ac(lambda e: e.copy(out=dlr_bf[:], in_=ps[0][0:DTR, 0:T]))
                ac(lambda e: e.copy(out=b_bf[:], in_=ps[4][0:NST, 0:T]))
                ta = acm(lambda e: e.copy(out=c_bf[:], in_=ps[5][0:NST, 0:T]))
                p.wait("tensor", "sAC", ta)
                p.wait("sync", "sAC", ta)
                for tkk in bc_last_toks:
                    p.wait("sync", tkk)
                tb1 = st(bc_dr[0:NST, :], b_bf[:])
                tb2 = st(bc_dr[NST:2 * NST, :], c_bf[:])

                if "tln_mo" in edges:
                    p.wait("scalar", "sDV", edges["tln_mo"])
                for hf in range(2):
                    for j in range(HHT):
                        n = hf * HHT + j
                        tp = pem(lambda e, n=n, j=j: e.matmul(
                            out=ps[j % 4][:, 0:T],
                            lhsT=wdt_bf[:, n * 128:(n + 1) * 128],
                            rhs=dlr_bf[:], start=True, stop=True))
                        if j % 4 == 3:
                            p.wait("scalar", "sPE", tp)
                            for jj in range(j - 3, j + 1):
                                n2 = hf * HHT + jj
                                ac(lambda e, jj=jj, n2=n2: e.copy(
                                    out=sC[:, jj * T:(jj + 1) * T],
                                    in_=ps[jj % 4][:, 0:T]))
                            ta = acm(lambda e: e.nop())
                            p.wait("tensor", "sAC", ta)
                    p.wait("vector", "sAC")
                    for j in range(HHT):
                        n = hf * HHT + j
                        s = slice(j * T, (j + 1) * T)
                        dv(lambda e, s=s, n=n: e.tensor_scalar_add(
                            out=sC[:, s], in0=sC[:, s],
                            scalar1=bdt_sb[:, n:n + 1]))
                    td = dvm(lambda e: e.nop())
                    p.wait("scalar", "sDV", td)
                    for j in range(HHT):
                        s = slice(j * T, (j + 1) * T)
                        ac(lambda e, s=s: e.activation(
                            out=sD[:, s], in_=sC[:, s], func=AF.Abs))
                        ac(lambda e, s=s: e.activation(
                            out=sD[:, s], in_=sD[:, s], func=AF.Exp,
                            scale=cst_sb[:, 2:3]))
                        ac(lambda e, s=s: e.activation(
                            out=sD[:, s], in_=sD[:, s], func=AF.Ln,
                            bias=cst_sb[:, 1:2]))
                    ta = acm(lambda e: e.nop())
                    edges["ta_sp"] = ta
                    p.wait("vector", "sAC", ta)
                    for j in range(HHT):
                        n = hf * HHT + j
                        s = slice(j * T, (j + 1) * T)
                        dv(lambda e, s=s: e.tensor_scalar_max(
                            out=sE[:, s], in0=sC[:, s], scalar1=0.0))
                        dv(lambda e, s=s: e.tensor_add(
                            out=sE[:, s], in0=sE[:, s], in1=sD[:, s]))
                        dv(lambda e, s=s, n=n: e.tensor_mul(
                            out=sF[:, s], in0=sE[:, s],
                            in1=sB[:, n * T:(n + 1) * T]))
                    td = dvm(lambda e: e.nop())
                    p.wait("scalar", "sDV", td)
                    p.wait("tensor", "sDV", td)
                    if DBG:
                        p.wait("sync", "sDV", td)
                        st(dbg_dl.rearrange("(n p) t -> p n t", p=128)
                           [:, hf * HHT:(hf + 1) * HHT, tok],
                           sE.rearrange("p (n t) -> p n t", n=HHT))

                    scan_done = {}
                    yp_dv = {}
                    yp_pe = {}
                    bc_free = {}
                    bc_tok = {}
                    for n in range(NST):
                        sl = n % 2
                        if n == 0:
                            p.wait("sync", "sST", tb1)
                            p.wait("sync", "sST", tb2)
                            p.wait("sync", "sDV")
                            p.wait("tensor", "sAC")
                        if n >= 2:
                            p.wait("sync", "sDV", bc_free[n - 2])
                        bc_tok[n] = (
                            ld(bcb[:, 2 * sl * T:(2 * sl + 1) * T],
                               bc_dr[n:n + 1, :].broadcast_to([128, T])),
                            ld(bcb[:, (2 * sl + 1) * T:(2 * sl + 2) * T],
                               bc_dr[NST + n:NST + n + 1, :]
                               .broadcast_to([128, T])))
                        for j in range(HHT):
                            ti = hf * HHT + j
                            it = n * HHT + j
                            da = sG[:, (it % 2) * T:(it % 2 + 1) * T]
                            dbx = sG[:, 2 * T:3 * T]
                            hh = sG[:, 3 * T:4 * T]
                            yp = sG[:, (4 + it % 2) * T:(5 + it % 2) * T]
                            Bb = bcb[:, 2 * sl * T:(2 * sl + 1) * T]
                            Cb = bcb[:, (2 * sl + 1) * T:(2 * sl + 2) * T]
                            s = slice(j * T, (j + 1) * T)
                            if it >= 2:
                                p.wait("scalar", "sDV", scan_done[it - 2])
                            ta = ac(lambda e, s=s, da=da, ti=ti, n=n:
                                    e.activation(
                                        out=da, in_=sE[:, s], func=AF.Exp,
                                        scale=A_sb[:, ti * NST + n:
                                                   ti * NST + n + 1]))
                            if j == 0:
                                p.wait("vector", bc_tok[n][0])
                                p.wait("vector", bc_tok[n][1])
                            dv(lambda e, s=s, dbx=dbx, Bb=Bb: e.tensor_tensor(
                                out=dbx, in0=sF[:, s], in1=Bb, op=ALU.mult))
                            p.wait("vector", "sAC", ta)
                            sd = dv(lambda e, da=da, dbx=dbx, hh=hh, ti=ti, n=n:
                                    e.tensor_tensor_scan(
                                        out=hh, data0=da, data1=dbx,
                                        initial=hcar_sb[:, ti * NST + n:
                                                        ti * NST + n + 1],
                                        op0=ALU.mult, op1=ALU.add))
                            scan_done[it] = sd
                            dv(lambda e, hh=hh, ti=ti, n=n: e.tensor_copy(
                                out=hcar_sb[:, ti * NST + n:ti * NST + n + 1],
                                in_=hh[:, T - 1:T]))
                            if it >= 2:
                                p.wait("vector", "sPE", yp_pe[it - 2])
                            tyd = dv(lambda e, hh=hh, yp=yp, Cb=Cb:
                                     e.tensor_tensor(out=yp, in0=hh, in1=Cb,
                                                     op=ALU.mult))
                            yp_dv[it] = tyd
                            p.wait("tensor", "sDV", tyd)
                            yp_pe[it] = pem(lambda e, yp=yp, j=j, n=n: e.matmul(
                                out=ps[j][:, 0:T], lhsT=id_sb[:], rhs=yp,
                                start=(n == 0), stop=(n == NST - 1)))
                            if j == HHT - 1:
                                bc_free[n] = tyd
                                if n >= NST - 2:
                                    bc_last_toks.extend(bc_tok[n])
                    p.wait("scalar", "sPE")
                    for j in range(HHT):
                        ac(lambda e, j=j: e.copy(
                            out=sA[:, j * T:(j + 1) * T], in_=ps[j][:, 0:T]))
                    tyev = ac(lambda e: e.nop())
                    p.wait("vector", "sAC", tyev)
                    p.wait("tensor", "sAC", tyev)

                    if DBG:
                        p.wait("sync", "sDV")
                        st(dbg_y.rearrange("(n p) t -> p n t", p=128)
                           [:, hf * HHT:(hf + 1) * HHT, tok],
                           sA.rearrange("p (n t) -> p n t", n=HHT))
                        p.wait("vector", "sST")
                    for j in range(HHT):
                        n = hf * HHT + j
                        s = slice(j * T, (j + 1) * T)
                        ac(lambda e, s=s, n=n: e.activation(
                            out=sD[:, s], in_=bZ[:, n * T:(n + 1) * T],
                            func=AF.Silu))
                    ta = acm(lambda e: e.nop())
                    p.wait("vector", "sAC", ta)
                    for j in range(HHT):
                        n = hf * HHT + j
                        s = slice(j * T, (j + 1) * T)
                        dv(lambda e, s=s, n=n: e.scalar_tensor_tensor(
                            out=sA[:, s], in0=sB[:, n * T:(n + 1) * T],
                            scalar=D_sb[:, n:n + 1], in1=sA[:, s],
                            op0=ALU.mult, op1=ALU.add))
                        dv(lambda e, s=s, n=n: e.tensor_tensor(
                            out=bC[:, n * T:(n + 1) * T], in0=sA[:, s],
                            in1=sD[:, s], op=ALU.mult))
                    td = dvm(lambda e: e.nop())
                    edges["td_gate"] = td
                    p.wait("tensor", "sDV", td)
                    p.wait("scalar", "sDV", td)

                for mb in range(4):
                    tl = ld(wwin[:], wbf_out[mb * 128:(mb + 1) * 128, :])
                    p.wait("tensor", "sLD", tl)
                    for m in range(2):
                        for k in range(NT):
                            f = (lambda e, m=m, k=k: e.matmul(
                                out=ps[m][:, 0:T],
                                lhsT=wwin[:, k * 256 + m * 128:
                                          k * 256 + (m + 1) * 128],
                                rhs=bC[:, k * T:(k + 1) * T],
                                start=(k == 0), stop=(k == NT - 1)))
                            tp = pem(f) if k == NT - 1 else pe(f)
                    p.wait("scalar", "sPE", tp)
                    for m in range(2):
                        gi = mb * 2 + m
                        ac(lambda e, m=m, gi=gi: e.copy(
                            out=sE[:, gi * T:(gi + 1) * T], in_=ps[m][:, 0:T]))
                    ta = acm(lambda e: e.nop())
                    p.wait("tensor", "sAC", ta)
                    p.wait("sync", "sPE", tp)
                edges["tp_wout"] = tp
                p.wait("vector", "sAC")

                if DBG:
                    p.wait("sync", "sAC")
                    st(dbg_mo.rearrange("(k p) t -> p k t", p=128)[:, :, tok],
                       sE.rearrange("p (k t) -> p k t", k=KT))
                    p.wait("vector", "sST")
                    p.wait("scalar", "sST")
                tl = ld(sA.rearrange("p (k t) -> p k t", k=KT),
                        x_in.rearrange("(k p) t -> p k t", p=128)[:, :, tok])
                p.wait("scalar", "sLD", tl)
                p.wait("vector", "sLD", tl)
                tln = layer_norm2(sE, bA, bX, bQ)
                p.wait("vector", "sDV", tln)
                for k in range(KT):
                    s = slice(k * T, (k + 1) * T)
                    dv(lambda e, s=s: e.tensor_tensor(
                        out=sE[:, s], in0=sE[:, s], in1=ps[7][:, 0:T],
                        op=ALU.mult))
                    dv(lambda e, s=s: e.tensor_add(
                        out=sA[:, s], in0=sA[:, s], in1=sE[:, s]))
                td = dvm(lambda e: e.nop())
                edges["tln_mo"] = td
                p.wait("sync", "sDV", td)
                p.wait("scalar", "sDV", td)
                tsx = st(x2_dr.rearrange("(k p) t -> p k t", p=128)[:, :, tok],
                         sA.rearrange("p (k t) -> p k t", k=KT))
                edges["x2_store"] = tsx
                p.wait("vector", "sST", tsx)
                tln = layer_norm2(sA, bA, bX, bQ)
                edges["sA_free"] = tln
                p.wait("sync", "sDV", tln)
                ts = st(ln2_dr.rearrange("(k p) t -> p k t", p=128)[:, :, tok],
                        bA.rearrange("p (k t) -> p k t", k=KT))
                edges["ln2_store"] = ts

        p.barrier()
        with ExitStack() as fctx:
            fsb = lambda n, s, d=F32: fctx.enter_context(nc.sbuf_tensor(n, s, d))
            fa = fsb("fa", [128, HT * T], BF16)
            fb = fsb("fb", [128, HT * T], BF16)
            fx2 = fsb("fx2", [128, KT * T])
            ffv = fsb("ffv", [128, KT * T])
            flb = fsb("flb", [128, KT * T], BF16)
            fxb = fsb("fxb", [128, 2 * T], BF16)
            fqb = fsb("fqb", [128, 2 * T], BF16)
            fss = fsb("fss", [128, T])
            scal_sb = fsb("scal_sb", [128, 4 * HT + KT])
            ld(scal_sb[:, 0:HT], s1_in)
            ld(scal_sb[:, HT:2 * HT], s2_in)
            ld(scal_sb[:, 2 * HT:3 * HT], sh1_in)
            ld(scal_sb[:, 3 * HT:4 * HT], sh2_in)
            tl = ld(scal_sb[:, 4 * HT:4 * HT + KT], s3_in)
            p.wait("scalar", "sLD", tl)
            p.wait("vector", "sLD", tl)

            for j in range(QT // T):
                p.barrier()
                def dyn_ld(dst, src_r, jj=j):
                    def f(e, dst=dst, src_r=src_r, jj=jj):
                        base = e.value_load(qidx_sb[0:1, 0:1])
                        return e.dma_start(out=dst,
                                           in_=src_r[:, :, bass.ts(base + jj, T)])
                    return _dma("sLD", f)

                t1 = dyn_ld(flb.rearrange("p (k t) -> p k t", k=KT),
                            ln2_dr.rearrange("(k p) t -> p k t", p=128))
                t2 = dyn_ld(fx2.rearrange("p (k t) -> p k t", k=KT),
                            x2_dr.rearrange("(k p) t -> p k t", p=128))
                for tok in (t1, t2):
                    p.wait("tensor", "sLD", tok)
                    p.wait("vector", "sLD", tok)
                    p.wait("scalar", "sLD", tok)

                for m in range(HT):
                    t1 = ld(wwin[:, 0:KT * 128],
                            wbf_1[m * 128:(m + 1) * 128, :])
                    t2 = ld(wwin[:, 2048:2048 + KT * 128],
                            wbf_2[m * 128:(m + 1) * 128, :])
                    p.wait("tensor", "sLD", t1)
                    p.wait("tensor", "sLD", t2)
                    for k in range(KT):
                        f = (lambda e, k=k: e.matmul(
                            out=ps[0][:, 0:T],
                            lhsT=wwin[:, k * 128:(k + 1) * 128],
                            rhs=flb[:, k * T:(k + 1) * T],
                            start=(k == 0), stop=(k == KT - 1)))
                        tp = pem(f) if k == KT - 1 else pe(f)
                    for k in range(KT):
                        f = (lambda e, k=k: e.matmul(
                            out=ps[1][:, 0:T],
                            lhsT=wwin[:, 2048 + k * 128:2048 + (k + 1) * 128],
                            rhs=flb[:, k * T:(k + 1) * T],
                            start=(k == 0), stop=(k == KT - 1)))
                        tp = pem(f) if k == KT - 1 else pe(f)
                    p.wait("scalar", "sPE", tp)
                    ta = acm(lambda e, m=m: e.activation(
                        out=fss[:], in_=ps[0][:, 0:T], func=AF.Silu,
                        scale=scal_sb[:, m:m + 1]))
                    p.wait("vector", "sAC", ta)
                    td = dvm(lambda e, m=m: e.scalar_tensor_tensor(
                        out=fa[:, m * T:(m + 1) * T], in0=ps[1][:, 0:T],
                        scalar=scal_sb[:, HT + m:HT + m + 1], in1=fss[:],
                        op0=ALU.mult, op1=ALU.mult))
                    p.wait("tensor", "sDV", td)
                    p.wait("sync", "sPE", tp)

                def hlayer(wsrc, scol, src, dst):
                    for m in range(HT):
                        toks = [ld(wwin[:, kb * 2048:(kb + 1) * 2048],
                                   wsrc[m * 128:(m + 1) * 128,
                                        kb * 2048:(kb + 1) * 2048])
                                for kb in range(2)]
                        for tok in toks:
                            p.wait("tensor", "sLD", tok)
                        for k in range(HT):
                            f = (lambda e, k=k: e.matmul(
                                out=ps[2][:, 0:T],
                                lhsT=wwin[:, k * 128:(k + 1) * 128],
                                rhs=src[:, k * T:(k + 1) * T],
                                start=(k == 0), stop=(k == HT - 1)))
                            tp = pem(f) if k == HT - 1 else pe(f)
                        p.wait("scalar", "sPE", tp)
                        ta = acm(lambda e, m=m: e.activation(
                            out=dst[:, m * T:(m + 1) * T], in_=ps[2][:, 0:T],
                            func=AF.Gelu,
                            scale=scal_sb[:, scol + m:scol + m + 1]))
                        p.wait("tensor", "sAC", ta)
                        p.wait("sync", "sPE", tp)

                hlayer(wbf_h1, 2 * HT, fa, fb)
                hlayer(wbf_h2, 3 * HT, fb, fa)

                for m in range(KT):
                    toks = [ld(wwin[:, kb * 2048:(kb + 1) * 2048],
                               wbf_3[m * 128:(m + 1) * 128,
                                     kb * 2048:(kb + 1) * 2048])
                            for kb in range(2)]
                    for tok in toks:
                        p.wait("tensor", "sLD", tok)
                    for k in range(HT):
                        f = (lambda e, k=k: e.matmul(
                            out=ps[3][:, 0:T],
                            lhsT=wwin[:, k * 128:(k + 1) * 128],
                            rhs=fa[:, k * T:(k + 1) * T],
                            start=(k == 0), stop=(k == HT - 1)))
                        tp = pem(f) if k == HT - 1 else pe(f)
                    p.wait("scalar", "sPE", tp)
                    ta = acm(lambda e, m=m: e.activation(
                        out=ffv[:, m * T:(m + 1) * T], in_=ps[3][:, 0:T],
                        func=AF.Copy,
                        scale=scal_sb[:, 4 * HT + m:4 * HT + m + 1]))
                    p.wait("tensor", "sAC", ta)
                    p.wait("sync", "sPE", tp)
                p.wait("vector", "sAC")

                tln = layer_norm2(ffv, flb, fxb, fqb)
                p.wait("vector", "sDV", tln)
                for k in range(KT):
                    s = slice(k * T, (k + 1) * T)
                    dv(lambda e, s=s: e.tensor_tensor(
                        out=ffv[:, s], in0=ffv[:, s], in1=ps[7][:, 0:T],
                        op=ALU.mult))
                    dv(lambda e, s=s: e.tensor_add(
                        out=fx2[:, s], in0=fx2[:, s], in1=ffv[:, s]))
                td = dvm(lambda e: e.nop())
                p.wait("sync", "sDV", td)
                st(y_out.rearrange("(k p) t -> p k t", p=128)
                   [:, :, j * T:(j + 1) * T],
                   fx2.rearrange("p (k t) -> p k t", k=KT))

        p.wait("sync", "sST")

        @block.sync
        def _(sync):
            for fn in p.ops["sync"]:
                fn(sync)

        @block.tensor
        def _(tensor):
            for fn in p.ops["tensor"]:
                fn(tensor)

        @block.scalar
        def _(scalar):
            for fn in p.ops["scalar"]:
                fn(scalar)

        @block.vector
        def _(vector):
            for fn in p.ops["vector"]:
                fn(vector)

    return nc


def _pack_wdt(w):
    out = np.empty((128, DI // 2), np.float32)
    for n in range(NT):
        out[DTR * (n % 2):DTR * (n % 2) + DTR,
            (n // 2) * 128:(n // 2) * 128 + 128] = w[:, n * 128:(n + 1) * 128]
    return np.ascontiguousarray(out)


def _prep_inputs(kw):
    f32 = np.float32
    cac = lambda a: np.ascontiguousarray(np.asarray(a, f32))

    def tile_vec(v, ntiles):
        return cac(np.asarray(v, f32).reshape(ntiles, 128).T)

    x = np.asarray(kw["L0_attn_x"], f32)
    x_t = np.ascontiguousarray(np.transpose(x, (0, 2, 1)))

    A_log = np.asarray(kw["L0_attn_A_log"], f32)
    alog_t = cac(A_log.reshape(NT, 128, NST).transpose(1, 0, 2)
                 .reshape(128, NT * NST))
    wconv = np.asarray(kw["L0_attn_W_conv"], f32).reshape(DI, 4)
    wconv_t = cac(wconv.reshape(NT, 128, 4).transpose(1, 0, 2)
                  .reshape(128, NT * 4))
    sel = np.zeros((NST, NST * 128), ml_dtypes.bfloat16)
    for n in range(NST):
        sel[n, n * 128:(n + 1) * 128] = 1
    common = dict(
        w_in=cac(kw["L0_attn_W_in"]),
        wconv_t=wconv_t,
        bconv_t=tile_vec(kw["L0_attn_b_conv"], NT),
        w_xproj=cac(kw["L0_attn_W_xproj"]),
        w_dt=cac(kw["L0_attn_W_dt"]),
        bdt_t=tile_vec(kw["L0_attn_b_dt"], NT),
        alog_t=alog_t,
        d_t=tile_vec(kw["L0_attn_D"], NT),
        w_out=cac(kw["L0_attn_W_out"]),
        w1t=cac(np.asarray(kw["L0_ffn_w1_w"], f32).T),
        w2t=cac(np.asarray(kw["L0_ffn_w2_w"], f32).T),
        h1t=cac(np.asarray(kw["L0_ffn_h1_w"], f32).T),
        h2t=cac(np.asarray(kw["L0_ffn_h2_w"], f32).T),
        w3t=cac(np.asarray(kw["L0_ffn_w3_w"], f32).T),
        s1_t=tile_vec(kw["L0_ffn_w1_s"], HT),
        s2_t=tile_vec(kw["L0_ffn_w2_s"], HT),
        sh1_t=tile_vec(kw["L0_ffn_h1_s"], HT),
        sh2_t=tile_vec(kw["L0_ffn_h2_s"], HT),
        s3_t=tile_vec(kw["L0_ffn_w3_s"], KT),
        sel_bf=sel,
        ones_bf=np.ones((128, 1), ml_dtypes.bfloat16),
        id_bf=np.eye(128, dtype=ml_dtypes.bfloat16),
        one1_bf=np.ones((1, 128), ml_dtypes.bfloat16),
        cst=np.tile(np.array([[EPS, 1.0, -1.0, 0.0]], np.float32), (128, 1)),
    )
    in_maps = []
    for c in range(8):
        s, q = c // 4, c % 4
        m = dict(common)
        m["x_seq"] = x_t[s]
        m["qidx"] = np.full((1, 1), q * (QT // T), np.int32)
        in_maps.append(m)
    return in_maps


def kernel(**inputs):
    in_maps = _prep_inputs(inputs)
    nc = build_program()
    res = run_bass_kernel_spmd(nc, in_maps, core_ids=list(range(8)))
    out = np.empty((B, L, DM), np.float32)
    for c in range(8):
        s, q = c // 4, c % 4
        out[s, q * QT:(q + 1) * QT, :] = res.results[c]["y_out"].T
    return out


# version 05
# speedup vs baseline: 1.1805x; 1.0053x over previous
import numpy as np
import ml_dtypes
import concourse.bass as bass
import concourse.mybir as mybir
from concourse.bass_utils import run_bass_kernel_spmd
from contextlib import ExitStack

F32 = mybir.dt.float32
BF16 = mybir.dt.bfloat16
I32 = mybir.dt.int32
AF = mybir.ActivationFunctionType
ALU = mybir.AluOpType

DM, DI, DH, NST, DTR = 1024, 2048, 4096, 16, 64
DBG = False
L, B = 8192, 2
T = 512
NCH = L // T
QT = 2048
EPS = 1e-5
KT = DM // 128
NT = DI // 128
HT = DH // 128
HHT = NT // 2

ENGS = ("sync", "tensor", "scalar", "vector")


class Prog:
    def __init__(self, nc):
        self.nc = nc
        self.ops = {e: [] for e in ENGS}
        self.cnt = {}
        self.sems = {}
        self.last_wait = {e: {} for e in ENGS}

    def sem(self, ctx, name):
        h = ctx.enter_context(self.nc.semaphore(name))
        self.cnt[name] = 0
        self.sems[name] = h
        return h

    def on(self, eng, fn):
        self.ops[eng].append(fn)

    def wait(self, eng, name, val=None):
        if isinstance(name, tuple):
            name, val = name
        elif isinstance(val, tuple):
            name, val = val
        if name in ("sLD", "sST") and val is None:
            for s in self.pools[name]:
                self.wait(eng, s, self.cnt[s])
            return
        if val is None:
            val = self.cnt[name]
        if val <= 0 or self.last_wait[eng].get(name, -1) >= val:
            return
        self.last_wait[eng][name] = val
        h = self.sems[name]
        self.on(eng, lambda e, h=h, v=val: e.wait_ge(h, v))

    def mark(self, eng, name, fn, amt):
        self.cnt[name] += amt
        h = self.sems[name]
        self.on(eng, lambda e, h=h, a=amt, f=fn: f(e).then_inc(h, a))
        return self.cnt[name]

    def emit(self, eng, fn):
        self.on(eng, fn)

    def barrier(self):
        for e in ENGS:
            for s in ("sPE", "sAC", "sDV"):
                self.wait(e, s)
            self.wait(e, "sLD")
            self.wait(e, "sST")


def build_program():
    nc = bass.Bass("TRN2", target_bir_lowering=False)
    dram = nc.dram_tensor

    x_in = dram("x_seq", [DM, L], F32, kind="ExternalInput").ap()
    qidx_in = dram("qidx", [1, 1], I32, kind="ExternalInput").ap()
    w_in = dram("w_in", [DM, 2 * DI], F32, kind="ExternalInput").ap()
    wconv_in = dram("wconv_t", [128, NT * 4], F32, kind="ExternalInput").ap()
    bconv_in = dram("bconv_t", [128, NT], F32, kind="ExternalInput").ap()
    wxp_in = dram("w_xproj", [DI, 96], F32, kind="ExternalInput").ap()
    wdt_in = dram("w_dt", [DTR, DI], F32, kind="ExternalInput").ap()
    bdt_in = dram("bdt_t", [128, NT], F32, kind="ExternalInput").ap()
    alog_in = dram("alog_t", [128, NT * NST], F32, kind="ExternalInput").ap()
    d_in = dram("d_t", [128, NT], F32, kind="ExternalInput").ap()
    w_out_in = dram("w_out", [DI, DM], F32, kind="ExternalInput").ap()
    w1_in = dram("w1t", [DM, DH], F32, kind="ExternalInput").ap()
    w2_in = dram("w2t", [DM, DH], F32, kind="ExternalInput").ap()
    h1_in = dram("h1t", [DH, DH], F32, kind="ExternalInput").ap()
    h2_in = dram("h2t", [DH, DH], F32, kind="ExternalInput").ap()
    w3_in = dram("w3t", [DH, DM], F32, kind="ExternalInput").ap()
    s1_in = dram("s1_t", [128, HT], F32, kind="ExternalInput").ap()
    s2_in = dram("s2_t", [128, HT], F32, kind="ExternalInput").ap()
    sh1_in = dram("sh1_t", [128, HT], F32, kind="ExternalInput").ap()
    sh2_in = dram("sh2_t", [128, HT], F32, kind="ExternalInput").ap()
    s3_in = dram("s3_t", [128, KT], F32, kind="ExternalInput").ap()
    sel_in = dram("sel_bf", [NST, NST * 128], BF16, kind="ExternalInput").ap()
    ones_in = dram("ones_bf", [128, 1], BF16, kind="ExternalInput").ap()
    one1_in = dram("one1_bf", [1, 128], BF16, kind="ExternalInput").ap()
    id_in = dram("id_bf", [128, 128], BF16, kind="ExternalInput").ap()
    cst_in = dram("cst", [128, 4], F32, kind="ExternalInput").ap()

    y_out = dram("y_out", [DM, QT], F32, kind="ExternalOutput").ap()

    wbf_in = dram("wbf_in", [8 * 128, KT * 512], BF16, kind="Internal").ap()
    wbf_out = dram("wbf_out", [4 * 128, NT * 256], BF16, kind="Internal").ap()
    wbf_1 = dram("wbf_1", [HT * 128, KT * 128], BF16, kind="Internal").ap()
    wbf_2 = dram("wbf_2", [HT * 128, KT * 128], BF16, kind="Internal").ap()
    wbf_h1 = dram("wbf_h1", [HT * 128, HT * 128], BF16, kind="Internal").ap()
    wbf_h2 = dram("wbf_h2", [HT * 128, HT * 128], BF16, kind="Internal").ap()
    wbf_3 = dram("wbf_3", [KT * 128, HT * 128], BF16, kind="Internal").ap()
    bc_dr = dram("bc_dr", [2 * NST, T], BF16, kind="Internal").ap()
    x2_dr = dram("x2_dr", [DM, L], F32, kind="Internal").ap()
    ln2_dr = dram("ln2_dr", [DM, L], BF16, kind="Internal").ap()
    if DBG:
        dbg_xc = dram("dbg_xc", [DI, L], F32, kind="Internal").ap()
        dbg_dl = dram("dbg_dl", [DI, L], F32, kind="Internal").ap()
        dbg_y = dram("dbg_y", [DI, L], F32, kind="Internal").ap()
        dbg_mo = dram("dbg_mo", [DM, L], F32, kind="Internal").ap()
        dbg_ln1 = dram("dbg_ln1", [DM, L], BF16, kind="Internal").ap()
        dbg_xi = dram("dbg_xi", [DI, L], BF16, kind="Internal").ap()
        dbg_z = dram("dbg_z", [DI, L], BF16, kind="Internal").ap()

    with ExitStack() as ctx:
        p = Prog(nc)
        for s in ("sPE", "sAC", "sDV"):
            p.sem(ctx, s)
        p.pools = {"sLD": [], "sST": []}
        for i in range(16):
            n = f"sL{i}"
            p.sem(ctx, n)
            p.pools["sLD"].append(n)
        for i in range(8):
            n = f"sS{i}"
            p.sem(ctx, n)
            p.pools["sST"].append(n)
        p.rot = {"sLD": 0, "sST": 0}
        sb = lambda n, s, d=F32: ctx.enter_context(nc.sbuf_tensor(n, s, d))

        A_sb = sb("A_sb", [128, NT * NST])
        D_sb = sb("D_sb", [128, NT])
        bconv_sb = sb("bconv_sb", [128, NT])
        bdt_sb = sb("bdt_sb", [128, NT])
        wconv_sb = sb("wconv_sb", [128, NT * 4])
        wxp_bf = sb("wxp_bf", [128, NT * 96], BF16)
        wdt_bf = sb("wdt_bf", [DTR, DI], BF16)
        sel_sb = sb("sel_sb", [NST, NST * 128], BF16)
        ones_sb = sb("ones_sb", [128, 1], BF16)
        one1_sb = sb("one1_sb", [1, 128], BF16)
        cst_sb = sb("cst_sb", [128, 4])
        mr_bf = sb("mr_bf", [1, 2 * T], BF16)
        hist_bf = sb("hist_bf", [128, NT * 3], BF16)
        hcar_sb = sb("hcar_sb", [128, NT * NST])
        qidx_sb = ctx.enter_context(nc.sbuf_tensor("qidx_sb", [1, 1], I32))
        dlr_bf = sb("dlr_bf", [DTR, T], BF16)
        b_bf = sb("b_bf", [NST, T], BF16)
        c_bf = sb("c_bf", [NST, T], BF16)
        wwin = sb("wwin", [128, 4096], BF16)
        id_sb = sb("id_sb", [128, 128], BF16)

        ps = [ctx.enter_context(nc.psum_tensor(f"ps{i}", [128, 512], F32))
              for i in range(8)]

        block = ctx.enter_context(nc.Block())

        def _dma(pool, fn):
            i = p.rot[pool]
            p.rot[pool] = (i + 1) % len(p.pools[pool])
            sname = p.pools[pool][i]
            p.wait("sync", sname, p.cnt[sname])
            v = p.mark("sync", sname, fn, 16)
            return (sname, v)

        def ld(dst, src):
            return _dma("sLD",
                        lambda e, d=dst, r=src: e.dma_start(out=d, in_=r))

        def st(dst, src):
            return _dma("sST",
                        lambda e, d=dst, r=src: e.dma_start(out=d, in_=r))

        def pe(fn):
            p.emit("tensor", fn)

        def pem(fn):
            return p.mark("tensor", "sPE", fn, 1)

        def ac(fn):
            p.wait("scalar", "sAC", p.cnt["sAC"])
            return p.mark("scalar", "sAC", fn, 1)

        acm = ac

        def dv(fn):
            p.wait("vector", "sDV", p.cnt["sDV"])
            return p.mark("vector", "sDV", fn, 1)

        dvm = dv

        with nc.sbuf_tensor("cast_f", [128, 4096], F32) as cast_f, \
             nc.sbuf_tensor("cast_b", [128, 4096], BF16) as cast_b:

            ld(D_sb[:], d_in)
            ld(bconv_sb[:], bconv_in)
            ld(bdt_sb[:], bdt_in)
            ld(wconv_sb[:], wconv_in)
            ld(sel_sb[:], sel_in)
            ld(ones_sb[:], ones_in)
            ld(one1_sb[:], one1_in)
            ld(id_sb[:], id_in)
            ld(qidx_sb[:], qidx_in)
            ld(cst_sb[:], cst_in)

            ld(cast_f[:, 0:NT * NST], alog_in)
            p.wait("scalar", "sLD")
            acm(lambda e: e.activation(out=A_sb[:], in_=cast_f[:, 0:NT * NST],
                                       func=AF.Exp))
            p.wait("sync", "sAC")
            p.wait("vector", "sAC")
            dvm(lambda e: e.tensor_scalar_mul(out=A_sb[:], in0=A_sb[:],
                                              scalar1=-1.0))

            ld(cast_f[:, 0:NT * 96],
               wxp_in.rearrange("(n p) c -> p n c", p=128))
            p.wait("scalar", "sLD")
            acm(lambda e: e.copy(out=wxp_bf[:], in_=cast_f[:, 0:NT * 96]))
            p.wait("sync", "sAC")
            ld(cast_f[0:DTR, 0:DI], wdt_in)
            p.wait("scalar", "sLD")
            acm(lambda e: e.copy(out=wdt_bf[:], in_=cast_f[0:DTR, 0:DI]))
            p.wait("sync", "sAC")

            def cast_tiled(src, dst, krows, mcols, slab_m, mc):
                for k in range(krows):
                    p.wait("sync", "sAC")
                    tl = ld(cast_f[:, 0:mcols * 128],
                            src[k * 128:(k + 1) * 128, :])
                    p.wait("scalar", "sLD", tl)
                    tc = acm(lambda e: e.copy(out=cast_b[:, 0:mcols * 128],
                                              in_=cast_f[:, 0:mcols * 128]))
                    p.wait("sync", "sAC", tc)
                    tsl = 0
                    for mb in range(mcols // slab_m):
                        w = slab_m * 128
                        tsl = st(dst[mb * 128:(mb + 1) * 128,
                                     k * w:(k + 1) * w],
                                 cast_b[:, mb * w:(mb + 1) * w])
                    p.wait("scalar", "sST")

            cast_tiled(w_in, wbf_in, KT, 2 * NT, 4, 512)
            cast_tiled(w_out_in, wbf_out, NT, KT, 2, 256)
            cast_tiled(w1_in, wbf_1, KT, HT, 1, 128)
            cast_tiled(w2_in, wbf_2, KT, HT, 1, 128)
            cast_tiled(h1_in, wbf_h1, HT, HT, 1, 128)
            cast_tiled(h2_in, wbf_h2, HT, HT, 1, 128)
            cast_tiled(w3_in, wbf_3, HT, KT, 1, 128)
            p.barrier()

        dvm(lambda e: e.memset(hist_bf[:], 0.0))
        dvm(lambda e: e.memset(hcar_sb[:], 0.0))
        p.barrier()

        def layer_norm2(v, out_bf, xb, qb):
            pe_done = []
            if getattr(p, "_ln_last_pe", 0):
                p.wait("scalar", "sPE", p._ln_last_pe)
            for k in range(KT):
                s = slice(k * T, (k + 1) * T)
                sl = slice((k % 2) * T, (k % 2 + 1) * T)
                if k >= 2:
                    p.wait("scalar", "sPE", pe_done[k - 2])
                ac(lambda e, s=s, sl=sl: e.copy(out=xb[:, sl], in_=v[:, s]))
                ta = acm(lambda e, s=s, sl=sl: e.activation(
                    out=qb[:, sl], in_=v[:, s], func=AF.Square))
                p.wait("tensor", "sAC", ta)
                if k < KT - 1:
                    pe(lambda e, sl=sl, k=k: e.matmul(
                        out=ps[6][0:1, 0:T], lhsT=ones_sb[:], rhs=xb[:, sl],
                        start=(k == 0), stop=False))
                else:
                    pem(lambda e, sl=sl, k=k: e.matmul(
                        out=ps[6][0:1, 0:T], lhsT=ones_sb[:], rhs=xb[:, sl],
                        start=(k == 0), stop=True))
                pe_done.append(pem(lambda e, sl=sl, k=k: e.matmul(
                    out=ps[7][0:1, 0:T], lhsT=ones_sb[:], rhs=qb[:, sl],
                    start=(k == 0), stop=(k == KT - 1))))
            p._ln_last_pe = pe_done[-1]
            p.wait("vector", "sPE")
            dv(lambda e: e.tensor_scalar_mul(out=ps[6][0:1, 0:T],
                                             in0=ps[6][0:1, 0:T],
                                             scalar1=1.0 / DM))
            td = dvm(lambda e: e.tensor_scalar_mul(out=ps[7][0:1, 0:T],
                                                   in0=ps[7][0:1, 0:T],
                                                   scalar1=1.0 / DM))
            p.wait("scalar", "sDV", td)
            ac(lambda e: e.copy(out=mr_bf[0:1, 0:T], in_=ps[6][0:1, 0:T]))
            ta = acm(lambda e: e.activation(out=mr_bf[0:1, T:2 * T],
                                            in_=ps[6][0:1, 0:T],
                                            func=AF.Square))
            p.wait("vector", "sAC", ta)
            td = dvm(lambda e: e.tensor_sub(out=ps[7][0:1, 0:T],
                                            in0=ps[7][0:1, 0:T],
                                            in1=mr_bf[0:1, T:2 * T]))
            p.wait("scalar", "sDV", td)
            ta = acm(lambda e: e.activation(out=ps[7][0:1, 0:T],
                                            in_=ps[7][0:1, 0:T],
                                            func=AF.Sqrt,
                                            bias=cst_sb[0:1, 0:1]))
            p.wait("vector", "sAC", ta)
            td = dvm(lambda e: e.reciprocal(out=ps[7][0:1, 0:T],
                                            in_=ps[7][0:1, 0:T]))
            p.wait("scalar", "sDV", td)
            ta = acm(lambda e: e.copy(out=mr_bf[0:1, T:2 * T],
                                      in_=ps[7][0:1, 0:T]))
            p.wait("tensor", "sAC", ta)
            pem(lambda e: e.matmul(out=ps[6][:, 0:T], lhsT=one1_sb[:],
                                   rhs=mr_bf[0:1, 0:T], start=True, stop=True))
            tp = pem(lambda e: e.matmul(out=ps[7][:, 0:T], lhsT=one1_sb[:],
                                        rhs=mr_bf[0:1, T:2 * T],
                                        start=True, stop=True))
            p.wait("vector", "sPE", tp)
            for k in range(KT):
                s = slice(k * T, (k + 1) * T)
                dv(lambda e, s=s: e.scalar_tensor_tensor(
                    out=v[:, s], in0=ps[6][:, 0:T], scalar=-1.0, in1=v[:, s],
                    op0=ALU.mult, op1=ALU.add))
                dv(lambda e, s=s: e.tensor_tensor(
                    out=out_bf[:, s], in0=v[:, s], in1=ps[7][:, 0:T],
                    op=ALU.mult))
            return dvm(lambda e: e.nop())

        with ExitStack() as mctx:
            msb = lambda n, s, d=F32: mctx.enter_context(nc.sbuf_tensor(n, s, d))
            sA = msb("sA", [128, KT * T])
            sB = msb("sB", [128, NT * T])
            sC = msb("sC", [128, HHT * T])
            sD = msb("sD", [128, HHT * T])
            sE = msb("sE", [128, HHT * T])
            sF = msb("sF", [128, HHT * T], BF16)
            sG = msb("sG", [128, 4 * T], BF16)
            sGf = msb("sGf", [128, 2 * T])
            bcb = msb("bcb", [128, 4 * T], BF16)
            bA = msb("bA", [128, KT * T], BF16)
            bX = msb("bX", [128, 2 * T], BF16)
            bQ = msb("bQ", [128, 2 * T], BF16)
            bI = msb("bI", [128, NT * T], BF16)
            bZ = msb("bZ", [128, NT * T], BF16)
            bC = msb("bC", [128, NT * T], BF16)

            bc_last_toks = []
            edges = {}
            for c in range(NCH):
                tok = slice(c * T, (c + 1) * T)
                if c > 0:
                    bc_last_toks = bc_last_toks[-4:]

                if "sA_free" in edges:
                    p.wait("sync", "sDV", edges["sA_free"])
                    p.wait("sync", edges["x2_store"])
                    p.wait("sync", edges["ln2_store"])
                tl = ld(sA.rearrange("p (k t) -> p k t", k=KT),
                        x_in.rearrange("(k p) t -> p k t", p=128)[:, :, tok])
                p.wait("scalar", "sLD", tl)
                p.wait("vector", "sLD", tl)
                if "tp_win" in edges:
                    p.wait("vector", "sPE", edges["tp_win"])
                    p.wait("scalar", "sPE", edges["tp_win"])
                tln = layer_norm2(sA, bA, bX, bQ)
                p.wait("tensor", "sDV", tln)
                if DBG:
                    p.wait("sync", "sDV", tln)
                    st(dbg_ln1.rearrange("(k p) t -> p k t", p=128)[:, :, tok],
                       bA.rearrange("p (k t) -> p k t", k=KT))

                if "td_conv" in edges:
                    p.wait("scalar", "sDV", edges["td_conv"])
                for mb in range(8):
                    tl = ld(wwin[:], wbf_in[mb * 128:(mb + 1) * 128, :])
                    p.wait("tensor", "sLD", tl)
                    for m in range(4):
                        for k in range(KT):
                            f = (lambda e, m=m, k=k: e.matmul(
                                out=ps[m][:, 0:T],
                                lhsT=wwin[:, k * 512 + m * 128:
                                          k * 512 + (m + 1) * 128],
                                rhs=bA[:, k * T:(k + 1) * T],
                                start=(k == 0), stop=(k == KT - 1)))
                            tp = pem(f) if k == KT - 1 else pe(f)
                    p.wait("scalar", "sPE", tp)
                    for m in range(4):
                        gi = mb * 4 + m
                        if gi < NT:
                            dst = bI[:, gi * T:(gi + 1) * T]
                        else:
                            dst = bZ[:, (gi - NT) * T:(gi - NT + 1) * T]
                        ac(lambda e, m=m, d=dst: e.copy(out=d,
                                                        in_=ps[m][:, 0:T]))
                    ta = acm(lambda e: e.nop())
                    p.wait("tensor", "sAC", ta)
                    p.wait("sync", "sPE", tp)
                edges["tp_win"] = tp
                p.wait("vector", "sAC")
                if DBG:
                    p.wait("sync", "sAC")
                    st(dbg_xi.rearrange("(n p) t -> p n t", p=128)[:, :, tok],
                       bI.rearrange("p (n t) -> p n t", n=NT)[:, :, 0:T])
                    st(dbg_z.rearrange("(n p) t -> p n t", p=128)[:, :, tok],
                       bZ.rearrange("p (n t) -> p n t", n=NT))

                if "ta_sp" in edges:
                    p.wait("vector", "sAC", edges["ta_sp"])
                if "td_gate" in edges:
                    p.wait("scalar", "sDV", edges["td_gate"])
                for hf in range(2):
                    for j in range(HHT):
                        n = hf * HHT + j
                        asl = slice(j * T, (j + 1) * T)
                        for k in range(4):
                            wcol = wconv_sb[:, n * 4 + k:n * 4 + k + 1]
                            lead = 3 - k
                            if lead > 0:
                                hsl = hist_bf[:, n * 3 + k:n * 3 + 3]
                                if k == 0:
                                    dv(lambda e, a=asl, h=hsl, w=wcol, ld_=lead:
                                       e.tensor_scalar(
                                           out=sC[:, a][:, 0:ld_], in0=h,
                                           scalar1=w, scalar2=None,
                                           op0=ALU.mult))
                                else:
                                    dv(lambda e, a=asl, h=hsl, w=wcol, ld_=lead:
                                       e.scalar_tensor_tensor(
                                           out=sC[:, a][:, 0:ld_], in0=h,
                                           scalar=w,
                                           in1=sC[:, a][:, 0:ld_],
                                           op0=ALU.mult, op1=ALU.add))
                            bsl = bI[:, n * T:n * T + T - lead]
                            osl = sC[:, j * T + lead:(j + 1) * T]
                            if k == 0:
                                dv(lambda e, b=bsl, o=osl, w=wcol:
                                   e.tensor_scalar(out=o, in0=b, scalar1=w,
                                                   scalar2=None, op0=ALU.mult))
                            else:
                                dv(lambda e, b=bsl, o=osl, w=wcol:
                                   e.scalar_tensor_tensor(
                                       out=o, in0=b, scalar=w, in1=o,
                                       op0=ALU.mult, op1=ALU.add))
                        dv(lambda e, n=n: e.tensor_copy(
                            out=hist_bf[:, n * 3:n * 3 + 3],
                            in_=bI[:, n * T + T - 3:n * T + T]))
                    td = dvm(lambda e: e.nop())
                    edges["td_conv"] = td
                    p.wait("scalar", "sDV", td)
                    for j in range(HHT):
                        n = hf * HHT + j
                        ac(lambda e, j=j, n=n: e.activation(
                            out=sB[:, n * T:(n + 1) * T],
                            in_=sC[:, j * T:(j + 1) * T], func=AF.Silu,
                            bias=bconv_sb[:, n:n + 1]))
                    ta = acm(lambda e: e.nop())
                    p.wait("vector", "sAC", ta)
                if "tp_wout" in edges:
                    p.wait("vector", "sPE", edges["tp_wout"])
                for n in range(NT):
                    s = slice(n * T, (n + 1) * T)
                    dv(lambda e, s=s: e.tensor_copy(out=bC[:, s], in_=sB[:, s]))
                td = dvm(lambda e: e.nop())
                p.wait("tensor", "sDV", td)
                if DBG:
                    p.wait("sync", "sDV", td)
                    st(dbg_xc.rearrange("(n p) t -> p n t", p=128)[:, :, tok],
                       sB.rearrange("p (n t) -> p n t", n=NT))

                for (bank, c0, c1) in ((0, 0, DTR), (4, DTR, DTR + NST),
                                       (5, DTR + NST, 96)):
                    for k in range(NT):
                        f = (lambda e, k=k, bank=bank, c0=c0, c1=c1: e.matmul(
                            out=ps[bank][0:c1 - c0, 0:T],
                            lhsT=wxp_bf[:, k * 96 + c0:k * 96 + c1],
                            rhs=bC[:, k * T:(k + 1) * T],
                            start=(k == 0), stop=(k == NT - 1)))
                        tp = pem(f) if k == NT - 1 else pe(f)
                p.wait("scalar", "sPE", tp)
                ac(lambda e: e.copy(out=dlr_bf[:], in_=ps[0][0:DTR, 0:T]))
                ac(lambda e: e.copy(out=b_bf[:], in_=ps[4][0:NST, 0:T]))
                ta = acm(lambda e: e.copy(out=c_bf[:], in_=ps[5][0:NST, 0:T]))
                p.wait("tensor", "sAC", ta)
                p.wait("sync", "sAC", ta)
                for tkk in bc_last_toks:
                    p.wait("sync", tkk)
                tb1 = st(bc_dr[0:NST, :], b_bf[:])
                tb2 = st(bc_dr[NST:2 * NST, :], c_bf[:])

                if "tln_mo" in edges:
                    p.wait("scalar", "sDV", edges["tln_mo"])
                for hf in range(2):
                    for j in range(HHT):
                        n = hf * HHT + j
                        tp = pem(lambda e, n=n, j=j: e.matmul(
                            out=ps[j % 4][:, 0:T],
                            lhsT=wdt_bf[:, n * 128:(n + 1) * 128],
                            rhs=dlr_bf[:], start=True, stop=True))
                        if j % 4 == 3:
                            p.wait("scalar", "sPE", tp)
                            for jj in range(j - 3, j + 1):
                                n2 = hf * HHT + jj
                                ac(lambda e, jj=jj, n2=n2: e.copy(
                                    out=sC[:, jj * T:(jj + 1) * T],
                                    in_=ps[jj % 4][:, 0:T]))
                            ta = acm(lambda e: e.nop())
                            p.wait("tensor", "sAC", ta)
                    p.wait("vector", "sAC")
                    for j in range(HHT):
                        n = hf * HHT + j
                        s = slice(j * T, (j + 1) * T)
                        dv(lambda e, s=s, n=n: e.tensor_scalar_add(
                            out=sC[:, s], in0=sC[:, s],
                            scalar1=bdt_sb[:, n:n + 1]))
                    td = dvm(lambda e: e.nop())
                    p.wait("scalar", "sDV", td)
                    for j in range(HHT):
                        s = slice(j * T, (j + 1) * T)
                        ac(lambda e, s=s: e.activation(
                            out=sD[:, s], in_=sC[:, s], func=AF.Abs))
                        ac(lambda e, s=s: e.activation(
                            out=sD[:, s], in_=sD[:, s], func=AF.Exp,
                            scale=cst_sb[:, 2:3]))
                        ac(lambda e, s=s: e.activation(
                            out=sD[:, s], in_=sD[:, s], func=AF.Ln,
                            bias=cst_sb[:, 1:2]))
                    ta = acm(lambda e: e.nop())
                    edges["ta_sp"] = ta
                    p.wait("vector", "sAC", ta)
                    for j in range(HHT):
                        n = hf * HHT + j
                        s = slice(j * T, (j + 1) * T)
                        dv(lambda e, s=s: e.tensor_scalar_max(
                            out=sE[:, s], in0=sC[:, s], scalar1=0.0))
                        dv(lambda e, s=s: e.tensor_add(
                            out=sE[:, s], in0=sE[:, s], in1=sD[:, s]))
                        dv(lambda e, s=s, n=n: e.tensor_mul(
                            out=sF[:, s], in0=sE[:, s],
                            in1=sB[:, n * T:(n + 1) * T]))
                    td = dvm(lambda e: e.nop())
                    p.wait("scalar", "sDV", td)
                    p.wait("tensor", "sDV", td)
                    if DBG:
                        p.wait("sync", "sDV", td)
                        st(dbg_dl.rearrange("(n p) t -> p n t", p=128)
                           [:, hf * HHT:(hf + 1) * HHT, tok],
                           sE.rearrange("p (n t) -> p n t", n=HHT))

                    scan_done = {}
                    yp_dv = {}
                    yp_pe = {}
                    bc_free = {}
                    bc_tok = {}
                    for n in range(NST):
                        sl = n % 2
                        if n == 0:
                            p.wait("sync", "sST", tb1)
                            p.wait("sync", "sST", tb2)
                            p.wait("sync", "sDV")
                            p.wait("tensor", "sAC")
                        if n >= 2:
                            p.wait("sync", "sDV", bc_free[n - 2])
                        bc_tok[n] = (
                            ld(bcb[:, 2 * sl * T:(2 * sl + 1) * T],
                               bc_dr[n:n + 1, :].broadcast_to([128, T])),
                            ld(bcb[:, (2 * sl + 1) * T:(2 * sl + 2) * T],
                               bc_dr[NST + n:NST + n + 1, :]
                               .broadcast_to([128, T])))
                        for j in range(HHT):
                            ti = hf * HHT + j
                            it = n * HHT + j
                            da = sGf[:, (it % 2) * T:(it % 2 + 1) * T]
                            dbx = sG[:, 0 * T:1 * T]
                            hh = sG[:, 1 * T:2 * T]
                            yp = sG[:, (2 + it % 2) * T:(3 + it % 2) * T]
                            Bb = bcb[:, 2 * sl * T:(2 * sl + 1) * T]
                            Cb = bcb[:, (2 * sl + 1) * T:(2 * sl + 2) * T]
                            s = slice(j * T, (j + 1) * T)
                            if it >= 2:
                                p.wait("scalar", "sDV", scan_done[it - 2])
                            ta = ac(lambda e, s=s, da=da, ti=ti, n=n:
                                    e.activation(
                                        out=da, in_=sE[:, s], func=AF.Exp,
                                        scale=A_sb[:, ti * NST + n:
                                                   ti * NST + n + 1]))
                            if j == 0:
                                p.wait("vector", bc_tok[n][0])
                                p.wait("vector", bc_tok[n][1])
                            dv(lambda e, s=s, dbx=dbx, Bb=Bb: e.tensor_tensor(
                                out=dbx, in0=sF[:, s], in1=Bb, op=ALU.mult))
                            p.wait("vector", "sAC", ta)
                            sd = dv(lambda e, da=da, dbx=dbx, hh=hh, ti=ti, n=n:
                                    e.tensor_tensor_scan(
                                        out=hh, data0=da, data1=dbx,
                                        initial=hcar_sb[:, ti * NST + n:
                                                        ti * NST + n + 1],
                                        op0=ALU.mult, op1=ALU.add))
                            scan_done[it] = sd
                            dv(lambda e, hh=hh, ti=ti, n=n: e.tensor_copy(
                                out=hcar_sb[:, ti * NST + n:ti * NST + n + 1],
                                in_=hh[:, T - 1:T]))
                            if it >= 2:
                                p.wait("vector", "sPE", yp_pe[it - 2])
                            tyd = dv(lambda e, hh=hh, yp=yp, Cb=Cb:
                                     e.tensor_tensor(out=yp, in0=hh, in1=Cb,
                                                     op=ALU.mult))
                            yp_dv[it] = tyd
                            p.wait("tensor", "sDV", tyd)
                            yp_pe[it] = pem(lambda e, yp=yp, j=j, n=n: e.matmul(
                                out=ps[j][:, 0:T], lhsT=id_sb[:], rhs=yp,
                                start=(n == 0), stop=(n == NST - 1)))
                            if j == HHT - 1:
                                bc_free[n] = tyd
                                if n >= NST - 2:
                                    bc_last_toks.extend(bc_tok[n])
                    p.wait("scalar", "sPE")
                    for j in range(HHT):
                        ac(lambda e, j=j: e.copy(
                            out=sA[:, j * T:(j + 1) * T], in_=ps[j][:, 0:T]))
                    tyev = ac(lambda e: e.nop())
                    p.wait("vector", "sAC", tyev)
                    p.wait("tensor", "sAC", tyev)

                    if DBG:
                        p.wait("sync", "sDV")
                        st(dbg_y.rearrange("(n p) t -> p n t", p=128)
                           [:, hf * HHT:(hf + 1) * HHT, tok],
                           sA.rearrange("p (n t) -> p n t", n=HHT))
                        p.wait("vector", "sST")
                    for j in range(HHT):
                        n = hf * HHT + j
                        s = slice(j * T, (j + 1) * T)
                        ac(lambda e, s=s, n=n: e.activation(
                            out=sD[:, s], in_=bZ[:, n * T:(n + 1) * T],
                            func=AF.Silu))
                    ta = acm(lambda e: e.nop())
                    p.wait("vector", "sAC", ta)
                    for j in range(HHT):
                        n = hf * HHT + j
                        s = slice(j * T, (j + 1) * T)
                        dv(lambda e, s=s, n=n: e.scalar_tensor_tensor(
                            out=sA[:, s], in0=sB[:, n * T:(n + 1) * T],
                            scalar=D_sb[:, n:n + 1], in1=sA[:, s],
                            op0=ALU.mult, op1=ALU.add))
                        dv(lambda e, s=s, n=n: e.tensor_tensor(
                            out=bC[:, n * T:(n + 1) * T], in0=sA[:, s],
                            in1=sD[:, s], op=ALU.mult))
                    td = dvm(lambda e: e.nop())
                    edges["td_gate"] = td
                    p.wait("tensor", "sDV", td)
                    p.wait("scalar", "sDV", td)

                for mb in range(4):
                    tl = ld(wwin[:], wbf_out[mb * 128:(mb + 1) * 128, :])
                    p.wait("tensor", "sLD", tl)
                    for m in range(2):
                        for k in range(NT):
                            f = (lambda e, m=m, k=k: e.matmul(
                                out=ps[m][:, 0:T],
                                lhsT=wwin[:, k * 256 + m * 128:
                                          k * 256 + (m + 1) * 128],
                                rhs=bC[:, k * T:(k + 1) * T],
                                start=(k == 0), stop=(k == NT - 1)))
                            tp = pem(f) if k == NT - 1 else pe(f)
                    p.wait("scalar", "sPE", tp)
                    for m in range(2):
                        gi = mb * 2 + m
                        ac(lambda e, m=m, gi=gi: e.copy(
                            out=sE[:, gi * T:(gi + 1) * T], in_=ps[m][:, 0:T]))
                    ta = acm(lambda e: e.nop())
                    p.wait("tensor", "sAC", ta)
                    p.wait("sync", "sPE", tp)
                edges["tp_wout"] = tp
                p.wait("vector", "sAC")

                if DBG:
                    p.wait("sync", "sAC")
                    st(dbg_mo.rearrange("(k p) t -> p k t", p=128)[:, :, tok],
                       sE.rearrange("p (k t) -> p k t", k=KT))
                    p.wait("vector", "sST")
                    p.wait("scalar", "sST")
                tl = ld(sA.rearrange("p (k t) -> p k t", k=KT),
                        x_in.rearrange("(k p) t -> p k t", p=128)[:, :, tok])
                p.wait("scalar", "sLD", tl)
                p.wait("vector", "sLD", tl)
                tln = layer_norm2(sE, bA, bX, bQ)
                p.wait("vector", "sDV", tln)
                for k in range(KT):
                    s = slice(k * T, (k + 1) * T)
                    dv(lambda e, s=s: e.tensor_tensor(
                        out=sE[:, s], in0=sE[:, s], in1=ps[7][:, 0:T],
                        op=ALU.mult))
                    dv(lambda e, s=s: e.tensor_add(
                        out=sA[:, s], in0=sA[:, s], in1=sE[:, s]))
                td = dvm(lambda e: e.nop())
                edges["tln_mo"] = td
                p.wait("sync", "sDV", td)
                p.wait("scalar", "sDV", td)
                tsx = st(x2_dr.rearrange("(k p) t -> p k t", p=128)[:, :, tok],
                         sA.rearrange("p (k t) -> p k t", k=KT))
                edges["x2_store"] = tsx
                p.wait("vector", "sST", tsx)
                tln = layer_norm2(sA, bA, bX, bQ)
                edges["sA_free"] = tln
                p.wait("sync", "sDV", tln)
                ts = st(ln2_dr.rearrange("(k p) t -> p k t", p=128)[:, :, tok],
                        bA.rearrange("p (k t) -> p k t", k=KT))
                edges["ln2_store"] = ts

        p.barrier()
        with ExitStack() as fctx:
            fsb = lambda n, s, d=F32: fctx.enter_context(nc.sbuf_tensor(n, s, d))
            fa = fsb("fa", [128, HT * T], BF16)
            fb = fsb("fb", [128, HT * T], BF16)
            fx2 = fsb("fx2", [128, KT * T])
            ffv = fsb("ffv", [128, KT * T])
            flb = fsb("flb", [128, KT * T], BF16)
            fxb = fsb("fxb", [128, 2 * T], BF16)
            fqb = fsb("fqb", [128, 2 * T], BF16)
            fss = fsb("fss", [128, T])
            scal_sb = fsb("scal_sb", [128, 4 * HT + KT])
            ld(scal_sb[:, 0:HT], s1_in)
            ld(scal_sb[:, HT:2 * HT], s2_in)
            ld(scal_sb[:, 2 * HT:3 * HT], sh1_in)
            ld(scal_sb[:, 3 * HT:4 * HT], sh2_in)
            tl = ld(scal_sb[:, 4 * HT:4 * HT + KT], s3_in)
            p.wait("scalar", "sLD", tl)
            p.wait("vector", "sLD", tl)

            for j in range(QT // T):
                p.barrier()
                def dyn_ld(dst, src_r, jj=j):
                    def f(e, dst=dst, src_r=src_r, jj=jj):
                        base = e.value_load(qidx_sb[0:1, 0:1])
                        return e.dma_start(out=dst,
                                           in_=src_r[:, :, bass.ts(base + jj, T)])
                    return _dma("sLD", f)

                t1 = dyn_ld(flb.rearrange("p (k t) -> p k t", k=KT),
                            ln2_dr.rearrange("(k p) t -> p k t", p=128))
                t2 = dyn_ld(fx2.rearrange("p (k t) -> p k t", k=KT),
                            x2_dr.rearrange("(k p) t -> p k t", p=128))
                for tok in (t1, t2):
                    p.wait("tensor", "sLD", tok)
                    p.wait("vector", "sLD", tok)
                    p.wait("scalar", "sLD", tok)

                for m in range(HT):
                    t1 = ld(wwin[:, 0:KT * 128],
                            wbf_1[m * 128:(m + 1) * 128, :])
                    t2 = ld(wwin[:, 2048:2048 + KT * 128],
                            wbf_2[m * 128:(m + 1) * 128, :])
                    p.wait("tensor", "sLD", t1)
                    p.wait("tensor", "sLD", t2)
                    for k in range(KT):
                        f = (lambda e, k=k: e.matmul(
                            out=ps[0][:, 0:T],
                            lhsT=wwin[:, k * 128:(k + 1) * 128],
                            rhs=flb[:, k * T:(k + 1) * T],
                            start=(k == 0), stop=(k == KT - 1)))
                        tp = pem(f) if k == KT - 1 else pe(f)
                    for k in range(KT):
                        f = (lambda e, k=k: e.matmul(
                            out=ps[1][:, 0:T],
                            lhsT=wwin[:, 2048 + k * 128:2048 + (k + 1) * 128],
                            rhs=flb[:, k * T:(k + 1) * T],
                            start=(k == 0), stop=(k == KT - 1)))
                        tp = pem(f) if k == KT - 1 else pe(f)
                    p.wait("scalar", "sPE", tp)
                    ta = acm(lambda e, m=m: e.activation(
                        out=fss[:], in_=ps[0][:, 0:T], func=AF.Silu,
                        scale=scal_sb[:, m:m + 1]))
                    p.wait("vector", "sAC", ta)
                    td = dvm(lambda e, m=m: e.scalar_tensor_tensor(
                        out=fa[:, m * T:(m + 1) * T], in0=ps[1][:, 0:T],
                        scalar=scal_sb[:, HT + m:HT + m + 1], in1=fss[:],
                        op0=ALU.mult, op1=ALU.mult))
                    p.wait("tensor", "sDV", td)
                    p.wait("sync", "sPE", tp)

                def hlayer(wsrc, scol, src, dst):
                    for m in range(HT):
                        toks = [ld(wwin[:, kb * 2048:(kb + 1) * 2048],
                                   wsrc[m * 128:(m + 1) * 128,
                                        kb * 2048:(kb + 1) * 2048])
                                for kb in range(2)]
                        for tok in toks:
                            p.wait("tensor", "sLD", tok)
                        for k in range(HT):
                            f = (lambda e, k=k: e.matmul(
                                out=ps[2][:, 0:T],
                                lhsT=wwin[:, k * 128:(k + 1) * 128],
                                rhs=src[:, k * T:(k + 1) * T],
                                start=(k == 0), stop=(k == HT - 1)))
                            tp = pem(f) if k == HT - 1 else pe(f)
                        p.wait("scalar", "sPE", tp)
                        ta = acm(lambda e, m=m: e.activation(
                            out=dst[:, m * T:(m + 1) * T], in_=ps[2][:, 0:T],
                            func=AF.Gelu,
                            scale=scal_sb[:, scol + m:scol + m + 1]))
                        p.wait("tensor", "sAC", ta)
                        p.wait("sync", "sPE", tp)

                hlayer(wbf_h1, 2 * HT, fa, fb)
                hlayer(wbf_h2, 3 * HT, fb, fa)

                for m in range(KT):
                    toks = [ld(wwin[:, kb * 2048:(kb + 1) * 2048],
                               wbf_3[m * 128:(m + 1) * 128,
                                     kb * 2048:(kb + 1) * 2048])
                            for kb in range(2)]
                    for tok in toks:
                        p.wait("tensor", "sLD", tok)
                    for k in range(HT):
                        f = (lambda e, k=k: e.matmul(
                            out=ps[3][:, 0:T],
                            lhsT=wwin[:, k * 128:(k + 1) * 128],
                            rhs=fa[:, k * T:(k + 1) * T],
                            start=(k == 0), stop=(k == HT - 1)))
                        tp = pem(f) if k == HT - 1 else pe(f)
                    p.wait("scalar", "sPE", tp)
                    ta = acm(lambda e, m=m: e.activation(
                        out=ffv[:, m * T:(m + 1) * T], in_=ps[3][:, 0:T],
                        func=AF.Copy,
                        scale=scal_sb[:, 4 * HT + m:4 * HT + m + 1]))
                    p.wait("tensor", "sAC", ta)
                    p.wait("sync", "sPE", tp)
                p.wait("vector", "sAC")

                tln = layer_norm2(ffv, flb, fxb, fqb)
                p.wait("vector", "sDV", tln)
                for k in range(KT):
                    s = slice(k * T, (k + 1) * T)
                    dv(lambda e, s=s: e.tensor_tensor(
                        out=ffv[:, s], in0=ffv[:, s], in1=ps[7][:, 0:T],
                        op=ALU.mult))
                    dv(lambda e, s=s: e.tensor_add(
                        out=fx2[:, s], in0=fx2[:, s], in1=ffv[:, s]))
                td = dvm(lambda e: e.nop())
                p.wait("sync", "sDV", td)
                st(y_out.rearrange("(k p) t -> p k t", p=128)
                   [:, :, j * T:(j + 1) * T],
                   fx2.rearrange("p (k t) -> p k t", k=KT))

        p.wait("sync", "sST")

        @block.sync
        def _(sync):
            for fn in p.ops["sync"]:
                fn(sync)

        @block.tensor
        def _(tensor):
            for fn in p.ops["tensor"]:
                fn(tensor)

        @block.scalar
        def _(scalar):
            for fn in p.ops["scalar"]:
                fn(scalar)

        @block.vector
        def _(vector):
            for fn in p.ops["vector"]:
                fn(vector)

    return nc


def _pack_wdt(w):
    out = np.empty((128, DI // 2), np.float32)
    for n in range(NT):
        out[DTR * (n % 2):DTR * (n % 2) + DTR,
            (n // 2) * 128:(n // 2) * 128 + 128] = w[:, n * 128:(n + 1) * 128]
    return np.ascontiguousarray(out)


def _prep_inputs(kw):
    f32 = np.float32
    cac = lambda a: np.ascontiguousarray(np.asarray(a, f32))

    def tile_vec(v, ntiles):
        return cac(np.asarray(v, f32).reshape(ntiles, 128).T)

    x = np.asarray(kw["L0_attn_x"], f32)
    x_t = np.ascontiguousarray(np.transpose(x, (0, 2, 1)))

    A_log = np.asarray(kw["L0_attn_A_log"], f32)
    alog_t = cac(A_log.reshape(NT, 128, NST).transpose(1, 0, 2)
                 .reshape(128, NT * NST))
    wconv = np.asarray(kw["L0_attn_W_conv"], f32).reshape(DI, 4)
    wconv_t = cac(wconv.reshape(NT, 128, 4).transpose(1, 0, 2)
                  .reshape(128, NT * 4))
    sel = np.zeros((NST, NST * 128), ml_dtypes.bfloat16)
    for n in range(NST):
        sel[n, n * 128:(n + 1) * 128] = 1
    common = dict(
        w_in=cac(kw["L0_attn_W_in"]),
        wconv_t=wconv_t,
        bconv_t=tile_vec(kw["L0_attn_b_conv"], NT),
        w_xproj=cac(kw["L0_attn_W_xproj"]),
        w_dt=cac(kw["L0_attn_W_dt"]),
        bdt_t=tile_vec(kw["L0_attn_b_dt"], NT),
        alog_t=alog_t,
        d_t=tile_vec(kw["L0_attn_D"], NT),
        w_out=cac(kw["L0_attn_W_out"]),
        w1t=cac(np.asarray(kw["L0_ffn_w1_w"], f32).T),
        w2t=cac(np.asarray(kw["L0_ffn_w2_w"], f32).T),
        h1t=cac(np.asarray(kw["L0_ffn_h1_w"], f32).T),
        h2t=cac(np.asarray(kw["L0_ffn_h2_w"], f32).T),
        w3t=cac(np.asarray(kw["L0_ffn_w3_w"], f32).T),
        s1_t=tile_vec(kw["L0_ffn_w1_s"], HT),
        s2_t=tile_vec(kw["L0_ffn_w2_s"], HT),
        sh1_t=tile_vec(kw["L0_ffn_h1_s"], HT),
        sh2_t=tile_vec(kw["L0_ffn_h2_s"], HT),
        s3_t=tile_vec(kw["L0_ffn_w3_s"], KT),
        sel_bf=sel,
        ones_bf=np.ones((128, 1), ml_dtypes.bfloat16),
        id_bf=np.eye(128, dtype=ml_dtypes.bfloat16),
        one1_bf=np.ones((1, 128), ml_dtypes.bfloat16),
        cst=np.tile(np.array([[EPS, 1.0, -1.0, 0.0]], np.float32), (128, 1)),
    )
    in_maps = []
    for c in range(8):
        s, q = c // 4, c % 4
        m = dict(common)
        m["x_seq"] = x_t[s]
        m["qidx"] = np.full((1, 1), q * (QT // T), np.int32)
        in_maps.append(m)
    return in_maps


def kernel(**inputs):
    in_maps = _prep_inputs(inputs)
    nc = build_program()
    res = run_bass_kernel_spmd(nc, in_maps, core_ids=list(range(8)))
    out = np.empty((B, L, DM), np.float32)
    for c in range(8):
        s, q = c // 4, c % 4
        out[s, q * QT:(q + 1) * QT, :] = res.results[c]["y_out"].T
    return out


# version 06 (final)
# speedup vs baseline: 1.2297x; 1.0416x over previous
import numpy as np
import ml_dtypes
import concourse.bass as bass
import concourse.mybir as mybir
from concourse.bass_utils import run_bass_kernel_spmd
from contextlib import ExitStack

F32 = mybir.dt.float32
BF16 = mybir.dt.bfloat16
I32 = mybir.dt.int32
AF = mybir.ActivationFunctionType
ALU = mybir.AluOpType

DM, DI, DH, NST, DTR = 1024, 2048, 4096, 16, 64
DBG = False
L, B = 8192, 2
T = 512
NCH = L // T
QT = 2048
EPS = 1e-5
KT = DM // 128
NT = DI // 128
HT = DH // 128
HHT = NT // 2

ENGS = ("sync", "tensor", "scalar", "vector")


class Prog:
    def __init__(self, nc):
        self.nc = nc
        self.ops = {e: [] for e in ENGS}
        self.cnt = {}
        self.sems = {}
        self.last_wait = {e: {} for e in ENGS}

    def sem(self, ctx, name):
        h = ctx.enter_context(self.nc.semaphore(name))
        self.cnt[name] = 0
        self.sems[name] = h
        return h

    def on(self, eng, fn):
        self.ops[eng].append(fn)

    def wait(self, eng, name, val=None):
        if isinstance(name, tuple):
            name, val = name
        elif isinstance(val, tuple):
            name, val = val
        if name in ("sLD", "sST") and val is None:
            for s in self.pools[name]:
                self.wait(eng, s, self.cnt[s])
            return
        if val is None:
            val = self.cnt[name]
        if val <= 0 or self.last_wait[eng].get(name, -1) >= val:
            return
        self.last_wait[eng][name] = val
        h = self.sems[name]
        self.on(eng, lambda e, h=h, v=val: e.wait_ge(h, v))

    def mark(self, eng, name, fn, amt):
        self.cnt[name] += amt
        h = self.sems[name]
        self.on(eng, lambda e, h=h, a=amt, f=fn: f(e).then_inc(h, a))
        return self.cnt[name]

    def emit(self, eng, fn):
        self.on(eng, fn)

    def barrier(self):
        for e in ENGS:
            for s in ("sPE", "sAC", "sDV"):
                self.wait(e, s)
            self.wait(e, "sLD")
            self.wait(e, "sST")


def build_program():
    nc = bass.Bass("TRN2", target_bir_lowering=False)
    dram = nc.dram_tensor

    x_in = dram("x_seq", [DM, L], F32, kind="ExternalInput").ap()
    qidx_in = dram("qidx", [1, 1], I32, kind="ExternalInput").ap()
    w_in = dram("w_in", [DM, 2 * DI], F32, kind="ExternalInput").ap()
    wconv_in = dram("wconv_t", [128, NT * 4], F32, kind="ExternalInput").ap()
    bconv_in = dram("bconv_t", [128, NT], F32, kind="ExternalInput").ap()
    wxp_in = dram("w_xproj", [DI, 96], F32, kind="ExternalInput").ap()
    wdt_in = dram("w_dt", [DTR, DI], F32, kind="ExternalInput").ap()
    bdt_in = dram("bdt_t", [128, NT], F32, kind="ExternalInput").ap()
    alog_in = dram("alog_t", [128, NT * NST], F32, kind="ExternalInput").ap()
    d_in = dram("d_t", [128, NT], F32, kind="ExternalInput").ap()
    w_out_in = dram("w_out", [DI, DM], F32, kind="ExternalInput").ap()
    w1_in = dram("w1t", [DM, DH], F32, kind="ExternalInput").ap()
    w2_in = dram("w2t", [DM, DH], F32, kind="ExternalInput").ap()
    h1_in = dram("h1t", [DH, DH], F32, kind="ExternalInput").ap()
    h2_in = dram("h2t", [DH, DH], F32, kind="ExternalInput").ap()
    w3_in = dram("w3t", [DH, DM], F32, kind="ExternalInput").ap()
    s1_in = dram("s1_t", [128, HT], F32, kind="ExternalInput").ap()
    s2_in = dram("s2_t", [128, HT], F32, kind="ExternalInput").ap()
    sh1_in = dram("sh1_t", [128, HT], F32, kind="ExternalInput").ap()
    sh2_in = dram("sh2_t", [128, HT], F32, kind="ExternalInput").ap()
    s3_in = dram("s3_t", [128, KT], F32, kind="ExternalInput").ap()
    sel_in = dram("sel_bf", [NST, NST * 128], BF16, kind="ExternalInput").ap()
    ones_in = dram("ones_bf", [128, 1], BF16, kind="ExternalInput").ap()
    one1_in = dram("one1_bf", [1, 128], BF16, kind="ExternalInput").ap()
    id_in = dram("id_bf", [128, 128], BF16, kind="ExternalInput").ap()
    cst_in = dram("cst", [128, 4], F32, kind="ExternalInput").ap()

    y_out = dram("y_out", [DM, QT], F32, kind="ExternalOutput").ap()

    wbf_in = dram("wbf_in", [8 * 128, KT * 512], BF16, kind="Internal").ap()
    wbf_out = dram("wbf_out", [4 * 128, NT * 256], BF16, kind="Internal").ap()
    wbf_1 = dram("wbf_1", [HT * 128, KT * 128], BF16, kind="Internal").ap()
    wbf_2 = dram("wbf_2", [HT * 128, KT * 128], BF16, kind="Internal").ap()
    wbf_h1 = dram("wbf_h1", [HT * 128, HT * 128], BF16, kind="Internal").ap()
    wbf_h2 = dram("wbf_h2", [HT * 128, HT * 128], BF16, kind="Internal").ap()
    wbf_3 = dram("wbf_3", [KT * 128, HT * 128], BF16, kind="Internal").ap()
    bc_dr = dram("bc_dr", [2 * NST, T], BF16, kind="Internal").ap()
    x2_dr = dram("x2_dr", [DM, L], F32, kind="Internal").ap()
    ln2_dr = dram("ln2_dr", [DM, L], BF16, kind="Internal").ap()
    if DBG:
        dbg_xc = dram("dbg_xc", [DI, L], F32, kind="Internal").ap()
        dbg_dl = dram("dbg_dl", [DI, L], F32, kind="Internal").ap()
        dbg_y = dram("dbg_y", [DI, L], F32, kind="Internal").ap()
        dbg_mo = dram("dbg_mo", [DM, L], F32, kind="Internal").ap()
        dbg_ln1 = dram("dbg_ln1", [DM, L], BF16, kind="Internal").ap()
        dbg_xi = dram("dbg_xi", [DI, L], BF16, kind="Internal").ap()
        dbg_z = dram("dbg_z", [DI, L], BF16, kind="Internal").ap()

    with ExitStack() as ctx:
        p = Prog(nc)
        for s in ("sPE", "sAC", "sDV"):
            p.sem(ctx, s)
        p.pools = {"sLD": [], "sST": []}
        for i in range(16):
            n = f"sL{i}"
            p.sem(ctx, n)
            p.pools["sLD"].append(n)
        for i in range(8):
            n = f"sS{i}"
            p.sem(ctx, n)
            p.pools["sST"].append(n)
        p.rot = {"sLD": 0, "sST": 0}
        sb = lambda n, s, d=F32: ctx.enter_context(nc.sbuf_tensor(n, s, d))

        A_sb = sb("A_sb", [128, NT * NST])
        D_sb = sb("D_sb", [128, NT])
        bconv_sb = sb("bconv_sb", [128, NT])
        bdt_sb = sb("bdt_sb", [128, NT])
        wconv_sb = sb("wconv_sb", [128, NT * 4])
        wxp_bf = sb("wxp_bf", [128, NT * 96], BF16)
        wdt_bf = sb("wdt_bf", [DTR, DI], BF16)
        sel_sb = sb("sel_sb", [NST, NST * 128], BF16)
        ones_sb = sb("ones_sb", [128, 1], BF16)
        one1_sb = sb("one1_sb", [1, 128], BF16)
        cst_sb = sb("cst_sb", [128, 4])
        mr_bf = sb("mr_bf", [1, 2 * T], BF16)
        hist_bf = sb("hist_bf", [128, NT * 3], BF16)
        hcar_sb = sb("hcar_sb", [128, NT * NST])
        qidx_sb = ctx.enter_context(nc.sbuf_tensor("qidx_sb", [1, 1], I32))
        dlr_bf = sb("dlr_bf", [DTR, T], BF16)
        b_bf = sb("b_bf", [NST, T], BF16)
        c_bf = sb("c_bf", [NST, T], BF16)
        wwin = sb("wwin", [128, 4096], BF16)
        id_sb = sb("id_sb", [128, 128], BF16)
        cstg_f = sb("cstg_f", [128, 1024])
        cstg_b = sb("cstg_b", [128, 1024], BF16)

        ps = [ctx.enter_context(nc.psum_tensor(f"ps{i}", [128, 512], F32))
              for i in range(8)]

        block = ctx.enter_context(nc.Block())

        def _dma(pool, fn):
            i = p.rot[pool]
            p.rot[pool] = (i + 1) % len(p.pools[pool])
            sname = p.pools[pool][i]
            p.wait("sync", sname, p.cnt[sname])
            v = p.mark("sync", sname, fn, 16)
            return (sname, v)

        def ld(dst, src):
            return _dma("sLD",
                        lambda e, d=dst, r=src: e.dma_start(out=d, in_=r))

        def st(dst, src):
            return _dma("sST",
                        lambda e, d=dst, r=src: e.dma_start(out=d, in_=r))

        def pe(fn):
            p.emit("tensor", fn)

        def pem(fn):
            return p.mark("tensor", "sPE", fn, 1)

        def ac(fn):
            p.wait("scalar", "sAC", p.cnt["sAC"])
            return p.mark("scalar", "sAC", fn, 1)

        acm = ac

        def dv(fn):
            p.wait("vector", "sDV", p.cnt["sDV"])
            return p.mark("vector", "sDV", fn, 1)

        dvm = dv

        with nc.sbuf_tensor("cast_f", [128, 4096], F32) as cast_f, \
             nc.sbuf_tensor("cast_b", [128, 4096], BF16) as cast_b:

            ld(D_sb[:], d_in)
            ld(bconv_sb[:], bconv_in)
            ld(bdt_sb[:], bdt_in)
            ld(wconv_sb[:], wconv_in)
            ld(sel_sb[:], sel_in)
            ld(ones_sb[:], ones_in)
            ld(one1_sb[:], one1_in)
            ld(id_sb[:], id_in)
            ld(qidx_sb[:], qidx_in)
            ld(cst_sb[:], cst_in)

            ld(cast_f[:, 0:NT * NST], alog_in)
            p.wait("scalar", "sLD")
            acm(lambda e: e.activation(out=A_sb[:], in_=cast_f[:, 0:NT * NST],
                                       func=AF.Exp))
            p.wait("sync", "sAC")
            p.wait("vector", "sAC")
            dvm(lambda e: e.tensor_scalar_mul(out=A_sb[:], in0=A_sb[:],
                                              scalar1=-1.0))

            ld(cast_f[:, 0:NT * 96],
               wxp_in.rearrange("(n p) c -> p n c", p=128))
            p.wait("scalar", "sLD")
            acm(lambda e: e.copy(out=wxp_bf[:], in_=cast_f[:, 0:NT * 96]))
            p.wait("sync", "sAC")
            ld(cast_f[0:DTR, 0:DI], wdt_in)
            p.wait("scalar", "sLD")
            acm(lambda e: e.copy(out=wdt_bf[:], in_=cast_f[0:DTR, 0:DI]))
            p.wait("sync", "sAC")

            def cast_tiled(src, dst, krows, mcols, slab_m, mc):
                for k in range(krows):
                    p.wait("sync", "sAC")
                    tl = ld(cast_f[:, 0:mcols * 128],
                            src[k * 128:(k + 1) * 128, :])
                    p.wait("scalar", "sLD", tl)
                    tc = acm(lambda e: e.copy(out=cast_b[:, 0:mcols * 128],
                                              in_=cast_f[:, 0:mcols * 128]))
                    p.wait("sync", "sAC", tc)
                    for mb in range(mcols // slab_m):
                        w = slab_m * 128
                        st(dst[mb * 128:(mb + 1) * 128,
                               k * w:(k + 1) * w],
                           cast_b[:, mb * w:(mb + 1) * w])
                    p.wait("scalar", "sST")

            cast_tiled(w_in, wbf_in, KT, 2 * NT, 4, 512)
            cast_tiled(w_out_in, wbf_out, NT, KT, 2, 256)
            p.barrier()

        dvm(lambda e: e.memset(hist_bf[:], 0.0))
        dvm(lambda e: e.memset(hcar_sb[:], 0.0))
        p.barrier()

        cast_steps = []
        cast_state = {"tc": 0}

        def make_cast_step(srcw, dst, k, c0, ncols, slab_m):
            def step():
                p.wait("sync", "sAC", cast_state["tc"])
                tl = ld(cstg_f[:, 0:ncols],
                        srcw[k * 128:(k + 1) * 128, c0:c0 + ncols])
                p.wait("scalar", "sLD", tl)
                for tok in cast_state.get("sts", []):
                    p.wait("scalar", tok)
                tc = ac(lambda e: e.copy(out=cstg_b[:, 0:ncols],
                                         in_=cstg_f[:, 0:ncols]))
                cast_state["tc"] = tc
                p.wait("sync", "sAC", tc)
                w = slab_m * 128
                sts = []
                for mb in range(c0 // w, (c0 + ncols) // w):
                    sts.append(st(dst[mb * 128:(mb + 1) * 128,
                                      k * w:(k + 1) * w],
                                  cstg_b[:, mb * w - c0:(mb + 1) * w - c0]))
                cast_state["sts"] = sts
            return step

        for (srcw, dst, krows, mcols) in (
                (w1_in, wbf_1, KT, HT), (w2_in, wbf_2, KT, HT),
                (h1_in, wbf_h1, HT, HT), (h2_in, wbf_h2, HT, HT),
                (w3_in, wbf_3, HT, KT)):
            for k in range(krows):
                for c0 in range(0, mcols * 128, 1024):
                    nc_ = min(1024, mcols * 128 - c0)
                    cast_steps.append(make_cast_step(srcw, dst, k, c0, nc_, 1))

        def layer_norm2(v, out_bf, xb, qb):
            pe_done = []
            if getattr(p, "_ln_last_pe", 0):
                p.wait("scalar", "sPE", p._ln_last_pe)
            for k in range(KT):
                s = slice(k * T, (k + 1) * T)
                sl = slice((k % 2) * T, (k % 2 + 1) * T)
                if k >= 2:
                    p.wait("scalar", "sPE", pe_done[k - 2])
                ac(lambda e, s=s, sl=sl: e.copy(out=xb[:, sl], in_=v[:, s]))
                ta = acm(lambda e, s=s, sl=sl: e.activation(
                    out=qb[:, sl], in_=v[:, s], func=AF.Square))
                p.wait("tensor", "sAC", ta)
                if k < KT - 1:
                    pe(lambda e, sl=sl, k=k: e.matmul(
                        out=ps[6][0:1, 0:T], lhsT=ones_sb[:], rhs=xb[:, sl],
                        start=(k == 0), stop=False))
                else:
                    pem(lambda e, sl=sl, k=k: e.matmul(
                        out=ps[6][0:1, 0:T], lhsT=ones_sb[:], rhs=xb[:, sl],
                        start=(k == 0), stop=True))
                pe_done.append(pem(lambda e, sl=sl, k=k: e.matmul(
                    out=ps[7][0:1, 0:T], lhsT=ones_sb[:], rhs=qb[:, sl],
                    start=(k == 0), stop=(k == KT - 1))))
            p._ln_last_pe = pe_done[-1]
            p.wait("vector", "sPE")
            dv(lambda e: e.tensor_scalar_mul(out=ps[6][0:1, 0:T],
                                             in0=ps[6][0:1, 0:T],
                                             scalar1=1.0 / DM))
            td = dvm(lambda e: e.tensor_scalar_mul(out=ps[7][0:1, 0:T],
                                                   in0=ps[7][0:1, 0:T],
                                                   scalar1=1.0 / DM))
            p.wait("scalar", "sDV", td)
            ac(lambda e: e.copy(out=mr_bf[0:1, 0:T], in_=ps[6][0:1, 0:T]))
            ta = acm(lambda e: e.activation(out=mr_bf[0:1, T:2 * T],
                                            in_=ps[6][0:1, 0:T],
                                            func=AF.Square))
            p.wait("vector", "sAC", ta)
            td = dvm(lambda e: e.tensor_sub(out=ps[7][0:1, 0:T],
                                            in0=ps[7][0:1, 0:T],
                                            in1=mr_bf[0:1, T:2 * T]))
            p.wait("scalar", "sDV", td)
            ta = acm(lambda e: e.activation(out=ps[7][0:1, 0:T],
                                            in_=ps[7][0:1, 0:T],
                                            func=AF.Sqrt,
                                            bias=cst_sb[0:1, 0:1]))
            p.wait("vector", "sAC", ta)
            td = dvm(lambda e: e.reciprocal(out=ps[7][0:1, 0:T],
                                            in_=ps[7][0:1, 0:T]))
            p.wait("scalar", "sDV", td)
            ta = acm(lambda e: e.copy(out=mr_bf[0:1, T:2 * T],
                                      in_=ps[7][0:1, 0:T]))
            p.wait("tensor", "sAC", ta)
            pem(lambda e: e.matmul(out=ps[6][:, 0:T], lhsT=one1_sb[:],
                                   rhs=mr_bf[0:1, 0:T], start=True, stop=True))
            tp = pem(lambda e: e.matmul(out=ps[7][:, 0:T], lhsT=one1_sb[:],
                                        rhs=mr_bf[0:1, T:2 * T],
                                        start=True, stop=True))
            p.wait("vector", "sPE", tp)
            for k in range(KT):
                s = slice(k * T, (k + 1) * T)
                dv(lambda e, s=s: e.scalar_tensor_tensor(
                    out=v[:, s], in0=ps[6][:, 0:T], scalar=-1.0, in1=v[:, s],
                    op0=ALU.mult, op1=ALU.add))
                dv(lambda e, s=s: e.tensor_tensor(
                    out=out_bf[:, s], in0=v[:, s], in1=ps[7][:, 0:T],
                    op=ALU.mult))
            return dvm(lambda e: e.nop())

        with ExitStack() as mctx:
            msb = lambda n, s, d=F32: mctx.enter_context(nc.sbuf_tensor(n, s, d))
            sA = msb("sA", [128, KT * T])
            sB = msb("sB", [128, NT * T])
            sC = msb("sC", [128, HHT * T])
            sD = msb("sD", [128, HHT * T])
            sE = msb("sE", [128, HHT * T])
            sF = msb("sF", [128, HHT * T], BF16)
            sG = msb("sG", [128, 4 * T], BF16)
            sGf = msb("sGf", [128, 4 * T])
            bcb = msb("bcb", [128, 4 * T], BF16)
            bA = msb("bA", [128, KT * T], BF16)
            bX = msb("bX", [128, 2 * T], BF16)
            bQ = msb("bQ", [128, 2 * T], BF16)
            bI = msb("bI", [128, NT * T], BF16)
            bZ = msb("bZ", [128, NT * T], BF16)
            bC = bI

            bc_last_toks = []
            edges = {}
            for c in range(NCH):
                tok = slice(c * T, (c + 1) * T)
                if c > 0:
                    bc_last_toks = bc_last_toks[-4:]

                if "sA_free" in edges:
                    p.wait("sync", "sDV", edges["sA_free"])
                    p.wait("sync", edges["x2_store"])
                    p.wait("sync", edges["ln2_store"])
                tl = ld(sA.rearrange("p (k t) -> p k t", k=KT),
                        x_in.rearrange("(k p) t -> p k t", p=128)[:, :, tok])
                p.wait("scalar", "sLD", tl)
                p.wait("vector", "sLD", tl)
                if "tp_win" in edges:
                    p.wait("vector", "sPE", edges["tp_win"])
                    p.wait("scalar", "sPE", edges["tp_win"])
                tln = layer_norm2(sA, bA, bX, bQ)
                p.wait("tensor", "sDV", tln)
                if DBG:
                    p.wait("sync", "sDV", tln)
                    st(dbg_ln1.rearrange("(k p) t -> p k t", p=128)[:, :, tok],
                       bA.rearrange("p (k t) -> p k t", k=KT))

                if "td_conv" in edges:
                    p.wait("scalar", "sDV", edges["td_conv"])
                if "tp_wout" in edges:
                    p.wait("scalar", "sPE", edges["tp_wout"])
                for mb in range(8):
                    tl = ld(wwin[:], wbf_in[mb * 128:(mb + 1) * 128, :])
                    p.wait("tensor", "sLD", tl)
                    for m in range(4):
                        for k in range(KT):
                            f = (lambda e, m=m, k=k: e.matmul(
                                out=ps[m][:, 0:T],
                                lhsT=wwin[:, k * 512 + m * 128:
                                          k * 512 + (m + 1) * 128],
                                rhs=bA[:, k * T:(k + 1) * T],
                                start=(k == 0), stop=(k == KT - 1)))
                            tp = pem(f) if k == KT - 1 else pe(f)
                    p.wait("scalar", "sPE", tp)
                    for m in range(4):
                        gi = mb * 4 + m
                        if gi < NT:
                            dst = bI[:, gi * T:(gi + 1) * T]
                        else:
                            dst = bZ[:, (gi - NT) * T:(gi - NT + 1) * T]
                        ac(lambda e, m=m, d=dst: e.copy(out=d,
                                                        in_=ps[m][:, 0:T]))
                    ta = acm(lambda e: e.nop())
                    p.wait("tensor", "sAC", ta)
                    p.wait("sync", "sPE", tp)
                edges["tp_win"] = tp
                nsteps = (len(cast_steps) + NCH - 1 - c) // (NCH - c) \
                    if c < NCH else 0
                for _ in range(min(nsteps, len(cast_steps))):
                    cast_steps.pop(0)()
                p.wait("vector", "sAC")
                if DBG:
                    p.wait("sync", "sAC")
                    st(dbg_xi.rearrange("(n p) t -> p n t", p=128)[:, :, tok],
                       bI.rearrange("p (n t) -> p n t", n=NT)[:, :, 0:T])
                    st(dbg_z.rearrange("(n p) t -> p n t", p=128)[:, :, tok],
                       bZ.rearrange("p (n t) -> p n t", n=NT))

                if "ta_sp" in edges:
                    p.wait("vector", "sAC", edges["ta_sp"])
                if "td_gate" in edges:
                    p.wait("scalar", "sDV", edges["td_gate"])
                for hf in range(2):
                    for j in range(HHT):
                        n = hf * HHT + j
                        asl = slice(j * T, (j + 1) * T)
                        for k in range(4):
                            wcol = wconv_sb[:, n * 4 + k:n * 4 + k + 1]
                            lead = 3 - k
                            if lead > 0:
                                hsl = hist_bf[:, n * 3 + k:n * 3 + 3]
                                if k == 0:
                                    dv(lambda e, a=asl, h=hsl, w=wcol, ld_=lead:
                                       e.tensor_scalar(
                                           out=sC[:, a][:, 0:ld_], in0=h,
                                           scalar1=w, scalar2=None,
                                           op0=ALU.mult))
                                else:
                                    dv(lambda e, a=asl, h=hsl, w=wcol, ld_=lead:
                                       e.scalar_tensor_tensor(
                                           out=sC[:, a][:, 0:ld_], in0=h,
                                           scalar=w,
                                           in1=sC[:, a][:, 0:ld_],
                                           op0=ALU.mult, op1=ALU.add))
                            bsl = bI[:, n * T:n * T + T - lead]
                            osl = sC[:, j * T + lead:(j + 1) * T]
                            if k == 0:
                                dv(lambda e, b=bsl, o=osl, w=wcol:
                                   e.tensor_scalar(out=o, in0=b, scalar1=w,
                                                   scalar2=None, op0=ALU.mult))
                            else:
                                dv(lambda e, b=bsl, o=osl, w=wcol:
                                   e.scalar_tensor_tensor(
                                       out=o, in0=b, scalar=w, in1=o,
                                       op0=ALU.mult, op1=ALU.add))
                        dv(lambda e, n=n: e.tensor_copy(
                            out=hist_bf[:, n * 3:n * 3 + 3],
                            in_=bI[:, n * T + T - 3:n * T + T]))
                    td = dvm(lambda e: e.nop())
                    edges["td_conv"] = td
                    p.wait("scalar", "sDV", td)
                    for j in range(HHT):
                        n = hf * HHT + j
                        ac(lambda e, j=j, n=n: e.activation(
                            out=sB[:, n * T:(n + 1) * T],
                            in_=sC[:, j * T:(j + 1) * T], func=AF.Silu,
                            bias=bconv_sb[:, n:n + 1]))
                    ta = acm(lambda e: e.nop())
                    p.wait("vector", "sAC", ta)
                if "tp_wout" in edges:
                    p.wait("vector", "sPE", edges["tp_wout"])
                for n in range(NT):
                    s = slice(n * T, (n + 1) * T)
                    dv(lambda e, s=s: e.tensor_copy(out=bC[:, s], in_=sB[:, s]))
                td = dvm(lambda e: e.nop())
                p.wait("tensor", "sDV", td)
                if DBG:
                    p.wait("sync", "sDV", td)
                    st(dbg_xc.rearrange("(n p) t -> p n t", p=128)[:, :, tok],
                       sB.rearrange("p (n t) -> p n t", n=NT))

                for (bank, c0, c1) in ((0, 0, DTR), (4, DTR, DTR + NST),
                                       (5, DTR + NST, 96)):
                    for k in range(NT):
                        f = (lambda e, k=k, bank=bank, c0=c0, c1=c1: e.matmul(
                            out=ps[bank][0:c1 - c0, 0:T],
                            lhsT=wxp_bf[:, k * 96 + c0:k * 96 + c1],
                            rhs=bC[:, k * T:(k + 1) * T],
                            start=(k == 0), stop=(k == NT - 1)))
                        tp = pem(f) if k == NT - 1 else pe(f)
                p.wait("scalar", "sPE", tp)
                ac(lambda e: e.copy(out=dlr_bf[:], in_=ps[0][0:DTR, 0:T]))
                ac(lambda e: e.copy(out=b_bf[:], in_=ps[4][0:NST, 0:T]))
                ta = acm(lambda e: e.copy(out=c_bf[:], in_=ps[5][0:NST, 0:T]))
                p.wait("tensor", "sAC", ta)
                p.wait("sync", "sAC", ta)
                for tkk in bc_last_toks:
                    p.wait("sync", tkk)
                tb1 = st(bc_dr[0:NST, :], b_bf[:])
                tb2 = st(bc_dr[NST:2 * NST, :], c_bf[:])

                if "tln_mo" in edges:
                    p.wait("scalar", "sDV", edges["tln_mo"])
                for hf in range(2):
                    for j in range(HHT):
                        n = hf * HHT + j
                        tp = pem(lambda e, n=n, j=j: e.matmul(
                            out=ps[j % 4][:, 0:T],
                            lhsT=wdt_bf[:, n * 128:(n + 1) * 128],
                            rhs=dlr_bf[:], start=True, stop=True))
                        if j % 4 == 3:
                            p.wait("scalar", "sPE", tp)
                            for jj in range(j - 3, j + 1):
                                n2 = hf * HHT + jj
                                ac(lambda e, jj=jj, n2=n2: e.copy(
                                    out=sC[:, jj * T:(jj + 1) * T],
                                    in_=ps[jj % 4][:, 0:T]))
                            ta = acm(lambda e: e.nop())
                            p.wait("tensor", "sAC", ta)
                    p.wait("vector", "sAC")
                    for j in range(HHT):
                        n = hf * HHT + j
                        s = slice(j * T, (j + 1) * T)
                        dv(lambda e, s=s, n=n: e.tensor_scalar_add(
                            out=sC[:, s], in0=sC[:, s],
                            scalar1=bdt_sb[:, n:n + 1]))
                    td = dvm(lambda e: e.nop())
                    p.wait("scalar", "sDV", td)
                    for j in range(HHT):
                        s = slice(j * T, (j + 1) * T)
                        ac(lambda e, s=s: e.activation(
                            out=sD[:, s], in_=sC[:, s], func=AF.Abs))
                        ac(lambda e, s=s: e.activation(
                            out=sD[:, s], in_=sD[:, s], func=AF.Exp,
                            scale=cst_sb[:, 2:3]))
                        ac(lambda e, s=s: e.activation(
                            out=sD[:, s], in_=sD[:, s], func=AF.Ln,
                            bias=cst_sb[:, 1:2]))
                    ta = acm(lambda e: e.nop())
                    edges["ta_sp"] = ta
                    p.wait("vector", "sAC", ta)
                    for j in range(HHT):
                        n = hf * HHT + j
                        s = slice(j * T, (j + 1) * T)
                        dv(lambda e, s=s: e.tensor_scalar_max(
                            out=sE[:, s], in0=sC[:, s], scalar1=0.0))
                        dv(lambda e, s=s: e.tensor_add(
                            out=sE[:, s], in0=sE[:, s], in1=sD[:, s]))
                        dv(lambda e, s=s, n=n: e.tensor_mul(
                            out=sF[:, s], in0=sE[:, s],
                            in1=sB[:, n * T:(n + 1) * T]))
                    td = dvm(lambda e: e.nop())
                    p.wait("scalar", "sDV", td)
                    p.wait("tensor", "sDV", td)
                    if DBG:
                        p.wait("sync", "sDV", td)
                        st(dbg_dl.rearrange("(n p) t -> p n t", p=128)
                           [:, hf * HHT:(hf + 1) * HHT, tok],
                           sE.rearrange("p (n t) -> p n t", n=HHT))

                    scan_done = {}
                    yp_dv = {}
                    yp_pe = {}
                    bc_free = {}
                    bc_tok = {}
                    for n in range(NST):
                        sl = n % 2
                        if n == 0:
                            p.wait("sync", "sST", tb1)
                            p.wait("sync", "sST", tb2)
                            p.wait("sync", "sDV")
                            p.wait("tensor", "sAC")
                        if n >= 2:
                            p.wait("sync", "sDV", bc_free[n - 2])
                        bc_tok[n] = (
                            ld(bcb[:, 2 * sl * T:(2 * sl + 1) * T],
                               bc_dr[n:n + 1, :].broadcast_to([128, T])),
                            ld(bcb[:, (2 * sl + 1) * T:(2 * sl + 2) * T],
                               bc_dr[NST + n:NST + n + 1, :]
                               .broadcast_to([128, T])))
                        for j in range(HHT):
                            ti = hf * HHT + j
                            it = n * HHT + j
                            da = sGf[:, (it % 4) * T:(it % 4 + 1) * T]
                            dbx = sG[:, 0 * T:1 * T]
                            hh = sG[:, 1 * T:2 * T]
                            yp = sG[:, (2 + it % 2) * T:(3 + it % 2) * T]
                            Bb = bcb[:, 2 * sl * T:(2 * sl + 1) * T]
                            Cb = bcb[:, (2 * sl + 1) * T:(2 * sl + 2) * T]
                            s = slice(j * T, (j + 1) * T)
                            if it >= 4:
                                p.wait("scalar", "sDV", scan_done[it - 4])
                            ta = ac(lambda e, s=s, da=da, ti=ti, n=n:
                                    e.activation(
                                        out=da, in_=sE[:, s], func=AF.Exp,
                                        scale=A_sb[:, ti * NST + n:
                                                   ti * NST + n + 1]))
                            if j == 0:
                                p.wait("vector", bc_tok[n][0])
                                p.wait("vector", bc_tok[n][1])
                            dv(lambda e, s=s, dbx=dbx, Bb=Bb: e.tensor_tensor(
                                out=dbx, in0=sF[:, s], in1=Bb, op=ALU.mult))
                            p.wait("vector", "sAC", ta)
                            sd = dv(lambda e, da=da, dbx=dbx, hh=hh, ti=ti, n=n:
                                    e.tensor_tensor_scan(
                                        out=hh, data0=da, data1=dbx,
                                        initial=hcar_sb[:, ti * NST + n:
                                                        ti * NST + n + 1],
                                        op0=ALU.mult, op1=ALU.add))
                            scan_done[it] = sd
                            dv(lambda e, hh=hh, ti=ti, n=n: e.tensor_copy(
                                out=hcar_sb[:, ti * NST + n:ti * NST + n + 1],
                                in_=hh[:, T - 1:T]))
                            if it >= 2:
                                p.wait("vector", "sPE", yp_pe[it - 2])
                            tyd = dv(lambda e, hh=hh, yp=yp, Cb=Cb:
                                     e.tensor_tensor(out=yp, in0=hh, in1=Cb,
                                                     op=ALU.mult))
                            yp_dv[it] = tyd
                            p.wait("tensor", "sDV", tyd)
                            yp_pe[it] = pem(lambda e, yp=yp, j=j, n=n: e.matmul(
                                out=ps[j][:, 0:T], lhsT=id_sb[:], rhs=yp,
                                start=(n == 0), stop=(n == NST - 1)))
                            if j == HHT - 1:
                                bc_free[n] = tyd
                                if n >= NST - 2:
                                    bc_last_toks.extend(bc_tok[n])
                    p.wait("scalar", "sPE")
                    for j in range(HHT):
                        ac(lambda e, j=j: e.copy(
                            out=sA[:, j * T:(j + 1) * T], in_=ps[j][:, 0:T]))
                    tyev = ac(lambda e: e.nop())
                    p.wait("vector", "sAC", tyev)
                    p.wait("tensor", "sAC", tyev)

                    if DBG:
                        p.wait("sync", "sDV")
                        st(dbg_y.rearrange("(n p) t -> p n t", p=128)
                           [:, hf * HHT:(hf + 1) * HHT, tok],
                           sA.rearrange("p (n t) -> p n t", n=HHT))
                        p.wait("vector", "sST")
                    for j in range(HHT):
                        n = hf * HHT + j
                        s = slice(j * T, (j + 1) * T)
                        ac(lambda e, s=s, n=n: e.activation(
                            out=sD[:, s], in_=bZ[:, n * T:(n + 1) * T],
                            func=AF.Silu))
                    ta = acm(lambda e: e.nop())
                    p.wait("vector", "sAC", ta)
                    for j in range(HHT):
                        n = hf * HHT + j
                        s = slice(j * T, (j + 1) * T)
                        dv(lambda e, s=s, n=n: e.scalar_tensor_tensor(
                            out=sA[:, s], in0=sB[:, n * T:(n + 1) * T],
                            scalar=D_sb[:, n:n + 1], in1=sA[:, s],
                            op0=ALU.mult, op1=ALU.add))
                        dv(lambda e, s=s, n=n: e.tensor_tensor(
                            out=bC[:, n * T:(n + 1) * T], in0=sA[:, s],
                            in1=sD[:, s], op=ALU.mult))
                    td = dvm(lambda e: e.nop())
                    edges["td_gate"] = td
                    p.wait("tensor", "sDV", td)
                    p.wait("scalar", "sDV", td)

                for mb in range(4):
                    tl = ld(wwin[:], wbf_out[mb * 128:(mb + 1) * 128, :])
                    p.wait("tensor", "sLD", tl)
                    for m in range(2):
                        for k in range(NT):
                            f = (lambda e, m=m, k=k: e.matmul(
                                out=ps[m][:, 0:T],
                                lhsT=wwin[:, k * 256 + m * 128:
                                          k * 256 + (m + 1) * 128],
                                rhs=bC[:, k * T:(k + 1) * T],
                                start=(k == 0), stop=(k == NT - 1)))
                            tp = pem(f) if k == NT - 1 else pe(f)
                    p.wait("scalar", "sPE", tp)
                    for m in range(2):
                        gi = mb * 2 + m
                        ac(lambda e, m=m, gi=gi: e.copy(
                            out=sE[:, gi * T:(gi + 1) * T], in_=ps[m][:, 0:T]))
                    ta = acm(lambda e: e.nop())
                    p.wait("tensor", "sAC", ta)
                    p.wait("sync", "sPE", tp)
                edges["tp_wout"] = tp
                p.wait("vector", "sAC")

                if DBG:
                    p.wait("sync", "sAC")
                    st(dbg_mo.rearrange("(k p) t -> p k t", p=128)[:, :, tok],
                       sE.rearrange("p (k t) -> p k t", k=KT))
                    p.wait("vector", "sST")
                    p.wait("scalar", "sST")
                tl = ld(sA.rearrange("p (k t) -> p k t", k=KT),
                        x_in.rearrange("(k p) t -> p k t", p=128)[:, :, tok])
                p.wait("scalar", "sLD", tl)
                p.wait("vector", "sLD", tl)
                tln = layer_norm2(sE, bA, bX, bQ)
                p.wait("vector", "sDV", tln)
                for k in range(KT):
                    s = slice(k * T, (k + 1) * T)
                    dv(lambda e, s=s: e.tensor_tensor(
                        out=sE[:, s], in0=sE[:, s], in1=ps[7][:, 0:T],
                        op=ALU.mult))
                    dv(lambda e, s=s: e.tensor_add(
                        out=sA[:, s], in0=sA[:, s], in1=sE[:, s]))
                td = dvm(lambda e: e.nop())
                edges["tln_mo"] = td
                p.wait("sync", "sDV", td)
                p.wait("scalar", "sDV", td)
                tsx = st(x2_dr.rearrange("(k p) t -> p k t", p=128)[:, :, tok],
                         sA.rearrange("p (k t) -> p k t", k=KT))
                edges["x2_store"] = tsx
                p.wait("vector", "sST", tsx)
                tln = layer_norm2(sA, bA, bX, bQ)
                edges["sA_free"] = tln
                p.wait("sync", "sDV", tln)
                ts = st(ln2_dr.rearrange("(k p) t -> p k t", p=128)[:, :, tok],
                        bA.rearrange("p (k t) -> p k t", k=KT))
                edges["ln2_store"] = ts

        p.barrier()
        with ExitStack() as fctx:
            fsb = lambda n, s, d=F32: fctx.enter_context(nc.sbuf_tensor(n, s, d))
            fa = fsb("fa", [128, HT * T], BF16)
            fb = fsb("fb", [128, HT * T], BF16)
            fx2 = fsb("fx2", [128, KT * T])
            ffv = fsb("ffv", [128, KT * T])
            flb = fsb("flb", [128, KT * T], BF16)
            fxb = fsb("fxb", [128, 2 * T], BF16)
            fqb = fsb("fqb", [128, 2 * T], BF16)
            fss = fsb("fss", [128, T])
            wwinF = fsb("wwinF", [128, 4096], BF16)
            scal_sb = fsb("scal_sb", [128, 4 * HT + KT])
            fwin_tp = {0: 0, 1: 0}
            ld(scal_sb[:, 0:HT], s1_in)
            ld(scal_sb[:, HT:2 * HT], s2_in)
            ld(scal_sb[:, 2 * HT:3 * HT], sh1_in)
            ld(scal_sb[:, 3 * HT:4 * HT], sh2_in)
            tl = ld(scal_sb[:, 4 * HT:4 * HT + KT], s3_in)
            p.wait("scalar", "sLD", tl)
            p.wait("vector", "sLD", tl)

            for j in range(QT // T):
                p.barrier()
                def dyn_ld(dst, src_r, jj=j):
                    def f(e, dst=dst, src_r=src_r, jj=jj):
                        base = e.value_load(qidx_sb[0:1, 0:1])
                        return e.dma_start(out=dst,
                                           in_=src_r[:, :, bass.ts(base + jj, T)])
                    return _dma("sLD", f)

                t1 = dyn_ld(flb.rearrange("p (k t) -> p k t", k=KT),
                            ln2_dr.rearrange("(k p) t -> p k t", p=128))
                t2 = dyn_ld(fx2.rearrange("p (k t) -> p k t", k=KT),
                            x2_dr.rearrange("(k p) t -> p k t", p=128))
                for tok in (t1, t2):
                    p.wait("tensor", "sLD", tok)
                    p.wait("vector", "sLD", tok)
                    p.wait("scalar", "sLD", tok)

                win_tp = fwin_tp
                for m in range(HT):
                    win = wwin if m % 2 == 0 else wwinF
                    p.wait("sync", "sPE", win_tp[m % 2])
                    t1 = ld(win[:, 0:KT * 128],
                            wbf_1[m * 128:(m + 1) * 128, :])
                    t2 = ld(win[:, 2048:2048 + KT * 128],
                            wbf_2[m * 128:(m + 1) * 128, :])
                    p.wait("tensor", "sLD", t1)
                    p.wait("tensor", "sLD", t2)
                    for k in range(KT):
                        f = (lambda e, k=k, win=win: e.matmul(
                            out=ps[0][:, 0:T],
                            lhsT=win[:, k * 128:(k + 1) * 128],
                            rhs=flb[:, k * T:(k + 1) * T],
                            start=(k == 0), stop=(k == KT - 1)))
                        tp = pem(f) if k == KT - 1 else pe(f)
                    for k in range(KT):
                        f = (lambda e, k=k, win=win: e.matmul(
                            out=ps[1][:, 0:T],
                            lhsT=win[:, 2048 + k * 128:2048 + (k + 1) * 128],
                            rhs=flb[:, k * T:(k + 1) * T],
                            start=(k == 0), stop=(k == KT - 1)))
                        tp = pem(f) if k == KT - 1 else pe(f)
                    win_tp[m % 2] = tp
                    p.wait("scalar", "sPE", tp)
                    ta = acm(lambda e, m=m: e.activation(
                        out=fss[:], in_=ps[0][:, 0:T], func=AF.Silu,
                        scale=scal_sb[:, m:m + 1]))
                    p.wait("vector", "sAC", ta)
                    td = dvm(lambda e, m=m: e.scalar_tensor_tensor(
                        out=fa[:, m * T:(m + 1) * T], in0=ps[1][:, 0:T],
                        scalar=scal_sb[:, HT + m:HT + m + 1], in1=fss[:],
                        op0=ALU.mult, op1=ALU.mult))
                    p.wait("tensor", "sDV", td)

                def hlayer(wsrc, scol, src, dst):
                    win_tp = fwin_tp
                    for m in range(HT):
                        win = wwin if m % 2 == 0 else wwinF
                        p.wait("sync", "sPE", win_tp[m % 2])
                        toks = [ld(win[:, kb * 2048:(kb + 1) * 2048],
                                   wsrc[m * 128:(m + 1) * 128,
                                        kb * 2048:(kb + 1) * 2048])
                                for kb in range(2)]
                        for tok in toks:
                            p.wait("tensor", "sLD", tok)
                        for k in range(HT):
                            f = (lambda e, k=k, win=win: e.matmul(
                                out=ps[2][:, 0:T],
                                lhsT=win[:, k * 128:(k + 1) * 128],
                                rhs=src[:, k * T:(k + 1) * T],
                                start=(k == 0), stop=(k == HT - 1)))
                            tp = pem(f) if k == HT - 1 else pe(f)
                        win_tp[m % 2] = tp
                        p.wait("scalar", "sPE", tp)
                        ta = acm(lambda e, m=m: e.activation(
                            out=dst[:, m * T:(m + 1) * T], in_=ps[2][:, 0:T],
                            func=AF.Gelu,
                            scale=scal_sb[:, scol + m:scol + m + 1]))
                        p.wait("tensor", "sAC", ta)

                hlayer(wbf_h1, 2 * HT, fa, fb)
                hlayer(wbf_h2, 3 * HT, fb, fa)

                win_tp = fwin_tp
                for m in range(KT):
                    win = wwin if m % 2 == 0 else wwinF
                    p.wait("sync", "sPE", win_tp[m % 2])
                    toks = [ld(win[:, kb * 2048:(kb + 1) * 2048],
                               wbf_3[m * 128:(m + 1) * 128,
                                     kb * 2048:(kb + 1) * 2048])
                            for kb in range(2)]
                    for tok in toks:
                        p.wait("tensor", "sLD", tok)
                    for k in range(HT):
                        f = (lambda e, k=k, win=win: e.matmul(
                            out=ps[3][:, 0:T],
                            lhsT=win[:, k * 128:(k + 1) * 128],
                            rhs=fa[:, k * T:(k + 1) * T],
                            start=(k == 0), stop=(k == HT - 1)))
                        tp = pem(f) if k == HT - 1 else pe(f)
                    win_tp[m % 2] = tp
                    p.wait("scalar", "sPE", tp)
                    ta = acm(lambda e, m=m: e.activation(
                        out=ffv[:, m * T:(m + 1) * T], in_=ps[3][:, 0:T],
                        func=AF.Copy,
                        scale=scal_sb[:, 4 * HT + m:4 * HT + m + 1]))
                    p.wait("tensor", "sAC", ta)
                p.wait("vector", "sAC")

                tln = layer_norm2(ffv, flb, fxb, fqb)
                p.wait("vector", "sDV", tln)
                for k in range(KT):
                    s = slice(k * T, (k + 1) * T)
                    dv(lambda e, s=s: e.tensor_tensor(
                        out=ffv[:, s], in0=ffv[:, s], in1=ps[7][:, 0:T],
                        op=ALU.mult))
                    dv(lambda e, s=s: e.tensor_add(
                        out=fx2[:, s], in0=fx2[:, s], in1=ffv[:, s]))
                td = dvm(lambda e: e.nop())
                p.wait("sync", "sDV", td)
                st(y_out.rearrange("(k p) t -> p k t", p=128)
                   [:, :, j * T:(j + 1) * T],
                   fx2.rearrange("p (k t) -> p k t", k=KT))

        p.wait("sync", "sST")

        @block.sync
        def _(sync):
            for fn in p.ops["sync"]:
                fn(sync)

        @block.tensor
        def _(tensor):
            for fn in p.ops["tensor"]:
                fn(tensor)

        @block.scalar
        def _(scalar):
            for fn in p.ops["scalar"]:
                fn(scalar)

        @block.vector
        def _(vector):
            for fn in p.ops["vector"]:
                fn(vector)

    return nc


def _pack_wdt(w):
    out = np.empty((128, DI // 2), np.float32)
    for n in range(NT):
        out[DTR * (n % 2):DTR * (n % 2) + DTR,
            (n // 2) * 128:(n // 2) * 128 + 128] = w[:, n * 128:(n + 1) * 128]
    return np.ascontiguousarray(out)


def _prep_inputs(kw):
    f32 = np.float32
    cac = lambda a: np.ascontiguousarray(np.asarray(a, f32))

    def tile_vec(v, ntiles):
        return cac(np.asarray(v, f32).reshape(ntiles, 128).T)

    x = np.asarray(kw["L0_attn_x"], f32)
    x_t = np.ascontiguousarray(np.transpose(x, (0, 2, 1)))

    A_log = np.asarray(kw["L0_attn_A_log"], f32)
    alog_t = cac(A_log.reshape(NT, 128, NST).transpose(1, 0, 2)
                 .reshape(128, NT * NST))
    wconv = np.asarray(kw["L0_attn_W_conv"], f32).reshape(DI, 4)
    wconv_t = cac(wconv.reshape(NT, 128, 4).transpose(1, 0, 2)
                  .reshape(128, NT * 4))
    sel = np.zeros((NST, NST * 128), ml_dtypes.bfloat16)
    for n in range(NST):
        sel[n, n * 128:(n + 1) * 128] = 1
    common = dict(
        w_in=cac(kw["L0_attn_W_in"]),
        wconv_t=wconv_t,
        bconv_t=tile_vec(kw["L0_attn_b_conv"], NT),
        w_xproj=cac(kw["L0_attn_W_xproj"]),
        w_dt=cac(kw["L0_attn_W_dt"]),
        bdt_t=tile_vec(kw["L0_attn_b_dt"], NT),
        alog_t=alog_t,
        d_t=tile_vec(kw["L0_attn_D"], NT),
        w_out=cac(kw["L0_attn_W_out"]),
        w1t=cac(np.asarray(kw["L0_ffn_w1_w"], f32).T),
        w2t=cac(np.asarray(kw["L0_ffn_w2_w"], f32).T),
        h1t=cac(np.asarray(kw["L0_ffn_h1_w"], f32).T),
        h2t=cac(np.asarray(kw["L0_ffn_h2_w"], f32).T),
        w3t=cac(np.asarray(kw["L0_ffn_w3_w"], f32).T),
        s1_t=tile_vec(kw["L0_ffn_w1_s"], HT),
        s2_t=tile_vec(kw["L0_ffn_w2_s"], HT),
        sh1_t=tile_vec(kw["L0_ffn_h1_s"], HT),
        sh2_t=tile_vec(kw["L0_ffn_h2_s"], HT),
        s3_t=tile_vec(kw["L0_ffn_w3_s"], KT),
        sel_bf=sel,
        ones_bf=np.ones((128, 1), ml_dtypes.bfloat16),
        id_bf=np.eye(128, dtype=ml_dtypes.bfloat16),
        one1_bf=np.ones((1, 128), ml_dtypes.bfloat16),
        cst=np.tile(np.array([[EPS, 1.0, -1.0, 0.0]], np.float32), (128, 1)),
    )
    in_maps = []
    for c in range(8):
        s, q = c // 4, c % 4
        m = dict(common)
        m["x_seq"] = x_t[s]
        m["qidx"] = np.full((1, 1), q * (QT // T), np.int32)
        in_maps.append(m)
    return in_maps


def kernel(**inputs):
    in_maps = _prep_inputs(inputs)
    nc = build_program()
    res = run_bass_kernel_spmd(nc, in_maps, core_ids=list(range(8)))
    out = np.empty((B, L, DM), np.float32)
    for c in range(8):
        s, q = c // 4, c % 4
        out[s, q * QT:(q + 1) * QT, :] = res.results[c]["y_out"].T
    return out


# version 09
# speedup vs baseline: 1.2413x; 1.0095x over previous
import numpy as np
import ml_dtypes
import concourse.bass as bass
import concourse.mybir as mybir
from concourse.bass_utils import run_bass_kernel_spmd
from contextlib import ExitStack

F32 = mybir.dt.float32
BF16 = mybir.dt.bfloat16
I32 = mybir.dt.int32
AF = mybir.ActivationFunctionType
ALU = mybir.AluOpType

DM, DI, DH, NST, DTR = 1024, 2048, 4096, 16, 64
DBG = False
L, B = 8192, 2
T = 512
NCH = L // T
QT = 2048
EPS = 1e-5
KT = DM // 128
NT = DI // 128
HT = DH // 128
HHT = NT // 2

ENGS = ("sync", "tensor", "scalar", "vector")


class Prog:
    def __init__(self, nc):
        self.nc = nc
        self.ops = {e: [] for e in ENGS}
        self.cnt = {}
        self.sems = {}
        self.last_wait = {e: {} for e in ENGS}

    def sem(self, ctx, name):
        h = ctx.enter_context(self.nc.semaphore(name))
        self.cnt[name] = 0
        self.sems[name] = h
        return h

    def on(self, eng, fn):
        self.ops[eng].append(fn)

    def wait(self, eng, name, val=None):
        if isinstance(name, tuple):
            name, val = name
        elif isinstance(val, tuple):
            name, val = val
        if name in ("sLD", "sST") and val is None:
            for s in self.pools[name]:
                self.wait(eng, s, self.cnt[s])
            return
        if val is None:
            val = self.cnt[name]
        if val <= 0 or self.last_wait[eng].get(name, -1) >= val:
            return
        self.last_wait[eng][name] = val
        h = self.sems[name]
        self.on(eng, lambda e, h=h, v=val: e.wait_ge(h, v))

    def mark(self, eng, name, fn, amt):
        self.cnt[name] += amt
        h = self.sems[name]
        self.on(eng, lambda e, h=h, a=amt, f=fn: f(e).then_inc(h, a))
        return self.cnt[name]

    def emit(self, eng, fn):
        self.on(eng, fn)

    def barrier(self):
        for e in ENGS:
            for s in ("sPE", "sAC", "sDV"):
                self.wait(e, s)
            self.wait(e, "sLD")
            self.wait(e, "sST")


def build_program():
    nc = bass.Bass("TRN2", target_bir_lowering=False)
    dram = nc.dram_tensor

    x_in = dram("x_seq", [DM, L], F32, kind="ExternalInput").ap()
    qidx_in = dram("qidx", [1, 1], I32, kind="ExternalInput").ap()
    w_in = dram("w_in", [DM, 2 * DI], F32, kind="ExternalInput").ap()
    wconv_in = dram("wconv_t", [128, NT * 4], F32, kind="ExternalInput").ap()
    bconv_in = dram("bconv_t", [128, NT], F32, kind="ExternalInput").ap()
    wxp_in = dram("w_xproj", [DI, 96], F32, kind="ExternalInput").ap()
    wdt_in = dram("w_dt", [DTR, DI], F32, kind="ExternalInput").ap()
    bdt_in = dram("bdt_t", [128, NT], F32, kind="ExternalInput").ap()
    alog_in = dram("alog_t", [128, NT * NST], F32, kind="ExternalInput").ap()
    d_in = dram("d_t", [128, NT], F32, kind="ExternalInput").ap()
    w_out_in = dram("w_out", [DI, DM], F32, kind="ExternalInput").ap()
    w1_in = dram("w1t", [DM, DH], F32, kind="ExternalInput").ap()
    w2_in = dram("w2t", [DM, DH], F32, kind="ExternalInput").ap()
    h1_in = dram("h1t", [DH, DH], F32, kind="ExternalInput").ap()
    h2_in = dram("h2t", [DH, DH], F32, kind="ExternalInput").ap()
    w3_in = dram("w3t", [DH, DM], F32, kind="ExternalInput").ap()
    s1_in = dram("s1_t", [128, HT], F32, kind="ExternalInput").ap()
    s2_in = dram("s2_t", [128, HT], F32, kind="ExternalInput").ap()
    sh1_in = dram("sh1_t", [128, HT], F32, kind="ExternalInput").ap()
    sh2_in = dram("sh2_t", [128, HT], F32, kind="ExternalInput").ap()
    s3_in = dram("s3_t", [128, KT], F32, kind="ExternalInput").ap()
    sel_in = dram("sel_bf", [NST, NST * 128], BF16, kind="ExternalInput").ap()
    ones_in = dram("ones_bf", [128, 1], BF16, kind="ExternalInput").ap()
    one1_in = dram("one1_bf", [1, 128], BF16, kind="ExternalInput").ap()
    id_in = dram("id_bf", [128, 128], BF16, kind="ExternalInput").ap()
    cst_in = dram("cst", [128, 4], F32, kind="ExternalInput").ap()

    y_out = dram("y_out", [DM, QT], F32, kind="ExternalOutput").ap()

    wbf_in = dram("wbf_in", [8 * 128, KT * 512], BF16, kind="Internal").ap()
    wbf_out = dram("wbf_out", [4 * 128, NT * 256], BF16, kind="Internal").ap()
    wbf_1 = dram("wbf_1", [HT * 128, KT * 128], BF16, kind="Internal").ap()
    wbf_2 = dram("wbf_2", [HT * 128, KT * 128], BF16, kind="Internal").ap()
    wbf_h1 = dram("wbf_h1", [HT * 128, HT * 128], BF16, kind="Internal").ap()
    wbf_h2 = dram("wbf_h2", [HT * 128, HT * 128], BF16, kind="Internal").ap()
    wbf_3 = dram("wbf_3", [KT * 128, HT * 128], BF16, kind="Internal").ap()
    bc_dr = dram("bc_dr", [2 * NST, T], BF16, kind="Internal").ap()
    x2_dr = dram("x2_dr", [DM, L], F32, kind="Internal").ap()
    ln2_dr = dram("ln2_dr", [DM, L], BF16, kind="Internal").ap()
    if DBG:
        dbg_xc = dram("dbg_xc", [DI, L], F32, kind="Internal").ap()
        dbg_dl = dram("dbg_dl", [DI, L], F32, kind="Internal").ap()
        dbg_y = dram("dbg_y", [DI, L], F32, kind="Internal").ap()
        dbg_mo = dram("dbg_mo", [DM, L], F32, kind="Internal").ap()
        dbg_ln1 = dram("dbg_ln1", [DM, L], BF16, kind="Internal").ap()
        dbg_xi = dram("dbg_xi", [DI, L], BF16, kind="Internal").ap()
        dbg_z = dram("dbg_z", [DI, L], BF16, kind="Internal").ap()

    with ExitStack() as ctx:
        p = Prog(nc)
        for s in ("sPE", "sAC", "sDV"):
            p.sem(ctx, s)
        p.pools = {"sLD": [], "sST": []}
        for i in range(16):
            n = f"sL{i}"
            p.sem(ctx, n)
            p.pools["sLD"].append(n)
        for i in range(8):
            n = f"sS{i}"
            p.sem(ctx, n)
            p.pools["sST"].append(n)
        p.rot = {"sLD": 0, "sST": 0}
        sb = lambda n, s, d=F32: ctx.enter_context(nc.sbuf_tensor(n, s, d))

        A_sb = sb("A_sb", [128, NT * NST])
        D_sb = sb("D_sb", [128, NT])
        bconv_sb = sb("bconv_sb", [128, NT])
        bdt_sb = sb("bdt_sb", [128, NT])
        wconv_sb = sb("wconv_sb", [128, NT * 4])
        wxp_bf = sb("wxp_bf", [128, NT * 96], BF16)
        wdt_bf = sb("wdt_bf", [DTR, DI], BF16)
        sel_sb = sb("sel_sb", [NST, NST * 128], BF16)
        ones_sb = sb("ones_sb", [128, 1], BF16)
        one1_sb = sb("one1_sb", [1, 128], BF16)
        cst_sb = sb("cst_sb", [128, 4])
        mr_bf = sb("mr_bf", [1, 2 * T], BF16)
        hist_bf = sb("hist_bf", [128, NT * 3], BF16)
        hcar_sb = sb("hcar_sb", [128, NT * NST])
        qidx_sb = ctx.enter_context(nc.sbuf_tensor("qidx_sb", [1, 1], I32))
        dlr_bf = sb("dlr_bf", [DTR, T], BF16)
        b_bf = sb("b_bf", [NST, T], BF16)
        c_bf = sb("c_bf", [NST, T], BF16)
        wwin = sb("wwin", [128, 4096], BF16)
        id_sb = sb("id_sb", [128, 128], BF16)
        cstg_f = sb("cstg_f", [128, 1024])
        cstg_b = sb("cstg_b", [128, 1024], BF16)

        ps = [ctx.enter_context(nc.psum_tensor(f"ps{i}", [128, 512], F32))
              for i in range(8)]

        block = ctx.enter_context(nc.Block())

        def _dma(pool, fn):
            i = p.rot[pool]
            p.rot[pool] = (i + 1) % len(p.pools[pool])
            sname = p.pools[pool][i]
            p.wait("sync", sname, p.cnt[sname])
            v = p.mark("sync", sname, fn, 16)
            return (sname, v)

        def ld(dst, src):
            return _dma("sLD",
                        lambda e, d=dst, r=src: e.dma_start(out=d, in_=r))

        def st(dst, src):
            return _dma("sST",
                        lambda e, d=dst, r=src: e.dma_start(out=d, in_=r))

        def pe(fn):
            p.emit("tensor", fn)

        def pem(fn):
            return p.mark("tensor", "sPE", fn, 1)

        def ac(fn):
            p.wait("scalar", "sAC", p.cnt["sAC"])
            return p.mark("scalar", "sAC", fn, 1)

        acm = ac

        def dv(fn):
            p.wait("vector", "sDV", p.cnt["sDV"])
            return p.mark("vector", "sDV", fn, 1)

        dvm = dv

        with nc.sbuf_tensor("cast_f", [128, 4096], F32) as cast_f, \
             nc.sbuf_tensor("cast_b", [128, 4096], BF16) as cast_b:

            ld(D_sb[:], d_in)
            ld(bconv_sb[:], bconv_in)
            ld(bdt_sb[:], bdt_in)
            ld(wconv_sb[:], wconv_in)
            ld(sel_sb[:], sel_in)
            ld(ones_sb[:], ones_in)
            ld(one1_sb[:], one1_in)
            ld(id_sb[:], id_in)
            ld(qidx_sb[:], qidx_in)
            ld(cst_sb[:], cst_in)

            ld(cast_f[:, 0:NT * NST], alog_in)
            p.wait("scalar", "sLD")
            acm(lambda e: e.activation(out=A_sb[:], in_=cast_f[:, 0:NT * NST],
                                       func=AF.Exp))
            p.wait("sync", "sAC")
            p.wait("vector", "sAC")
            dvm(lambda e: e.tensor_scalar_mul(out=A_sb[:], in0=A_sb[:],
                                              scalar1=-1.0))

            ld(cast_f[:, 0:NT * 96],
               wxp_in.rearrange("(n p) c -> p n c", p=128))
            p.wait("scalar", "sLD")
            acm(lambda e: e.copy(out=wxp_bf[:], in_=cast_f[:, 0:NT * 96]))
            p.wait("sync", "sAC")
            ld(cast_f[0:DTR, 0:DI], wdt_in)
            p.wait("scalar", "sLD")
            acm(lambda e: e.copy(out=wdt_bf[:], in_=cast_f[0:DTR, 0:DI]))
            p.wait("sync", "sAC")

            def cast_tiled(src, dst, krows, mcols, slab_m, mc):
                for k in range(krows):
                    p.wait("sync", "sAC")
                    tl = ld(cast_f[:, 0:mcols * 128],
                            src[k * 128:(k + 1) * 128, :])
                    p.wait("scalar", "sLD", tl)
                    tc = acm(lambda e: e.copy(out=cast_b[:, 0:mcols * 128],
                                              in_=cast_f[:, 0:mcols * 128]))
                    p.wait("sync", "sAC", tc)
                    for mb in range(mcols // slab_m):
                        w = slab_m * 128
                        st(dst[mb * 128:(mb + 1) * 128,
                               k * w:(k + 1) * w],
                           cast_b[:, mb * w:(mb + 1) * w])
                    p.wait("scalar", "sST")

            cast_tiled(w_in, wbf_in, KT, 2 * NT, 4, 512)
            cast_tiled(w_out_in, wbf_out, NT, KT, 2, 256)
            p.barrier()

        dvm(lambda e: e.memset(hist_bf[:], 0.0))
        dvm(lambda e: e.memset(hcar_sb[:], 0.0))
        p.barrier()

        cast_steps = []
        cast_state = {"tc": 0}

        def make_cast_step(srcw, dst, k, c0, ncols, slab_m):
            def step():
                p.wait("sync", "sAC", cast_state["tc"])
                tl = ld(cstg_f[:, 0:ncols],
                        srcw[k * 128:(k + 1) * 128, c0:c0 + ncols])
                p.wait("scalar", "sLD", tl)
                for tok in cast_state.get("sts", []):
                    p.wait("scalar", tok)
                tc = ac(lambda e: e.copy(out=cstg_b[:, 0:ncols],
                                         in_=cstg_f[:, 0:ncols]))
                cast_state["tc"] = tc
                p.wait("sync", "sAC", tc)
                w = slab_m * 128
                sts = []
                for mb in range(c0 // w, (c0 + ncols) // w):
                    sts.append(st(dst[mb * 128:(mb + 1) * 128,
                                      k * w:(k + 1) * w],
                                  cstg_b[:, mb * w - c0:(mb + 1) * w - c0]))
                cast_state["sts"] = sts
            return step

        for (srcw, dst, krows, mcols) in (
                (w1_in, wbf_1, KT, HT), (w2_in, wbf_2, KT, HT),
                (h1_in, wbf_h1, HT, HT), (h2_in, wbf_h2, HT, HT),
                (w3_in, wbf_3, HT, KT)):
            for k in range(krows):
                for c0 in range(0, mcols * 128, 1024):
                    nc_ = min(1024, mcols * 128 - c0)
                    cast_steps.append(make_cast_step(srcw, dst, k, c0, nc_, 1))

        def layer_norm2(v, out_bf, xb, qb):
            pe_done = []
            if getattr(p, "_ln_last_pe", 0):
                p.wait("scalar", "sPE", p._ln_last_pe)
            for k in range(KT):
                s = slice(k * T, (k + 1) * T)
                sl = slice((k % 2) * T, (k % 2 + 1) * T)
                if k >= 2:
                    p.wait("scalar", "sPE", pe_done[k - 2])
                ac(lambda e, s=s, sl=sl: e.copy(out=xb[:, sl], in_=v[:, s]))
                ta = acm(lambda e, s=s, sl=sl: e.activation(
                    out=qb[:, sl], in_=v[:, s], func=AF.Square))
                p.wait("tensor", "sAC", ta)
                if k < KT - 1:
                    pe(lambda e, sl=sl, k=k: e.matmul(
                        out=ps[6][0:1, 0:T], lhsT=ones_sb[:], rhs=xb[:, sl],
                        start=(k == 0), stop=False))
                else:
                    pem(lambda e, sl=sl, k=k: e.matmul(
                        out=ps[6][0:1, 0:T], lhsT=ones_sb[:], rhs=xb[:, sl],
                        start=(k == 0), stop=True))
                pe_done.append(pem(lambda e, sl=sl, k=k: e.matmul(
                    out=ps[7][0:1, 0:T], lhsT=ones_sb[:], rhs=qb[:, sl],
                    start=(k == 0), stop=(k == KT - 1))))
            p._ln_last_pe = pe_done[-1]
            p.wait("vector", "sPE")
            dv(lambda e: e.tensor_scalar_mul(out=ps[6][0:1, 0:T],
                                             in0=ps[6][0:1, 0:T],
                                             scalar1=1.0 / DM))
            td = dvm(lambda e: e.tensor_scalar_mul(out=ps[7][0:1, 0:T],
                                                   in0=ps[7][0:1, 0:T],
                                                   scalar1=1.0 / DM))
            p.wait("scalar", "sDV", td)
            ac(lambda e: e.copy(out=mr_bf[0:1, 0:T], in_=ps[6][0:1, 0:T]))
            ta = acm(lambda e: e.activation(out=mr_bf[0:1, T:2 * T],
                                            in_=ps[6][0:1, 0:T],
                                            func=AF.Square))
            p.wait("vector", "sAC", ta)
            td = dvm(lambda e: e.tensor_sub(out=ps[7][0:1, 0:T],
                                            in0=ps[7][0:1, 0:T],
                                            in1=mr_bf[0:1, T:2 * T]))
            p.wait("scalar", "sDV", td)
            ta = acm(lambda e: e.activation(out=ps[7][0:1, 0:T],
                                            in_=ps[7][0:1, 0:T],
                                            func=AF.Sqrt,
                                            bias=cst_sb[0:1, 0:1]))
            p.wait("vector", "sAC", ta)
            td = dvm(lambda e: e.reciprocal(out=ps[7][0:1, 0:T],
                                            in_=ps[7][0:1, 0:T]))
            p.wait("scalar", "sDV", td)
            ta = acm(lambda e: e.copy(out=mr_bf[0:1, T:2 * T],
                                      in_=ps[7][0:1, 0:T]))
            p.wait("tensor", "sAC", ta)
            pem(lambda e: e.matmul(out=ps[6][:, 0:T], lhsT=one1_sb[:],
                                   rhs=mr_bf[0:1, 0:T], start=True, stop=True))
            tp = pem(lambda e: e.matmul(out=ps[7][:, 0:T], lhsT=one1_sb[:],
                                        rhs=mr_bf[0:1, T:2 * T],
                                        start=True, stop=True))
            p.wait("vector", "sPE", tp)
            for k in range(KT):
                s = slice(k * T, (k + 1) * T)
                dv(lambda e, s=s: e.scalar_tensor_tensor(
                    out=v[:, s], in0=ps[6][:, 0:T], scalar=-1.0, in1=v[:, s],
                    op0=ALU.mult, op1=ALU.add))
                dv(lambda e, s=s: e.tensor_tensor(
                    out=out_bf[:, s], in0=v[:, s], in1=ps[7][:, 0:T],
                    op=ALU.mult))
            return dvm(lambda e: e.nop())

        with ExitStack() as mctx:
            msb = lambda n, s, d=F32: mctx.enter_context(nc.sbuf_tensor(n, s, d))
            sA = msb("sA", [128, KT * T])
            sB = msb("sB", [128, NT * T])
            sC = msb("sC", [128, HHT * T])
            sD = msb("sD", [128, HHT * T])
            sE = msb("sE", [128, HHT * T])
            sF = msb("sF", [128, HHT * T], BF16)
            sG = msb("sG", [128, 6 * T], BF16)
            sGf = msb("sGf", [128, 4 * T])
            bcb = msb("bcb", [128, 4 * T], BF16)
            bA = msb("bA", [128, KT * T], BF16)
            bX = msb("bX", [128, 2 * T], BF16)
            bQ = msb("bQ", [128, 2 * T], BF16)
            bI = msb("bI", [128, NT * T], BF16)
            bZ = msb("bZ", [128, NT * T], BF16)
            bC = bI

            bc_last_toks = []
            edges = {}
            for c in range(NCH):
                tok = slice(c * T, (c + 1) * T)
                if c > 0:
                    bc_last_toks = bc_last_toks[-4:]

                if "sA_free" in edges:
                    p.wait("sync", "sDV", edges["sA_free"])
                    p.wait("sync", edges["x2_store"])
                    p.wait("sync", edges["ln2_store"])
                tl = ld(sA.rearrange("p (k t) -> p k t", k=KT),
                        x_in.rearrange("(k p) t -> p k t", p=128)[:, :, tok])
                p.wait("scalar", "sLD", tl)
                p.wait("vector", "sLD", tl)
                if "tp_win" in edges:
                    p.wait("vector", "sPE", edges["tp_win"])
                    p.wait("scalar", "sPE", edges["tp_win"])
                tln = layer_norm2(sA, bA, bX, bQ)
                p.wait("tensor", "sDV", tln)
                if DBG:
                    p.wait("sync", "sDV", tln)
                    st(dbg_ln1.rearrange("(k p) t -> p k t", p=128)[:, :, tok],
                       bA.rearrange("p (k t) -> p k t", k=KT))

                if "td_conv" in edges:
                    p.wait("scalar", "sDV", edges["td_conv"])
                if "tp_wout" in edges:
                    p.wait("scalar", "sPE", edges["tp_wout"])
                for mb in range(8):
                    tl = ld(wwin[:], wbf_in[mb * 128:(mb + 1) * 128, :])
                    p.wait("tensor", "sLD", tl)
                    for m in range(4):
                        for k in range(KT):
                            f = (lambda e, m=m, k=k: e.matmul(
                                out=ps[m][:, 0:T],
                                lhsT=wwin[:, k * 512 + m * 128:
                                          k * 512 + (m + 1) * 128],
                                rhs=bA[:, k * T:(k + 1) * T],
                                start=(k == 0), stop=(k == KT - 1)))
                            tp = pem(f) if k == KT - 1 else pe(f)
                    p.wait("scalar", "sPE", tp)
                    for m in range(4):
                        gi = mb * 4 + m
                        if gi < NT:
                            dst = bI[:, gi * T:(gi + 1) * T]
                        else:
                            dst = bZ[:, (gi - NT) * T:(gi - NT + 1) * T]
                        ac(lambda e, m=m, d=dst: e.copy(out=d,
                                                        in_=ps[m][:, 0:T]))
                    ta = acm(lambda e: e.nop())
                    p.wait("tensor", "sAC", ta)
                    p.wait("sync", "sPE", tp)
                edges["tp_win"] = tp
                nsteps = (len(cast_steps) + NCH - 1 - c) // (NCH - c) \
                    if c < NCH else 0
                for _ in range(min(nsteps, len(cast_steps))):
                    cast_steps.pop(0)()
                p.wait("vector", "sAC")
                if DBG:
                    p.wait("sync", "sAC")
                    st(dbg_xi.rearrange("(n p) t -> p n t", p=128)[:, :, tok],
                       bI.rearrange("p (n t) -> p n t", n=NT)[:, :, 0:T])
                    st(dbg_z.rearrange("(n p) t -> p n t", p=128)[:, :, tok],
                       bZ.rearrange("p (n t) -> p n t", n=NT))

                if "ta_sp" in edges:
                    p.wait("vector", "sAC", edges["ta_sp"])
                if "td_gate" in edges:
                    p.wait("scalar", "sDV", edges["td_gate"])
                for hf in range(2):
                    for j in range(HHT):
                        n = hf * HHT + j
                        asl = slice(j * T, (j + 1) * T)
                        for k in range(4):
                            wcol = wconv_sb[:, n * 4 + k:n * 4 + k + 1]
                            lead = 3 - k
                            if lead > 0:
                                hsl = hist_bf[:, n * 3 + k:n * 3 + 3]
                                if k == 0:
                                    dv(lambda e, a=asl, h=hsl, w=wcol, ld_=lead:
                                       e.tensor_scalar(
                                           out=sC[:, a][:, 0:ld_], in0=h,
                                           scalar1=w, scalar2=None,
                                           op0=ALU.mult))
                                else:
                                    dv(lambda e, a=asl, h=hsl, w=wcol, ld_=lead:
                                       e.scalar_tensor_tensor(
                                           out=sC[:, a][:, 0:ld_], in0=h,
                                           scalar=w,
                                           in1=sC[:, a][:, 0:ld_],
                                           op0=ALU.mult, op1=ALU.add))
                            bsl = bI[:, n * T:n * T + T - lead]
                            osl = sC[:, j * T + lead:(j + 1) * T]
                            if k == 0:
                                dv(lambda e, b=bsl, o=osl, w=wcol:
                                   e.tensor_scalar(out=o, in0=b, scalar1=w,
                                                   scalar2=None, op0=ALU.mult))
                            else:
                                dv(lambda e, b=bsl, o=osl, w=wcol:
                                   e.scalar_tensor_tensor(
                                       out=o, in0=b, scalar=w, in1=o,
                                       op0=ALU.mult, op1=ALU.add))
                        dv(lambda e, n=n: e.tensor_copy(
                            out=hist_bf[:, n * 3:n * 3 + 3],
                            in_=bI[:, n * T + T - 3:n * T + T]))
                    td = dvm(lambda e: e.nop())
                    edges["td_conv"] = td
                    p.wait("scalar", "sDV", td)
                    for j in range(HHT):
                        n = hf * HHT + j
                        ac(lambda e, j=j, n=n: e.activation(
                            out=sB[:, n * T:(n + 1) * T],
                            in_=sC[:, j * T:(j + 1) * T], func=AF.Silu,
                            bias=bconv_sb[:, n:n + 1]))
                    ta = acm(lambda e: e.nop())
                    p.wait("vector", "sAC", ta)
                if "tp_wout" in edges:
                    p.wait("vector", "sPE", edges["tp_wout"])
                for n in range(NT):
                    s = slice(n * T, (n + 1) * T)
                    dv(lambda e, s=s: e.tensor_copy(out=bC[:, s], in_=sB[:, s]))
                td = dvm(lambda e: e.nop())
                p.wait("tensor", "sDV", td)
                if DBG:
                    p.wait("sync", "sDV", td)
                    st(dbg_xc.rearrange("(n p) t -> p n t", p=128)[:, :, tok],
                       sB.rearrange("p (n t) -> p n t", n=NT))

                for (bank, c0, c1) in ((0, 0, DTR), (4, DTR, DTR + NST),
                                       (5, DTR + NST, 96)):
                    for k in range(NT):
                        f = (lambda e, k=k, bank=bank, c0=c0, c1=c1: e.matmul(
                            out=ps[bank][0:c1 - c0, 0:T],
                            lhsT=wxp_bf[:, k * 96 + c0:k * 96 + c1],
                            rhs=bC[:, k * T:(k + 1) * T],
                            start=(k == 0), stop=(k == NT - 1)))
                        tp = pem(f) if k == NT - 1 else pe(f)
                p.wait("scalar", "sPE", tp)
                ac(lambda e: e.copy(out=dlr_bf[:], in_=ps[0][0:DTR, 0:T]))
                ac(lambda e: e.copy(out=b_bf[:], in_=ps[4][0:NST, 0:T]))
                ta = acm(lambda e: e.copy(out=c_bf[:], in_=ps[5][0:NST, 0:T]))
                p.wait("tensor", "sAC", ta)
                p.wait("sync", "sAC", ta)
                for tkk in bc_last_toks:
                    p.wait("sync", tkk)
                tb1 = st(bc_dr[0:NST, :], b_bf[:])
                tb2 = st(bc_dr[NST:2 * NST, :], c_bf[:])

                if "tln_mo" in edges:
                    p.wait("scalar", "sDV", edges["tln_mo"])
                for hf in range(2):
                    for j in range(HHT):
                        n = hf * HHT + j
                        tp = pem(lambda e, n=n, j=j: e.matmul(
                            out=ps[j % 4][:, 0:T],
                            lhsT=wdt_bf[:, n * 128:(n + 1) * 128],
                            rhs=dlr_bf[:], start=True, stop=True))
                        if j % 4 == 3:
                            p.wait("scalar", "sPE", tp)
                            for jj in range(j - 3, j + 1):
                                n2 = hf * HHT + jj
                                ac(lambda e, jj=jj, n2=n2: e.copy(
                                    out=sC[:, jj * T:(jj + 1) * T],
                                    in_=ps[jj % 4][:, 0:T]))
                            ta = acm(lambda e: e.nop())
                            p.wait("tensor", "sAC", ta)
                    p.wait("vector", "sAC")
                    for j in range(HHT):
                        n = hf * HHT + j
                        s = slice(j * T, (j + 1) * T)
                        dv(lambda e, s=s, n=n: e.tensor_scalar_add(
                            out=sC[:, s], in0=sC[:, s],
                            scalar1=bdt_sb[:, n:n + 1]))
                    td = dvm(lambda e: e.nop())
                    p.wait("scalar", "sDV", td)
                    for j in range(HHT):
                        s = slice(j * T, (j + 1) * T)
                        ac(lambda e, s=s: e.activation(
                            out=sD[:, s], in_=sC[:, s], func=AF.Abs))
                        ac(lambda e, s=s: e.activation(
                            out=sD[:, s], in_=sD[:, s], func=AF.Exp,
                            scale=cst_sb[:, 2:3]))
                        ac(lambda e, s=s: e.activation(
                            out=sD[:, s], in_=sD[:, s], func=AF.Ln,
                            bias=cst_sb[:, 1:2]))
                    ta = acm(lambda e: e.nop())
                    edges["ta_sp"] = ta
                    p.wait("vector", "sAC", ta)
                    for j in range(HHT):
                        n = hf * HHT + j
                        s = slice(j * T, (j + 1) * T)
                        dv(lambda e, s=s: e.tensor_scalar_max(
                            out=sE[:, s], in0=sC[:, s], scalar1=0.0))
                        dv(lambda e, s=s: e.tensor_add(
                            out=sE[:, s], in0=sE[:, s], in1=sD[:, s]))
                        dv(lambda e, s=s, n=n: e.tensor_mul(
                            out=sF[:, s], in0=sE[:, s],
                            in1=sB[:, n * T:(n + 1) * T]))
                    td = dvm(lambda e: e.nop())
                    p.wait("scalar", "sDV", td)
                    p.wait("tensor", "sDV", td)
                    if DBG:
                        p.wait("sync", "sDV", td)
                        st(dbg_dl.rearrange("(n p) t -> p n t", p=128)
                           [:, hf * HHT:(hf + 1) * HHT, tok],
                           sE.rearrange("p (n t) -> p n t", n=HHT))

                    scan_done = {}
                    yp_dv = {}
                    yp_pe = {}
                    bc_free = {}
                    bc_tok = {}
                    for n in range(NST):
                        sl = n % 2
                        if n == 0:
                            p.wait("sync", "sST", tb1)
                            p.wait("sync", "sST", tb2)
                            p.wait("sync", "sDV")
                            p.wait("tensor", "sAC")
                        if n >= 2:
                            p.wait("sync", "sDV", bc_free[n - 2])
                        bc_tok[n] = (
                            ld(bcb[:, 2 * sl * T:(2 * sl + 1) * T],
                               bc_dr[n:n + 1, :].broadcast_to([128, T])),
                            ld(bcb[:, (2 * sl + 1) * T:(2 * sl + 2) * T],
                               bc_dr[NST + n:NST + n + 1, :]
                               .broadcast_to([128, T])))
                        for j in range(HHT):
                            ti = hf * HHT + j
                            it = n * HHT + j
                            da = sGf[:, (it % 4) * T:(it % 4 + 1) * T]
                            dbx = sG[:, 0 * T:1 * T]
                            hh = sG[:, 1 * T:2 * T]
                            yp = sG[:, (2 + it % 4) * T:(3 + it % 4) * T]
                            Bb = bcb[:, 2 * sl * T:(2 * sl + 1) * T]
                            Cb = bcb[:, (2 * sl + 1) * T:(2 * sl + 2) * T]
                            s = slice(j * T, (j + 1) * T)
                            if it >= 4:
                                p.wait("scalar", "sDV", scan_done[it - 4])
                            ta = ac(lambda e, s=s, da=da, ti=ti, n=n:
                                    e.activation(
                                        out=da, in_=sE[:, s], func=AF.Exp,
                                        scale=A_sb[:, ti * NST + n:
                                                   ti * NST + n + 1]))
                            if j == 0:
                                p.wait("vector", bc_tok[n][0])
                                p.wait("vector", bc_tok[n][1])
                            dv(lambda e, s=s, dbx=dbx, Bb=Bb: e.tensor_tensor(
                                out=dbx, in0=sF[:, s], in1=Bb, op=ALU.mult))
                            p.wait("vector", "sAC", ta)
                            sd = dv(lambda e, da=da, dbx=dbx, hh=hh, ti=ti, n=n:
                                    e.tensor_tensor_scan(
                                        out=hh, data0=da, data1=dbx,
                                        initial=hcar_sb[:, ti * NST + n:
                                                        ti * NST + n + 1],
                                        op0=ALU.mult, op1=ALU.add))
                            scan_done[it] = sd
                            dv(lambda e, hh=hh, ti=ti, n=n: e.tensor_copy(
                                out=hcar_sb[:, ti * NST + n:ti * NST + n + 1],
                                in_=hh[:, T - 1:T]))
                            if it >= 4:
                                p.wait("vector", "sPE", yp_pe[it - 4])
                            tyd = dv(lambda e, hh=hh, yp=yp, Cb=Cb:
                                     e.tensor_tensor(out=yp, in0=hh, in1=Cb,
                                                     op=ALU.mult))
                            yp_dv[it] = tyd
                            p.wait("tensor", "sDV", tyd)
                            yp_pe[it] = pem(lambda e, yp=yp, j=j, n=n: e.matmul(
                                out=ps[j][:, 0:T], lhsT=id_sb[:], rhs=yp,
                                start=(n == 0), stop=(n == NST - 1)))
                            if j == HHT - 1:
                                bc_free[n] = tyd
                                if n >= NST - 2:
                                    bc_last_toks.extend(bc_tok[n])
                    p.wait("scalar", "sPE")
                    for j in range(HHT):
                        ac(lambda e, j=j: e.copy(
                            out=sA[:, j * T:(j + 1) * T], in_=ps[j][:, 0:T]))
                    tyev = ac(lambda e: e.nop())
                    p.wait("vector", "sAC", tyev)
                    p.wait("tensor", "sAC", tyev)

                    if DBG:
                        p.wait("sync", "sDV")
                        st(dbg_y.rearrange("(n p) t -> p n t", p=128)
                           [:, hf * HHT:(hf + 1) * HHT, tok],
                           sA.rearrange("p (n t) -> p n t", n=HHT))
                        p.wait("vector", "sST")
                    for j in range(HHT):
                        n = hf * HHT + j
                        s = slice(j * T, (j + 1) * T)
                        ac(lambda e, s=s, n=n: e.activation(
                            out=sD[:, s], in_=bZ[:, n * T:(n + 1) * T],
                            func=AF.Silu))
                    ta = acm(lambda e: e.nop())
                    p.wait("vector", "sAC", ta)
                    for j in range(HHT):
                        n = hf * HHT + j
                        s = slice(j * T, (j + 1) * T)
                        dv(lambda e, s=s, n=n: e.scalar_tensor_tensor(
                            out=sA[:, s], in0=sB[:, n * T:(n + 1) * T],
                            scalar=D_sb[:, n:n + 1], in1=sA[:, s],
                            op0=ALU.mult, op1=ALU.add))
                        dv(lambda e, s=s, n=n: e.tensor_tensor(
                            out=bC[:, n * T:(n + 1) * T], in0=sA[:, s],
                            in1=sD[:, s], op=ALU.mult))
                    td = dvm(lambda e: e.nop())
                    edges["td_gate"] = td
                    p.wait("tensor", "sDV", td)
                    p.wait("scalar", "sDV", td)

                for mb in range(4):
                    tl = ld(wwin[:], wbf_out[mb * 128:(mb + 1) * 128, :])
                    p.wait("tensor", "sLD", tl)
                    for m in range(2):
                        for k in range(NT):
                            f = (lambda e, m=m, k=k: e.matmul(
                                out=ps[m][:, 0:T],
                                lhsT=wwin[:, k * 256 + m * 128:
                                          k * 256 + (m + 1) * 128],
                                rhs=bC[:, k * T:(k + 1) * T],
                                start=(k == 0), stop=(k == NT - 1)))
                            tp = pem(f) if k == NT - 1 else pe(f)
                    p.wait("scalar", "sPE", tp)
                    for m in range(2):
                        gi = mb * 2 + m
                        ac(lambda e, m=m, gi=gi: e.copy(
                            out=sE[:, gi * T:(gi + 1) * T], in_=ps[m][:, 0:T]))
                    ta = acm(lambda e: e.nop())
                    p.wait("tensor", "sAC", ta)
                    p.wait("sync", "sPE", tp)
                edges["tp_wout"] = tp
                p.wait("vector", "sAC")

                if DBG:
                    p.wait("sync", "sAC")
                    st(dbg_mo.rearrange("(k p) t -> p k t", p=128)[:, :, tok],
                       sE.rearrange("p (k t) -> p k t", k=KT))
                    p.wait("vector", "sST")
                    p.wait("scalar", "sST")
                tl = ld(sA.rearrange("p (k t) -> p k t", k=KT),
                        x_in.rearrange("(k p) t -> p k t", p=128)[:, :, tok])
                p.wait("scalar", "sLD", tl)
                p.wait("vector", "sLD", tl)
                tln = layer_norm2(sE, bA, bX, bQ)
                p.wait("vector", "sDV", tln)
                for k in range(KT):
                    s = slice(k * T, (k + 1) * T)
                    dv(lambda e, s=s: e.tensor_tensor(
                        out=sE[:, s], in0=sE[:, s], in1=ps[7][:, 0:T],
                        op=ALU.mult))
                    dv(lambda e, s=s: e.tensor_add(
                        out=sA[:, s], in0=sA[:, s], in1=sE[:, s]))
                td = dvm(lambda e: e.nop())
                edges["tln_mo"] = td
                p.wait("sync", "sDV", td)
                p.wait("scalar", "sDV", td)
                tsx = st(x2_dr.rearrange("(k p) t -> p k t", p=128)[:, :, tok],
                         sA.rearrange("p (k t) -> p k t", k=KT))
                edges["x2_store"] = tsx
                p.wait("vector", "sST", tsx)
                tln = layer_norm2(sA, bA, bX, bQ)
                edges["sA_free"] = tln
                p.wait("sync", "sDV", tln)
                ts = st(ln2_dr.rearrange("(k p) t -> p k t", p=128)[:, :, tok],
                        bA.rearrange("p (k t) -> p k t", k=KT))
                edges["ln2_store"] = ts

        p.barrier()
        with ExitStack() as fctx:
            fsb = lambda n, s, d=F32: fctx.enter_context(nc.sbuf_tensor(n, s, d))
            fa = fsb("fa", [128, HT * T], BF16)
            fb = fsb("fb", [128, HT * T], BF16)
            fx2 = fsb("fx2", [128, KT * T])
            ffv = fsb("ffv", [128, KT * T])
            flb = fsb("flb", [128, KT * T], BF16)
            fxb = fsb("fxb", [128, 2 * T], BF16)
            fqb = fsb("fqb", [128, 2 * T], BF16)
            fss = fsb("fss", [128, T])
            wwinF = fsb("wwinF", [128, 4096], BF16)
            scal_sb = fsb("scal_sb", [128, 4 * HT + KT])
            fwin_tp = {0: 0, 1: 0}
            ld(scal_sb[:, 0:HT], s1_in)
            ld(scal_sb[:, HT:2 * HT], s2_in)
            ld(scal_sb[:, 2 * HT:3 * HT], sh1_in)
            ld(scal_sb[:, 3 * HT:4 * HT], sh2_in)
            tl = ld(scal_sb[:, 4 * HT:4 * HT + KT], s3_in)
            p.wait("scalar", "sLD", tl)
            p.wait("vector", "sLD", tl)

            for j in range(QT // T):
                p.barrier()
                def dyn_ld(dst, src_r, jj=j):
                    def f(e, dst=dst, src_r=src_r, jj=jj):
                        base = e.value_load(qidx_sb[0:1, 0:1])
                        return e.dma_start(out=dst,
                                           in_=src_r[:, :, bass.ts(base + jj, T)])
                    return _dma("sLD", f)

                t1 = dyn_ld(flb.rearrange("p (k t) -> p k t", k=KT),
                            ln2_dr.rearrange("(k p) t -> p k t", p=128))
                t2 = dyn_ld(fx2.rearrange("p (k t) -> p k t", k=KT),
                            x2_dr.rearrange("(k p) t -> p k t", p=128))
                for tok in (t1, t2):
                    p.wait("tensor", "sLD", tok)
                    p.wait("vector", "sLD", tok)
                    p.wait("scalar", "sLD", tok)

                win_tp = fwin_tp
                for m in range(HT):
                    win = wwin if m % 2 == 0 else wwinF
                    p.wait("sync", "sPE", win_tp[m % 2])
                    t1 = ld(win[:, 0:KT * 128],
                            wbf_1[m * 128:(m + 1) * 128, :])
                    t2 = ld(win[:, 2048:2048 + KT * 128],
                            wbf_2[m * 128:(m + 1) * 128, :])
                    p.wait("tensor", "sLD", t1)
                    p.wait("tensor", "sLD", t2)
                    for k in range(KT):
                        f = (lambda e, k=k, win=win: e.matmul(
                            out=ps[0][:, 0:T],
                            lhsT=win[:, k * 128:(k + 1) * 128],
                            rhs=flb[:, k * T:(k + 1) * T],
                            start=(k == 0), stop=(k == KT - 1)))
                        tp = pem(f) if k == KT - 1 else pe(f)
                    for k in range(KT):
                        f = (lambda e, k=k, win=win: e.matmul(
                            out=ps[1][:, 0:T],
                            lhsT=win[:, 2048 + k * 128:2048 + (k + 1) * 128],
                            rhs=flb[:, k * T:(k + 1) * T],
                            start=(k == 0), stop=(k == KT - 1)))
                        tp = pem(f) if k == KT - 1 else pe(f)
                    win_tp[m % 2] = tp
                    p.wait("scalar", "sPE", tp)
                    ta = acm(lambda e, m=m: e.activation(
                        out=fss[:], in_=ps[0][:, 0:T], func=AF.Silu,
                        scale=scal_sb[:, m:m + 1]))
                    p.wait("vector", "sAC", ta)
                    td = dvm(lambda e, m=m: e.scalar_tensor_tensor(
                        out=fa[:, m * T:(m + 1) * T], in0=ps[1][:, 0:T],
                        scalar=scal_sb[:, HT + m:HT + m + 1], in1=fss[:],
                        op0=ALU.mult, op1=ALU.mult))
                    p.wait("tensor", "sDV", td)

                def hlayer(wsrc, scol, src, dst):
                    win_tp = fwin_tp
                    for m in range(HT):
                        win = wwin if m % 2 == 0 else wwinF
                        p.wait("sync", "sPE", win_tp[m % 2])
                        toks = [ld(win[:, kb * 2048:(kb + 1) * 2048],
                                   wsrc[m * 128:(m + 1) * 128,
                                        kb * 2048:(kb + 1) * 2048])
                                for kb in range(2)]
                        for tok in toks:
                            p.wait("tensor", "sLD", tok)
                        for k in range(HT):
                            f = (lambda e, k=k, win=win: e.matmul(
                                out=ps[2][:, 0:T],
                                lhsT=win[:, k * 128:(k + 1) * 128],
                                rhs=src[:, k * T:(k + 1) * T],
                                start=(k == 0), stop=(k == HT - 1)))
                            tp = pem(f) if k == HT - 1 else pe(f)
                        win_tp[m % 2] = tp
                        p.wait("scalar", "sPE", tp)
                        ta = acm(lambda e, m=m: e.activation(
                            out=dst[:, m * T:(m + 1) * T], in_=ps[2][:, 0:T],
                            func=AF.Gelu,
                            scale=scal_sb[:, scol + m:scol + m + 1]))
                        p.wait("tensor", "sAC", ta)

                hlayer(wbf_h1, 2 * HT, fa, fb)
                hlayer(wbf_h2, 3 * HT, fb, fa)

                win_tp = fwin_tp
                for m in range(KT):
                    win = wwin if m % 2 == 0 else wwinF
                    p.wait("sync", "sPE", win_tp[m % 2])
                    toks = [ld(win[:, kb * 2048:(kb + 1) * 2048],
                               wbf_3[m * 128:(m + 1) * 128,
                                     kb * 2048:(kb + 1) * 2048])
                            for kb in range(2)]
                    for tok in toks:
                        p.wait("tensor", "sLD", tok)
                    for k in range(HT):
                        f = (lambda e, k=k, win=win: e.matmul(
                            out=ps[3][:, 0:T],
                            lhsT=win[:, k * 128:(k + 1) * 128],
                            rhs=fa[:, k * T:(k + 1) * T],
                            start=(k == 0), stop=(k == HT - 1)))
                        tp = pem(f) if k == HT - 1 else pe(f)
                    win_tp[m % 2] = tp
                    p.wait("scalar", "sPE", tp)
                    ta = acm(lambda e, m=m: e.activation(
                        out=ffv[:, m * T:(m + 1) * T], in_=ps[3][:, 0:T],
                        func=AF.Copy,
                        scale=scal_sb[:, 4 * HT + m:4 * HT + m + 1]))
                    p.wait("tensor", "sAC", ta)
                p.wait("vector", "sAC")

                tln = layer_norm2(ffv, flb, fxb, fqb)
                p.wait("vector", "sDV", tln)
                for k in range(KT):
                    s = slice(k * T, (k + 1) * T)
                    dv(lambda e, s=s: e.tensor_tensor(
                        out=ffv[:, s], in0=ffv[:, s], in1=ps[7][:, 0:T],
                        op=ALU.mult))
                    dv(lambda e, s=s: e.tensor_add(
                        out=fx2[:, s], in0=fx2[:, s], in1=ffv[:, s]))
                td = dvm(lambda e: e.nop())
                p.wait("sync", "sDV", td)
                st(y_out.rearrange("(k p) t -> p k t", p=128)
                   [:, :, j * T:(j + 1) * T],
                   fx2.rearrange("p (k t) -> p k t", k=KT))

        p.wait("sync", "sST")

        @block.sync
        def _(sync):
            for fn in p.ops["sync"]:
                fn(sync)

        @block.tensor
        def _(tensor):
            for fn in p.ops["tensor"]:
                fn(tensor)

        @block.scalar
        def _(scalar):
            for fn in p.ops["scalar"]:
                fn(scalar)

        @block.vector
        def _(vector):
            for fn in p.ops["vector"]:
                fn(vector)

    return nc


def _pack_wdt(w):
    out = np.empty((128, DI // 2), np.float32)
    for n in range(NT):
        out[DTR * (n % 2):DTR * (n % 2) + DTR,
            (n // 2) * 128:(n // 2) * 128 + 128] = w[:, n * 128:(n + 1) * 128]
    return np.ascontiguousarray(out)


def _prep_inputs(kw):
    f32 = np.float32
    cac = lambda a: np.ascontiguousarray(np.asarray(a, f32))

    def tile_vec(v, ntiles):
        return cac(np.asarray(v, f32).reshape(ntiles, 128).T)

    x = np.asarray(kw["L0_attn_x"], f32)
    x_t = np.ascontiguousarray(np.transpose(x, (0, 2, 1)))

    A_log = np.asarray(kw["L0_attn_A_log"], f32)
    alog_t = cac(A_log.reshape(NT, 128, NST).transpose(1, 0, 2)
                 .reshape(128, NT * NST))
    wconv = np.asarray(kw["L0_attn_W_conv"], f32).reshape(DI, 4)
    wconv_t = cac(wconv.reshape(NT, 128, 4).transpose(1, 0, 2)
                  .reshape(128, NT * 4))
    sel = np.zeros((NST, NST * 128), ml_dtypes.bfloat16)
    for n in range(NST):
        sel[n, n * 128:(n + 1) * 128] = 1
    common = dict(
        w_in=cac(kw["L0_attn_W_in"]),
        wconv_t=wconv_t,
        bconv_t=tile_vec(kw["L0_attn_b_conv"], NT),
        w_xproj=cac(kw["L0_attn_W_xproj"]),
        w_dt=cac(kw["L0_attn_W_dt"]),
        bdt_t=tile_vec(kw["L0_attn_b_dt"], NT),
        alog_t=alog_t,
        d_t=tile_vec(kw["L0_attn_D"], NT),
        w_out=cac(kw["L0_attn_W_out"]),
        w1t=cac(np.asarray(kw["L0_ffn_w1_w"], f32).T),
        w2t=cac(np.asarray(kw["L0_ffn_w2_w"], f32).T),
        h1t=cac(np.asarray(kw["L0_ffn_h1_w"], f32).T),
        h2t=cac(np.asarray(kw["L0_ffn_h2_w"], f32).T),
        w3t=cac(np.asarray(kw["L0_ffn_w3_w"], f32).T),
        s1_t=tile_vec(kw["L0_ffn_w1_s"], HT),
        s2_t=tile_vec(kw["L0_ffn_w2_s"], HT),
        sh1_t=tile_vec(kw["L0_ffn_h1_s"], HT),
        sh2_t=tile_vec(kw["L0_ffn_h2_s"], HT),
        s3_t=tile_vec(kw["L0_ffn_w3_s"], KT),
        sel_bf=sel,
        ones_bf=np.ones((128, 1), ml_dtypes.bfloat16),
        id_bf=np.eye(128, dtype=ml_dtypes.bfloat16),
        one1_bf=np.ones((1, 128), ml_dtypes.bfloat16),
        cst=np.tile(np.array([[EPS, 1.0, -1.0, 0.0]], np.float32), (128, 1)),
    )
    in_maps = []
    for c in range(8):
        s, q = c // 4, c % 4
        m = dict(common)
        m["x_seq"] = x_t[s]
        m["qidx"] = np.full((1, 1), q * (QT // T), np.int32)
        in_maps.append(m)
    return in_maps


def kernel(**inputs):
    in_maps = _prep_inputs(inputs)
    nc = build_program()
    res = run_bass_kernel_spmd(nc, in_maps, core_ids=list(range(8)))
    out = np.empty((B, L, DM), np.float32)
    for c in range(8):
        s, q = c // 4, c % 4
        out[s, q * QT:(q + 1) * QT, :] = res.results[c]["y_out"].T
    return out
